# version 02
# baseline (speedup 1.0000x reference)
import math
from contextlib import ExitStack
import numpy as np
import ml_dtypes
import concourse.bass as bass
import concourse.mybir as mybir
from concourse.bass_utils import run_bass_kernel_spmd

F32 = mybir.dt.float32
BF16 = mybir.dt.bfloat16
F16 = mybir.dt.float16
I32 = mybir.dt.int32
AF = mybir.ActivationFunctionType
ALU = mybir.AluOpType
BF = ml_dtypes.bfloat16

NCORE = 8
T = 4096
TH = 2048
D = 4096
MD = 512
MF = 2048
NT = T // 128
NTH = TH // 128
EPS = 1e-5
ATT_SCALE = float(128 ** -0.5)
WTHRESH = 60.0
NEG = -1e30

_UN = [0]


def _u(n):
    _UN[0] += 1
    return f"{n}_{_UN[0]}"


def alibi_slopes(n_heads=32):
    cp2 = 2 ** math.floor(math.log2(n_heads))
    base = 2.0 ** (-(2.0 ** (-(math.log2(cp2) - 3))))
    slopes = [base ** k for k in range(1, cp2 + 1)]
    if cp2 != n_heads:
        eb = 2.0 ** (-(2.0 ** (-(math.log2(2 * cp2) - 3))))
        ne = min(cp2, n_heads - cp2)
        slopes += [eb ** k for k in range(1, 2 * ne + 1, 2)]
    return np.asarray(slopes, dtype=np.float32)


def ap(tsr, off, *dims):
    return bass.AP(tsr, off, [list(d) for d in dims])


_NOSYNC = {"wait_ge", "nop", "engine_nop", "register", "reg_mov", "print"}


class _Auto:

    def __init__(self, ch, eng):
        self._ch = ch
        self._eng = eng

    def matmul_q(self, *a, **k):
        return self._eng.matmul(*a, **k)

    def mark(self, inst):
        inst.then_inc(self._ch.csem, 1)
        self._ch.ct += 1
        return inst

    def __getattr__(self, name):
        target = getattr(self._eng, name)
        if name.startswith("_") or not callable(target) or name in _NOSYNC:
            return target

        def wrap(*a, **k):
            inst = target(*a, **k)
            if inst is None or not hasattr(inst, "then_inc"):
                return inst
            if name in ("dma_start", "indirect_dma_start"):
                from concourse.bass import BassGpSimd
                if isinstance(self._eng, BassGpSimd):
                    inst.then_inc(self._ch.gsem, 16)
                    self._ch.gt += 16
                else:
                    inst.then_inc(self._ch.dsem, 16)
                    self._ch.dt += 16
            else:
                inst.then_inc(self._ch.csem, 1)
                self._ch.ct += 1
                self._eng.wait_ge(self._ch.csem, self._ch.ct)
            return inst
        return wrap


class Chain:

    def __init__(self, nc):
        self.nc = nc
        self.block = nc.Block()
        self.block.__enter__()
        self.csem = nc.alloc_semaphore("ch_c")
        self.dsem = nc.alloc_semaphore("ch_d")
        self.gsem = nc.alloc_semaphore("ch_g")
        self.ct = 0
        self.dt = 0
        self.gt = 0
        self.last = {}

    def step(self, engine):
        def deco(fn):
            ct0, dt0, gt0 = self.ct, self.dt, self.gt

            def body(eng):
                if ct0 > 0 and self.last.get("c") != (engine, ct0):
                    eng.wait_ge(self.csem, ct0)
                if dt0 > 0 and self.last.get("d") != (engine, dt0):
                    eng.wait_ge(self.dsem, dt0)
                if gt0 > 0 and self.last.get("g") != (engine, gt0):
                    eng.wait_ge(self.gsem, gt0)
                fn(_Auto(self, eng), None)
            getattr(self.block, engine)(body)
            self.last["c"] = (engine, self.ct)
            self.last["d"] = (engine, self.dt)
            self.last["g"] = (engine, self.gt)
            return fn
        return deco

    def finish(self):
        ct, dt, gt = self.ct, self.dt, self.gt

        def fin(g):
            g.wait_ge(self.csem, ct)
            g.wait_ge(self.dsem, dt)
            g.wait_ge(self.gsem, gt)
        self.block.gpsimd(fin)
        self.block.__exit__(None, None, None)


def emit_ln(nc, ch, st, x_dram, xn_dram, ntiles):
    sb = lambda n, sh, dt: st.enter_context(nc.sbuf_tensor(_u(n), sh, dt))
    xt = sb("ln_xt", [128, D], F32)
    ot = sb("ln_ot", [128, D], BF16)
    scr = sb("ln_scr", [128, D], F32)
    sums = sb("ln_sums", [128, 1], F32)
    sumsq = sb("ln_sumsq", [128, 1], F32)
    nmu = sb("ln_nmu", [128, 1], F32)
    ve = sb("ln_ve", [128, 1], F32)
    rstd = sb("ln_rstd", [128, 1], F32)
    scr0 = sb("ln_scr0", [128, 1], F32)
    for j in range(ntiles):
        @ch.step("sync")
        def _(sync, sem=None):
            sync.dma_start(xt[:, :], x_dram[128 * j:128 * (j + 1), :])

        @ch.step("vector")
        def _(v, sem=None):
            v.tensor_scalar(scr[:, :], xt[:, :], 1.0, 0.0, ALU.mult, ALU.add,
                            accum_out=sums[:, :])
            v.scalar_tensor_tensor(scr[:, :], xt[:, :], 1.0, xt[:, :],
                                   ALU.mult, ALU.mult, accum_out=sumsq[:, :])
            v.tensor_scalar(nmu[:, :], sums[:, :], -1.0 / D, None, ALU.mult)
            v.tensor_scalar(ve[:, :], sumsq[:, :], 1.0 / D, EPS,
                            ALU.mult, ALU.add)
            v.tensor_tensor(scr0[:, :], nmu[:, :], nmu[:, :], ALU.mult)
            v.tensor_tensor(ve[:, :], ve[:, :], scr0[:, :], ALU.subtract)
            v.reciprocal(ve[:, :], ve[:, :])

        @ch.step("scalar")
        def _(s, sem=None):
            s.activation(rstd[:, :], ve[:, :], AF.Sqrt)

        @ch.step("vector")
        def _(v, sem=None):
            v.tensor_scalar(ot[:, :], xt[:, :], nmu[:, 0:1], rstd[:, 0:1],
                            ALU.add, ALU.mult)

        @ch.step("sync")
        def _(sync, sem=None):
            sync.dma_start(xn_dram[128 * j:128 * (j + 1), :], ot[:, :])


def emit_wln_chunk(ch, hsb_j, lwrep, lbrep, out_bf, tmp, width=MD):
    scr, sums, sumsq, nmu, ve, rstd, scr0, lnt = tmp

    @ch.step("vector")
    def _(v, sem=None):
        v.tensor_scalar(scr[:, :], hsb_j[:, :], 1.0, 0.0, ALU.mult, ALU.add,
                        accum_out=sums[:, :])
        v.scalar_tensor_tensor(scr[:, :], hsb_j[:, :], 1.0, hsb_j[:, :],
                               ALU.mult, ALU.mult, accum_out=sumsq[:, :])
        v.tensor_scalar(nmu[:, :], sums[:, :], -1.0 / width, None, ALU.mult)
        v.tensor_scalar(ve[:, :], sumsq[:, :], 1.0 / width, EPS,
                        ALU.mult, ALU.add)
        v.tensor_tensor(scr0[:, :], nmu[:, :], nmu[:, :], ALU.mult)
        v.tensor_tensor(ve[:, :], ve[:, :], scr0[:, :], ALU.subtract)
        v.reciprocal(ve[:, :], ve[:, :])

    @ch.step("scalar")
    def _(s, sem=None):
        s.activation(rstd[:, :], ve[:, :], AF.Sqrt)

    @ch.step("vector")
    def _(v, sem=None):
        v.tensor_scalar(lnt[:, :], hsb_j[:, :], nmu[:, 0:1], rstd[:, 0:1],
                        ALU.add, ALU.mult)
        v.tensor_tensor(lnt[:, :], lnt[:, :], lwrep[:, :], ALU.mult)
        v.tensor_tensor(out_bf[:, :], lnt[:, :], lbrep[:, :], ALU.add)


def emit_h0_gate0(nc, ch, st, dr):
    sb = lambda n, sh, dt: st.enter_context(nc.sbuf_tensor(_u(n), sh, dt))
    NV = D // 128
    zsn = sb("g0_zsn", [32, D], BF16)
    ssb = sb("g0_ssb", [32, D], BF16)
    with ExitStack() as dst:
        dsb = lambda n, sh, dt: dst.enter_context(
            nc.sbuf_tensor(_u(n), sh, dt))
        qwrep = dsb("g0_qwrep", [128, D], I32)
        srep = dsb("g0_srep", [128, D], BF16)
        w4g = dsb("g0_w4g", [128, D], BF16)
        sh_sb = dsb("g0_sh", [128, 1], I32)
        ztmp = dsb("g0_ztmp", [32, 512], I32)
        qzsb = dsb("g0_qz", [32, 512], I32)

        @ch.step("sync")
        def _(sync, sem=None):
            sync.dma_start(qzsb[:, :], dr["qz"][:, :])
            sync.dma_start(sh_sb[:, :], dr["shamt"][:, :])

        @ch.step("gpsimd")
        def _(gp, sem=None):
            gp.dma_start(ssb[0:32, :], dr["scales"][:, :])

        @ch.step("vector")
        def _(v, sem=None):
            for nib in range(8):
                v.tensor_scalar(ztmp[0:32, :], qzsb[0:32, :], 4 * nib, 15,
                                ALU.logical_shift_right, ALU.bitwise_and)
                v.tensor_copy(ap(zsn, nib, (D, 32), (8, 512)), ztmp[0:32, :])
            v.tensor_tensor(zsn[0:32, :], zsn[0:32, :], ssb[0:32, :], ALU.mult)
            v.tensor_scalar(zsn[0:32, :], zsn[0:32, :], -1.0, None, ALU.mult)

        for g in range(32):
            @ch.step("sync")
            def _(sync, sem=None):
                sync.dma_start(qwrep[:, :],
                               bass.AP(dr["qw"], 16 * g * D,
                                       [[D, 16], [0, 8], [1, D]]))

            @ch.step("gpsimd")
            def _(gp, sem=None):
                gp.dma_start(srep[:, :],
                             bass.AP(dr["scales"], g * D, [[0, 128], [1, D]]))

            @ch.step("vector")
            def _(v, sem=None):
                v.tensor_scalar(qwrep[:, :], qwrep[:, :], sh_sb[:, 0:1], 15,
                                ALU.logical_shift_right, ALU.bitwise_and)
                v.tensor_copy(w4g[:, :], qwrep[:, :])
                v.tensor_tensor(w4g[:, :], w4g[:, :], srep[:, :], ALU.mult)

            @ch.step("sync")
            def _(sync, sem=None):
                sync.dma_start(dr["w4_scr"][128 * g:128 * (g + 1), :],
                               w4g[:, :])

    xsb = [sb(f"hg_x{v}", [128, TH], BF16) for v in range(NV)]
    winsb = sb("hg_win", [128, NV * MD], BF16)
    brep = sb("hg_bin", [128, MD], F32)
    b0rep = sb("hg_b0", [128, 32], F32)
    b0h = sb("hg_b0h", [128, 32], F16)
    hout = sb("hg_hout", [128, MD], F32)
    w4c = sb("hg_w4c", [128, D], BF16)
    gout = sb("hg_gout", [128, TH], BF16)
    xs_sb = sb("hg_xs", [32, TH], BF16)
    OHsb = sb("hg_OH", [128, 32 * 32], BF16)
    h_ps = [st.enter_context(nc.psum_tensor(_u("hg_hps"), [128, MD], F32))
            for _i in range(2)]
    xs_ps = [st.enter_context(nc.psum_tensor(_u("hg_xsps"), [32, 512], F32))
             for _i in range(2)]
    g_ps = st.enter_context(nc.psum_tensor(_u("hg_gps"), [128, TH], F32))
    OH32_np = np.zeros((128, 32 * 32), dtype=BF)
    for g in range(32):
        OH32_np[:, g * 32 + g] = 1.0
    OH32_d = nc.inline_tensor(np.asarray(OH32_np), "hg_OH_d")

    @ch.step("sync")
    def _(sync, sem=None):
        sync.dma_start(OHsb[:, :], OH32_d[:, :])
        sync.dma_start(brep[:, :], bass.AP(dr["binm"], 0, [[0, 128], [1, MD]]))
        sync.dma_start(b0h[:, :], dr["biasc0"][:, :])

    @ch.step("gpsimd")
    def _(gp, sem=None):
        for v in range(NV):
            gp.dma_start(winsb[:, v * MD:(v + 1) * MD],
                         dr["WinT"][128 * v:128 * (v + 1), :])

    @ch.step("vector")
    def _(v, sem=None):
        v.tensor_copy(b0rep[:, :], b0h[:, :])

    for th in range(2):
        t0 = th * TH

        @ch.step("sync")
        def _(sync, sem=None):
            for v in range(NV):
                sync.dma_start(xsb[v][:, :],
                               dr["xn_scr"][t0:t0 + TH,
                                            128 * v:128 * (v + 1)],
                               transpose=True)

        for tq in range(4):
            @ch.step("tensor")
            def _(te, sem=None):
                for g in range(NV):
                    mi = te.matmul_q(xs_ps[tq % 2][0:32, :],
                                     OHsb[:, 32 * g:32 * (g + 1)],
                                     xsb[g][:, 512 * tq:512 * (tq + 1)],
                                     start=(g == 0), stop=(g == NV - 1),
                                     skip_group_check=True)
                te.mark(mi)

            @ch.step("vector")
            def _(v, sem=None):
                v.tensor_copy(xs_sb[0:32, 512 * tq:512 * (tq + 1)],
                              xs_ps[tq % 2][0:32, :])

        for t in range(NTH):
            @ch.step("tensor")
            def _(te, sem=None):
                for v in range(NV):
                    mi = te.matmul_q(h_ps[t % 2][:, :],
                                     xsb[v][:, 128 * t:128 * (t + 1)],
                                     winsb[:, v * MD:(v + 1) * MD],
                                     start=(v == 0), stop=(v == NV - 1))
                te.mark(mi)

            @ch.step("vector")
            def _(v, sem=None):
                v.tensor_tensor(hout[:, :], h_ps[t % 2][:, :], brep[:, :],
                                ALU.add)

            @ch.step("sync")
            def _(sync, sem=None):
                sync.dma_start(
                    dr["h_scr"][t0 + 128 * t:t0 + 128 * (t + 1), :],
                    hout[:, :])

        for oc in range(32):
            @ch.step("sync")
            def _(sync, sem=None):
                sync.dma_start(
                    ap(w4c, 0, (D, 128), (128, 32), (1, 128)),
                    ap(dr["w4_scr"], oc * 128, (D, 128), (128 * D, 32),
                       (1, 128)))

            @ch.step("tensor")
            def _(te, sem=None):
                for tq in range(4):
                    for g in range(NV):
                        te.matmul_q(g_ps[:, 512 * tq:512 * (tq + 1)],
                                    w4c[:, 128 * g:128 * (g + 1)],
                                    xsb[g][:, 512 * tq:512 * (tq + 1)],
                                    start=(g == 0), stop=False,
                                    skip_group_check=True)
                    mi = te.matmul_q(g_ps[:, 512 * tq:512 * (tq + 1)],
                                     zsn[0:32, 128 * oc:128 * (oc + 1)],
                                     xs_sb[0:32, 512 * tq:512 * (tq + 1)],
                                     start=False, stop=True,
                                     skip_group_check=True)
                te.mark(mi)

            @ch.step("scalar")
            def _(s, sem=None):
                s.activation(gout[:, :], g_ps[:, :], AF.Silu,
                             bias=b0rep[:, oc:oc + 1])

            @ch.step("sync")
            def _(sync, sem=None):
                sync.dma_start(dr["g0T_scr"][128 * oc:128 * (oc + 1),
                                             t0:t0 + TH], gout[:, :])


def _dft_consts_np():
    i = np.arange(128)
    f = np.arange(128)
    ang = 2 * np.pi * np.outer(i, f) / 256.0
    WfrT = np.cos(ang).astype(BF)
    WfiT = (-np.sin(ang)).astype(BF)
    alt = ((-1.0) ** i).astype(BF)[:, None]
    w = np.full(128, 2.0)
    w[0] = 1.0
    angl = 2 * np.pi * np.outer(f, np.arange(128)) / 256.0
    VcT = (w[:, None] * np.cos(angl) / 256.0).astype(BF)
    VsT = (-w[:, None] * np.sin(angl) / 256.0).astype(BF)
    altl = ((((-1.0) ** np.arange(128)) / 256.0).astype(BF))[None, :]
    ident = np.eye(128, dtype=np.float32)
    OH = np.zeros((128, 256), dtype=BF)
    for j in range(16):
        OH[:, j * 16 + j] = 1.0
    return WfrT, WfiT, alt, VcT, VsT, altl, ident, OH


def emit_mini_stack(nc, ch, st, dr):
    sb = lambda n, sh, dt: st.enter_context(nc.sbuf_tensor(_u(n), sh, dt))
    consts = _dft_consts_np()
    names = ["WfrT", "WfiT", "alt", "VcT", "VsT", "altl", "ident", "OH"]
    shapes = [[128, 128], [128, 128], [128, 1], [128, 128], [128, 128],
              [1, 128], [128, 128], [128, 256]]
    dts = [BF16, BF16, BF16, BF16, BF16, BF16, F32, BF16]
    csb = {}
    cds = {}
    for nm, shp, dty, arr in zip(names, shapes, dts, consts):
        csb[nm] = sb("m_" + nm, shp, dty)
        cds[nm] = nc.inline_tensor(np.asarray(arr), "m_d_" + nm)
    hsb = [sb(f"m_h{t}", [128, MD], F32) for t in range(NT)]
    cmask = sb("m_cmask", [128, 8], F32)

    @ch.step("sync")
    def _(sync, sem=None):
        for nm in names:
            sync.dma_start(csb[nm][:, :], cds[nm][:, :])
        sync.dma_start(cmask[:, :], dr["cmask"][:, :])
        for t in range(NT):
            sync.dma_start(hsb[t][:, :], dr["h_scr"][128 * t:128 * (t + 1), :])

    for dd in range(4):
        _emit_mini_layer(nc, ch, st, dr, dd, hsb, csb, cmask)

    with ExitStack() as fst:
        fsb = lambda n, sh, dt: fst.enter_context(
            nc.sbuf_tensor(_u(n), sh, dt))
        lwrep = fsb("sf_lw", [128, MD], F32)
        lbrep = fsb("sf_lb", [128, MD], F32)
        borep = fsb("sf_bo", [128, D], F32)
        lny = fsb("sf_lny", [128, MD], BF16)
        hnT = [fsb(f"sf_hnT{cb}", [128, T], BF16) for cb in range(4)]
        wsb = [fsb(f"sf_w{cb}", [128, D], BF16) for cb in range(4)]
        sout = fsb("sf_sout", [128, D], BF16)
        tmp = [fsb("sf_scr", [128, MD], F32), fsb("sf_su", [128, 1], F32),
               fsb("sf_sq", [128, 1], F32), fsb("sf_nm", [128, 1], F32),
               fsb("sf_ve", [128, 1], F32), fsb("sf_rs", [128, 1], F32),
               fsb("sf_s0", [128, 1], F32), fsb("sf_lnt", [128, MD], F32)]
        psum = fst.enter_context(nc.psum_tensor(_u("sf_ps"), [128, D], F32))

        @ch.step("sync")
        def _(sync, sem=None):
            sync.dma_start(lwrep[:, :],
                           bass.AP(dr["lnlw"], 0, [[0, 128], [1, MD]]))
            sync.dma_start(lbrep[:, :],
                           bass.AP(dr["lnlb"], 0, [[0, 128], [1, MD]]))
            sync.dma_start(borep[:, :],
                           bass.AP(dr["bout"], 0, [[0, 128], [1, D]]))

        @ch.step("gpsimd")
        def _(gp, sem=None):
            for cb in range(4):
                gp.dma_start(wsb[cb][:, :],
                             dr["WoutT"][128 * cb:128 * (cb + 1), :])

        for j in range(NT):
            emit_wln_chunk(ch, hsb[j], lwrep, lbrep, lny, tmp)

            @ch.step("sync")
            def _(sync, sem=None):
                sync.dma_start(dr["lny_scr"][128 * j:128 * (j + 1), :],
                               lny[:, :])

        @ch.step("sync")
        def _(sync, sem=None):
            for cb in range(4):
                sync.dma_start(hnT[cb][:, :],
                               dr["lny_scr"][:, 128 * cb:128 * (cb + 1)],
                               transpose=True)

        for t in range(NT):
            @ch.step("tensor")
            def _(te, sem=None):
                for oq in range(8):
                    for cb in range(4):
                        mi = te.matmul_q(psum[:, 512 * oq:512 * (oq + 1)],
                                         hnT[cb][:, 128 * t:128 * (t + 1)],
                                         wsb[cb][:, 512 * oq:512 * (oq + 1)],
                                         start=(cb == 0), stop=(cb == 3))
                te.mark(mi)

            @ch.step("vector")
            def _(v, sem=None):
                v.tensor_tensor(sout[:, :], psum[:, :], borep[:, :], ALU.add)

            @ch.step("sync")
            def _(sync, sem=None):
                sync.dma_start(dr["s0_scr"][128 * t:128 * (t + 1), :],
                               sout[:, :])


def _emit_mini_layer(nc, ch, st, dr, dd, hsb, cc, cmask):
    with ExitStack() as lst:
        sb = lambda n, sh, dt: lst.enter_context(nc.sbuf_tensor(_u(n), sh, dt))
        WfrT, WfiT, alt = cc["WfrT"], cc["WfiT"], cc["alt"]
        VcT, VsT, altl = cc["VcT"], cc["VsT"], cc["altl"]
        ident, OH = cc["ident"], cc["OH"]
        PIT = lambda t_: t_.shape[1]

        lnt = sb("ml_lnt", [128, MD], F32)
        sums = sb("ml_sums", [128, 1], F32)
        sumsq = sb("ml_sumsq", [128, 1], F32)
        nmu = sb("ml_nmu", [128, 1], F32)
        ve = sb("ml_ve", [128, 1], F32)
        rstd = sb("ml_rstd", [128, 1], F32)
        scr = sb("ml_scr", [128, MD], F32)
        scr0 = sb("ml_scr0", [128, 1], F32)
        pst = ExitStack()
        sv = lambda n, sh, dt: pst.enter_context(
            nc.sbuf_tensor(_u(n), sh, dt))
        lwrep = sv("ml_lw", [128, MD], F32)
        lbrep = sv("ml_lb", [128, MD], F32)
        sclrep = sv("ml_scl", [128, MD], F32)
        prep_r = sv("ml_pr", [128, MD], F32)
        prep_i = sv("ml_pi", [128, MD], F32)
        pz = sv("ml_pz", [128, 8], F32)
        psc = sv("ml_psc", [128, 8], F32)
        przr = sv("ml_przr", [1, MD], F32)
        przi = sv("ml_przi", [1, MD], F32)
        rowsq = sv("ml_rowsq", [1, MD], F32)
        rowtmp = sv("ml_rowtmp", [1, MD], F32)
        prowpr = sv("ml_prowpr", [1, MD], F32)
        prowpi = sv("ml_prowpi", [1, MD], F32)
        tr8 = sv("ml_tr8", [128, 8], F32)
        Pcm_r = sv("ml_Pcmr", [128, 4 * 132], F32)
        Pcm_i = sv("ml_Pcmi", [128, 4 * 132], F32)
        tmpA = sv("ml_tmpA", [128, 4 * 64], F32)
        tmpB = sv("ml_tmpB", [128, 4 * 64], F32)
        Pmc_r = sv("ml_Pmcr", [128, MD], F32)
        Pmc_i = sv("ml_Pmci", [128, MD], F32)
        Pmc_rbf = sv("ml_Pmcrb", [128, MD], BF16)
        Prev_r = sv("ml_Prevr", [128, MD], F32)
        Prev_i = sv("ml_Previ", [128, MD], F32)
        frrev = sv("ml_frrev", [128, MD], BF16)
        firev = sv("ml_firev", [128, MD], BF16)
        Frr = sv("ml_Frr", [128, MD], BF16)
        Fri = sv("ml_Fri", [128, MD], BF16)
        Frr128 = sv("ml_Frr128", [1, MD], BF16)
        Pp1r = sv("ml_Pp1r", [128, MD], BF16)
        Pp1i = sv("ml_Pp1i", [128, MD], BF16)
        tbig1 = sv("ml_tb1", [128, MD], F32)
        tbig2 = sv("ml_tb2", [128, MD], F32)
        P128 = sv("ml_P128", [128, 8], F32)
        P128p = [sv(f"ml_P128p{k}", [128, 8], F32) for k in range(4)]
        conv_sb = sv("ml_conv", [128, MD], F32)
        ysb = sv("ml_ysb", [128, MD], BF16)
        Yr_sb = sv("ml_Yr", [128, MD], BF16)
        Yi_sb = sv("ml_Yi", [128, MD], BF16)
        yr128 = sv("ml_yr128", [1, MD], BF16)
        Hr_sb = sv("ml_Hr", [128, MD], BF16)
        Hi_sb = sv("ml_Hi", [128, MD], BF16)
        Hr128 = sv("ml_Hr128", [1, MD], BF16)
        tre = sv("ml_tre", [128, MD], BF16)
        tim = sv("ml_tim", [128, MD], BF16)
        tA = sv("ml_tA", [128, MD], BF16)
        tB = sv("ml_tB", [128, MD], BF16)
        U_r = sv("ml_Ur", [128, 64], F32)
        U_i = sv("ml_Ui", [128, 64], F32)
        Urow_r = sv("ml_Urowr", [16, MD], F32)
        Urow_i = sv("ml_Urowi", [16, MD], F32)
        PPc_r = sv("ml_PPcr", [128, 64], F32)
        PPc_i = sv("ml_PPci", [128, 64], F32)
        ts1 = sv("ml_ts1", [128, 64], F32)
        ts2 = sv("ml_ts2", [128, 64], F32)
        ts3 = sv("ml_ts3", [128, 64], F32)
        ts4 = sv("ml_ts4", [128, 64], F32)
        csh_r = sv("ml_cshr", [128, 64], F32)
        csh_i = sv("ml_cshi", [128, 64], F32)
        crow_sb = sv("ml_crow", [16, 2 * MD], F32)
        ci_sb = sv("ml_ci", [128, 8], F32)
        ci2_sb = sv("ml_ci2", [128, 8], F32)
        cfin_sb = sv("ml_cfin", [128, 8], F32)
        cinj_sb = sv("ml_cinj", [128, 8], F32)
        crj_r = sv("ml_crjr", [128, MD], BF16)
        crj_i = sv("ml_crji", [128, MD], BF16)

        @ch.step("sync")
        def _(sync, sem=None):
            for dst_, src_ in [(lwrep, dr[f"lnw{dd}"]),
                               (lbrep, dr[f"lnb{dd}"]),
                               (sclrep, dr[f"scl{dd}"])]:
                sync.dma_start(dst_[:, :],
                               bass.AP(src_, 0, [[0, 128], [1, MD]]))
            sync.dma_start(pz[:, :], dr[f"phzs{dd}"][:, :])
            sync.dma_start(przr[:, :], dr[f"phzrow{dd}"][0:1, :])
            sync.dma_start(przi[:, :], dr[f"phzrow{dd}"][1:2, :])
            sync.dma_start(ci_sb[:, :], dr[f"cinit{dd}"][:, :])
            sync.dma_start(cinj_sb[:, :], dr[f"cinj{dd}"][:, :])

        psm = lambda n, shp: pst.enter_context(
            nc.psum_tensor(_u(n), shp, F32))
        Yr_ps = psm("ml_Yrps", [128, MD])
        Yi_ps = psm("ml_Yips", [128, MD])
        yrow_ps = psm("ml_yrps", [1, MD])
        conv_ps = psm("ml_cps", [128, MD])
        Ur_ps = psm("ml_Urps", [16, MD])
        Ui_ps = psm("ml_Uips", [16, MD])
        tr_ps = psm("ml_trps", [128, 128])

        @ch.step("vector")
        def _(v, sem=None):
            v.tensor_tensor(tr8[:, 0:4], pz[:, 0:4], pz[:, 0:4], ALU.mult)
            v.tensor_tensor(tr8[:, 4:8], pz[:, 4:8], pz[:, 4:8], ALU.mult)
            v.tensor_tensor(tr8[:, 0:4], tr8[:, 0:4], tr8[:, 4:8],
                            ALU.add)
            v.tensor_tensor(rowsq[:, :], przr[:, :], przr[:, :], ALU.mult)
            v.tensor_tensor(rowtmp[:, :], przi[:, :], przi[:, :],
                            ALU.mult)
            v.tensor_tensor(rowsq[:, :], rowsq[:, :], rowtmp[:, :],
                            ALU.add)

        @ch.step("scalar")
        def _(s, sem=None):
            s.activation(tr8[:, 4:8], tr8[:, 0:4], AF.Sqrt)
            s.activation(tr8[:, 0:4], tr8[:, 4:8], AF.Tanh)
            s.activation(rowsq[:, :], rowsq[:, :], AF.Sqrt)
            s.activation(rowtmp[:, :], rowsq[:, :], AF.Tanh)

        @ch.step("vector")
        def _(v, sem=None):
            v.reciprocal(tr8[:, 4:8], tr8[:, 4:8])
            v.tensor_tensor(tr8[:, 4:8], tr8[:, 4:8], tr8[:, 0:4],
                            ALU.mult)
            v.tensor_tensor(psc[:, 0:4], pz[:, 0:4], tr8[:, 4:8],
                            ALU.mult)
            v.tensor_tensor(psc[:, 4:8], pz[:, 4:8], tr8[:, 4:8],
                            ALU.mult)
            v.reciprocal(rowsq[:, :], rowsq[:, :])
            v.tensor_tensor(rowsq[:, :], rowsq[:, :], rowtmp[:, :],
                            ALU.mult)
            v.tensor_tensor(prowpr[:, :], przr[:, :], rowsq[:, :],
                            ALU.mult)
            v.tensor_tensor(prowpi[:, :], przi[:, :], rowsq[:, :],
                            ALU.mult)

        @ch.step("gpsimd")
        def _(g, sem=None):
            g.dma_start(dr["prow_scr"][0:1, :], prowpr[:, :])
            g.dma_start(dr["prow_scr"][1:2, :], prowpi[:, :])

        @ch.step("gpsimd")
        def _(g, sem=None):
            g.dma_start(prep_r[:, :],
                        bass.AP(dr["prow_scr"], 0, [[0, 128], [1, MD]]))
            g.dma_start(prep_i[:, :],
                        bass.AP(dr["prow_scr"], MD, [[0, 128], [1, MD]]))

        @ch.step("vector")
        def _(v, sem=None):
            for cb in range(4):
                o = cb * 132
                v.memset(Pcm_r[:, o:o + 1], 1.0)
                v.memset(Pcm_i[:, o:o + 1], 0.0)
                v.tensor_copy(Pcm_r[:, o + 1:o + 2], psc[:, cb:cb + 1])
                v.tensor_copy(Pcm_i[:, o + 1:o + 2],
                              psc[:, 4 + cb:5 + cb])
            w = 1
            for k in range(7):
                sl = lambda tsr, off, cnt: ap(tsr, off, (PIT(tsr), 128),
                                              (132, 4), (1, cnt))
                Ar, Ai = sl(Pcm_r, 1, w), sl(Pcm_i, 1, w)
                Br = ap(Pcm_r, w, (PIT(Pcm_r), 128), (132, 4), (0, w))
                Bi = ap(Pcm_i, w, (PIT(Pcm_i), 128), (132, 4), (0, w))
                Or, Oi = sl(Pcm_r, w + 1, w), sl(Pcm_i, w + 1, w)
                t1 = ap(tmpA, 0, (PIT(tmpA), 128), (64, 4), (1, w))
                t2 = ap(tmpB, 0, (PIT(tmpB), 128), (64, 4), (1, w))
                v.tensor_tensor(t1, Ar, Br, ALU.mult)
                v.tensor_tensor(t2, Ai, Bi, ALU.mult)
                v.tensor_tensor(Or, t1, t2, ALU.subtract)
                v.tensor_tensor(t1, Ar, Bi, ALU.mult)
                v.tensor_tensor(t2, Ai, Br, ALU.mult)
                v.tensor_tensor(Oi, t1, t2, ALU.add)
                w *= 2

        for cb in range(4):
            for ri in range(2):
                @ch.step("tensor")
                def _(te, sem=None):
                    src = [Pcm_r, Pcm_i][ri]
                    te.transpose(tr_ps[:, :],
                                 src[:, cb * 132:cb * 132 + 128],
                                 ident[:, :])

                @ch.step("vector")
                def _(v, sem=None):
                    dst = [Pmc_r, Pmc_i][ri]
                    v.tensor_copy(dst[:, cb * 128:(cb + 1) * 128],
                                  tr_ps[:, :])

        @ch.step("vector")
        def _(v, sem=None):
            v.tensor_copy(ap(Prev_r, 0, (PIT(Prev_r), 128), (128, 4),
                             (1, 128)),
                          ap(Pcm_r, 127, (PIT(Pcm_r), 128), (132, 4),
                             (-1, 128)))
            v.tensor_copy(ap(Prev_i, 0, (PIT(Prev_i), 128), (128, 4),
                             (1, 128)),
                          ap(Pcm_i, 127, (PIT(Pcm_i), 128), (132, 4),
                             (-1, 128)))

        for cb in range(4):
            for ri in range(2):
                @ch.step("tensor")
                def _(te, sem=None):
                    src = [Prev_r, Prev_i][ri]
                    te.transpose(tr_ps[:, :],
                                 src[:, cb * 128:(cb + 1) * 128],
                                 ident[:, :])

                @ch.step("vector")
                def _(v, sem=None):
                    dst = [frrev, firev][ri]
                    v.tensor_copy(dst[:, cb * 128:(cb + 1) * 128],
                                  tr_ps[:, :])

        @ch.step("vector")
        def _(v, sem=None):
            v.tensor_copy(Pmc_rbf[:, :], Pmc_r[:, :])

        @ch.step("tensor")
        def _(te, sem=None):
            te.matmul_q(Yr_ps[:, :], WfrT[:, :], Pmc_rbf[:, :],
                        start=True, stop=True)
            te.matmul_q(Yi_ps[:, :], WfiT[:, :], Pmc_rbf[:, :],
                        start=True, stop=True)
            te.matmul(yrow_ps[0:1, :], alt[:, :], Pmc_rbf[:, :],
                      start=True, stop=True)

        @ch.step("vector")
        def _(v, sem=None):
            v.tensor_tensor(Frr[:, :], Yr_ps[:, :], sclrep[:, :],
                            ALU.mult)
            v.tensor_tensor(Fri[:, :], Yi_ps[:, :], sclrep[:, :],
                            ALU.mult)
            v.tensor_tensor(Frr128[0:1, :], yrow_ps[0:1, :],
                            sclrep[0:1, :], ALU.mult)

        @ch.step("vector")
        def _(v, sem=None):
            v.tensor_tensor(tbig1[:, :], Pmc_r[:, :], prep_r[:, :],
                            ALU.mult)
            v.tensor_tensor(tbig2[:, :], Pmc_i[:, :], prep_i[:, :],
                            ALU.mult)
            v.tensor_tensor(tbig1[:, :], tbig1[:, :], tbig2[:, :],
                            ALU.subtract)
            v.tensor_tensor(scr[:, :], Pmc_r[:, :], prep_i[:, :],
                            ALU.mult)
            v.tensor_tensor(tbig2[:, :], Pmc_i[:, :], prep_r[:, :],
                            ALU.mult)
            v.tensor_tensor(tbig2[:, :], scr[:, :], tbig2[:, :], ALU.add)
            v.tensor_tensor(Pp1r[:, :], tbig1[:, :], sclrep[:, :],
                            ALU.mult)
            v.tensor_tensor(Pp1i[:, :], tbig2[:, :], sclrep[:, :],
                            ALU.mult)
            v.tensor_copy(P128[:, 0:4],
                          ap(Pcm_r, 128, (PIT(Pcm_r), 128), (132, 4)))
            v.tensor_copy(P128[:, 4:8],
                          ap(Pcm_i, 128, (PIT(Pcm_i), 128), (132, 4)))
            v.tensor_copy(P128p[0][:, :], P128[:, :])
            for k in range(1, 4):
                a = P128p[k - 1]
                o = P128p[k]
                v.tensor_tensor(tr8[:, 0:4], a[:, 0:4], a[:, 0:4],
                                ALU.mult)
                v.tensor_tensor(tr8[:, 4:8], a[:, 4:8], a[:, 4:8],
                                ALU.mult)
                v.tensor_tensor(o[:, 0:4], tr8[:, 0:4], tr8[:, 4:8],
                                ALU.subtract)
                v.tensor_tensor(tr8[:, 0:4], a[:, 0:4], a[:, 4:8],
                                ALU.mult)
                v.tensor_tensor(o[:, 4:8], tr8[:, 0:4], tr8[:, 0:4],
                                ALU.add)

        for seg in range(2):
            j0 = seg * 16
            cinit = ci_sb if seg == 0 else ci2_sb
            for jl in range(16):
                j = j0 + jl

                @ch.step("vector")
                def _(v, sem=None):
                    v.tensor_scalar(scr[:, :], hsb[j][:, :], 1.0, 0.0,
                                    ALU.mult, ALU.add,
                                    accum_out=sums[:, :])
                    v.scalar_tensor_tensor(scr[:, :], hsb[j][:, :], 1.0,
                                           hsb[j][:, :], ALU.mult,
                                           ALU.mult,
                                           accum_out=sumsq[:, :])
                    v.tensor_scalar(nmu[:, :], sums[:, :], -1.0 / MD,
                                    None, ALU.mult)
                    v.tensor_scalar(ve[:, :], sumsq[:, :], 1.0 / MD, EPS,
                                    ALU.mult, ALU.add)
                    v.tensor_tensor(scr0[:, :], nmu[:, :], nmu[:, :],
                                    ALU.mult)
                    v.tensor_tensor(ve[:, :], ve[:, :], scr0[:, :],
                                    ALU.subtract)
                    v.reciprocal(ve[:, :], ve[:, :])

                @ch.step("scalar")
                def _(s, sem=None):
                    s.activation(rstd[:, :], ve[:, :], AF.Sqrt)

                @ch.step("vector")
                def _(v, sem=None):
                    v.tensor_scalar(lnt[:, :], hsb[j][:, :], nmu[:, 0:1],
                                    rstd[:, 0:1], ALU.add, ALU.mult)
                    v.tensor_tensor(lnt[:, :], lnt[:, :], lwrep[:, :],
                                    ALU.mult)
                    v.tensor_tensor(lnt[:, :], lnt[:, :], lbrep[:, :],
                                    ALU.add)

                @ch.step("scalar")
                def _(s, sem=None):
                    s.activation(ysb[:, :], lnt[:, :], AF.Silu)

                @ch.step("tensor")
                def _(te, sem=None):
                    te.matmul_q(Yr_ps[:, :], WfrT[:, :], ysb[:, :],
                                start=True, stop=True)
                    te.matmul_q(Yi_ps[:, :], WfiT[:, :], ysb[:, :],
                                start=True, stop=True)
                    te.matmul(yrow_ps[0:1, :], alt[:, :], ysb[:, :],
                              start=True, stop=True)

                @ch.step("scalar")
                def _(s, sem=None):
                    s.copy(Yr_sb[:, :], Yr_ps[:, :])
                    s.copy(Yi_sb[:, :], Yi_ps[:, :])
                    s.copy(yr128[0:1, :], yrow_ps[0:1, :])

                @ch.step("vector")
                def _(v, sem=None):
                    v.tensor_tensor(tA[:, :], Yr_sb[:, :], Frr[:, :],
                                    ALU.mult)
                    v.tensor_tensor(tB[:, :], Yi_sb[:, :], Fri[:, :],
                                    ALU.mult)
                    v.tensor_tensor(Hr_sb[:, :], tA[:, :], tB[:, :],
                                    ALU.subtract)
                    v.tensor_tensor(tA[:, :], Yr_sb[:, :], Fri[:, :],
                                    ALU.mult)
                    v.tensor_tensor(tB[:, :], Yi_sb[:, :], Frr[:, :],
                                    ALU.mult)
                    v.tensor_tensor(Hi_sb[:, :], tA[:, :], tB[:, :],
                                    ALU.add)
                    v.tensor_tensor(Hr128[0:1, :], yr128[0:1, :],
                                    Frr128[0:1, :], ALU.mult)
                    v.tensor_tensor(tre[:, :], ysb[:, :], frrev[:, :],
                                    ALU.mult)
                    v.tensor_tensor(tim[:, :], ysb[:, :], firev[:, :],
                                    ALU.mult)

                @ch.step("tensor")
                def _(te, sem=None):
                    te.matmul_q(Ur_ps[0:16, :],
                                OH[:, jl * 16:(jl + 1) * 16],
                                tre[:, :], start=(jl == 0),
                                stop=(jl == 15), skip_group_check=True)
                    te.matmul_q(Ui_ps[0:16, :],
                                OH[:, jl * 16:(jl + 1) * 16],
                                tim[:, :], start=(jl == 0),
                                stop=(jl == 15), skip_group_check=True)
                    te.matmul_q(conv_ps[:, :], VcT[:, :], Hr_sb[:, :],
                                start=True, stop=False,
                                skip_group_check=True)
                    te.matmul_q(conv_ps[:, :], VsT[:, :], Hi_sb[:, :],
                                start=False, stop=False,
                                skip_group_check=True)
                    te.matmul(conv_ps[:, :], altl[0:1, :], Hr128[0:1, :],
                              start=False, stop=True,
                              skip_group_check=True)

                @ch.step("scalar")
                def _(s, sem=None):
                    s.copy(conv_sb[:, :], conv_ps[:, :])

                @ch.step("vector")
                def _(v, sem=None):
                    v.tensor_tensor(hsb[j][:, :], hsb[j][:, :],
                                    conv_sb[:, :], ALU.add)

            @ch.step("vector")
            def _(v, sem=None):
                v.tensor_copy(Urow_r[:, :], Ur_ps[:, :])
                v.tensor_copy(Urow_i[:, :], Ui_ps[:, :])

            for cb in range(4):
                for ri in range(2):
                    @ch.step("tensor")
                    def _(te, sem=None):
                        src = [Urow_r, Urow_i][ri]
                        te.transpose(tr_ps[0:128, 0:16],
                                     src[0:16, cb * 128:(cb + 1) * 128],
                                     ident[0:16, 0:16])

                    @ch.step("vector")
                    def _(v, sem=None):
                        dst = [U_r, U_i][ri]
                        v.tensor_copy(dst[:, cb * 16:(cb + 1) * 16],
                                      tr_ps[0:128, 0:16])

            @ch.step("vector")
            def _(v, sem=None):
                w = 1
                for k in range(4):
                    n = 16 - w
                    sl = lambda tsr, off: ap(tsr, off, (PIT(tsr), 128),
                                             (16, 4), (1, n))
                    zsr, zsi = sl(U_r, 0), sl(U_i, 0)
                    tgr, tgi = sl(U_r, w), sl(U_i, w)
                    Pr = ap(P128p[k], 0, (8, 128), (1, 4), (0, n))
                    Pi = ap(P128p[k], 4, (8, 128), (1, 4), (0, n))
                    t1 = ap(ts1, 0, (PIT(ts1), 128), (16, 4), (1, n))
                    t2 = ap(ts2, 0, (PIT(ts2), 128), (16, 4), (1, n))
                    t3 = ap(ts3, 0, (PIT(ts3), 128), (16, 4), (1, n))
                    t4 = ap(ts4, 0, (PIT(ts4), 128), (16, 4), (1, n))
                    v.tensor_tensor(t1, zsr, Pr, ALU.mult)
                    v.tensor_tensor(t2, zsi, Pi, ALU.mult)
                    v.tensor_tensor(t3, zsr, Pi, ALU.mult)
                    v.tensor_tensor(t4, zsi, Pr, ALU.mult)
                    v.tensor_tensor(t1, t1, t2, ALU.subtract)
                    v.tensor_tensor(t3, t3, t4, ALU.add)
                    v.tensor_tensor(tgr, tgr, t1, ALU.add)
                    v.tensor_tensor(tgi, tgi, t3, ALU.add)
                    w *= 2
                v.tensor_copy(ap(PPc_r, 0, (PIT(PPc_r), 128), (16, 4)),
                              P128[:, 0:4])
                v.tensor_copy(ap(PPc_i, 0, (PIT(PPc_i), 128), (16, 4)),
                              P128[:, 4:8])
                w = 1
                for k in range(4):
                    n = min(2 * w, 16) - w
                    sl2 = lambda tsr, off: ap(tsr, off, (PIT(tsr), 128),
                                              (16, 4), (1, n))
                    ar, ai = sl2(PPc_r, 0), sl2(PPc_i, 0)
                    tgr, tgi = sl2(PPc_r, w), sl2(PPc_i, w)
                    Pr = ap(P128p[k], 0, (8, 128), (1, 4), (0, n))
                    Pi = ap(P128p[k], 4, (8, 128), (1, 4), (0, n))
                    t1 = ap(ts1, 0, (PIT(ts1), 128), (16, 4), (1, n))
                    t2 = ap(ts2, 0, (PIT(ts2), 128), (16, 4), (1, n))
                    v.tensor_tensor(t1, ar, Pr, ALU.mult)
                    v.tensor_tensor(t2, ai, Pi, ALU.mult)
                    v.tensor_tensor(tgr, t1, t2, ALU.subtract)
                    v.tensor_tensor(t1, ar, Pi, ALU.mult)
                    v.tensor_tensor(t2, ai, Pr, ALU.mult)
                    v.tensor_tensor(tgi, t1, t2, ALU.add)
                    w *= 2
                cir = ap(cinit, 0, (8, 128), (1, 4), (0, 16))
                cii = ap(cinit, 4, (8, 128), (1, 4), (0, 16))
                sl3 = lambda tsr: ap(tsr, 0, (PIT(tsr), 128), (16, 4),
                                     (1, 16))
                v.tensor_tensor(sl3(ts1), sl3(PPc_r), cir, ALU.mult)
                v.tensor_tensor(sl3(ts2), sl3(PPc_i), cii, ALU.mult)
                v.tensor_tensor(sl3(ts1), sl3(ts1), sl3(ts2),
                                ALU.subtract)
                v.tensor_tensor(sl3(U_r), sl3(U_r), sl3(ts1), ALU.add)
                v.tensor_tensor(sl3(ts1), sl3(PPc_r), cii, ALU.mult)
                v.tensor_tensor(sl3(ts2), sl3(PPc_i), cir, ALU.mult)
                v.tensor_tensor(sl3(ts1), sl3(ts1), sl3(ts2), ALU.add)
                v.tensor_tensor(sl3(U_i), sl3(U_i), sl3(ts1), ALU.add)
                v.tensor_copy(cfin_sb[:, 0:4],
                              ap(U_r, 15, (PIT(U_r), 128), (16, 4)))
                v.tensor_copy(cfin_sb[:, 4:8],
                              ap(U_i, 15, (PIT(U_i), 128), (16, 4)))
                v.tensor_copy(ap(csh_r, 0, (PIT(csh_r), 128), (16, 4)),
                              cinit[:, 0:4])
                v.tensor_copy(ap(csh_i, 0, (PIT(csh_i), 128), (16, 4)),
                              cinit[:, 4:8])
                v.tensor_copy(ap(csh_r, 1, (PIT(csh_r), 128), (16, 4),
                                 (1, 15)),
                              ap(U_r, 0, (PIT(U_r), 128), (16, 4),
                                 (1, 15)))
                v.tensor_copy(ap(csh_i, 1, (PIT(csh_i), 128), (16, 4),
                                 (1, 15)),
                              ap(U_i, 0, (PIT(U_i), 128), (16, 4),
                                 (1, 15)))
                if seg == 0:
                    v.tensor_tensor(ci2_sb[:, :], cfin_sb[:, :],
                                    cmask[:, :], ALU.mult)
                    v.tensor_tensor(ci2_sb[:, :], ci2_sb[:, :],
                                    cinj_sb[:, :], ALU.add)

            for cb in range(4):
                for ri in range(2):
                    @ch.step("tensor")
                    def _(te, sem=None):
                        src = [csh_r, csh_i][ri]
                        te.transpose(tr_ps[0:16, 0:128],
                                     src[:, cb * 16:(cb + 1) * 16],
                                     ident[:, :])

                    @ch.step("vector")
                    def _(v, sem=None):
                        v.tensor_copy(
                            crow_sb[0:16, ri * MD + cb * 128:
                                    ri * MD + (cb + 1) * 128],
                            tr_ps[0:16, 0:128])

            @ch.step("sync")
            def _(sync, sem=None):
                sync.dma_start(
                    ap(dr["crow_scr"], 0, (512, 16), (8192, 2), (1, 512)),
                    ap(crow_sb, 0, (PIT(crow_sb), 16), (MD, 2), (1, MD)))

            for jl in range(16):
                j = j0 + jl

                @ch.step("gpsimd")
                def _(g, sem=None):
                    g.dma_start(crj_r[:, :],
                                bass.AP(dr["crow_scr"], jl * MD,
                                        [[0, 128], [1, MD]]))
                    g.dma_start(crj_i[:, :],
                                bass.AP(dr["crow_scr"], 8192 + jl * MD,
                                        [[0, 128], [1, MD]]))

                @ch.step("vector")
                def _(v, sem=None):
                    v.tensor_tensor(tbig1[:, :], crj_r[:, :], Pp1r[:, :],
                                    ALU.mult)
                    v.tensor_tensor(tbig2[:, :], crj_i[:, :], Pp1i[:, :],
                                    ALU.mult)
                    v.tensor_tensor(tbig1[:, :], tbig1[:, :], tbig2[:, :],
                                    ALU.subtract)
                    v.tensor_tensor(hsb[j][:, :], hsb[j][:, :],
                                    tbig1[:, :], ALU.add)

        pst.close()

        lw2 = sb("ml_lw2", [128, MD], F32)
        lb2 = sb("ml_lb2", [128, MD], F32)
        b2rep = sb("ml_b2", [128, MD], F32)
        b1sb = sb("ml_b1", [128, 16], F32)
        lny = sb("ml_lny", [128, MD], BF16)
        lnyT = [sb(f"ml_lnyT{cb}", [128, TH], BF16) for cb in range(4)]
        w1sb = [sb(f"ml_w1{cb}", [128, MF], BF16) for cb in range(4)]
        w2sb = [sb(f"ml_w2{f}", [128, MD], BF16) for f in range(16)]
        y2sb = [sb(f"ml_y2{f}", [128, 512], BF16) for f in range(16)]
        ffn_ps = [lst.enter_context(nc.psum_tensor(_u("ml_fps"), [128, 512],
                                                   F32)) for _i in range(2)]
        ffh_ps = [lst.enter_context(nc.psum_tensor(_u("ml_fhps"), [128, MD],
                                                   F32)) for _i in range(2)]
        tmp = (scr, sums, sumsq, nmu, ve, rstd, scr0, lnt)

        @ch.step("sync")
        def _(sync, sem=None):
            for dst_, src_ in [(lw2, dr[f"ln2w{dd}"]), (lb2, dr[f"ln2b{dd}"]),
                               (b2rep, dr[f"b2_{dd}"])]:
                sync.dma_start(dst_[:, :],
                               bass.AP(src_, 0, [[0, 128], [1, MD]]))
            sync.dma_start(b1sb[:, :], dr[f"b1c{dd}"][:, :])

        @ch.step("gpsimd")
        def _(g, sem=None):
            for cb in range(4):
                g.dma_start(w1sb[cb][:, :],
                            dr[f"W1T{dd}"][128 * cb:128 * (cb + 1), :])
            for f in range(16):
                g.dma_start(w2sb[f][:, :],
                            dr[f"W2T{dd}"][128 * f:128 * (f + 1), :])

        for j in range(NT):
            emit_wln_chunk(ch, hsb[j], lw2, lb2, lny, tmp)

            @ch.step("sync")
            def _(sync, sem=None):
                sync.dma_start(dr["lny_scr"][128 * j:128 * (j + 1), :],
                               lny[:, :])

        for th2 in range(2):
            t0 = th2 * TH

            @ch.step("sync")
            def _(sync, sem=None):
                for cb in range(4):
                    sync.dma_start(lnyT[cb][:, :],
                                   dr["lny_scr"][t0:t0 + TH,
                                                 128 * cb:128 * (cb + 1)],
                                   transpose=True)

            for tb in range(4):
                c0 = tb * 512
                for f in range(16):
                    @ch.step("tensor")
                    def _(te, sem=None):
                        for cb in range(4):
                            mi = te.matmul_q(
                                ffn_ps[f % 2][:, :],
                                w1sb[cb][:, 128 * f:128 * (f + 1)],
                                lnyT[cb][:, c0:c0 + 512],
                                start=(cb == 0), stop=(cb == 3))
                        te.mark(mi)

                    @ch.step("scalar")
                    def _(s, sem=None):
                        s.activation(y2sb[f][:, :], ffn_ps[f % 2][:, :],
                                     AF.Silu, bias=b1sb[:, f:f + 1])

                for tt in range(4):
                    j = th2 * 16 + tb * 4 + tt

                    @ch.step("tensor")
                    def _(te, sem=None):
                        for f in range(16):
                            mi = te.matmul_q(
                                ffh_ps[tt % 2][:, :],
                                y2sb[f][:, 128 * tt:128 * (tt + 1)],
                                w2sb[f][:, :],
                                start=(f == 0), stop=(f == 15))
                        te.mark(mi)

                    @ch.step("vector")
                    def _(v, sem=None):
                        v.tensor_tensor(scr[:, :], ffh_ps[tt % 2][:, :],
                                        b2rep[:, :], ALU.add)
                        v.tensor_tensor(hsb[j][:, :], hsb[j][:, :], scr[:, :],
                                        ALU.add)


def emit_combine(nc, ch, st, xsrc, s_scr, gT_scr, xdst, ntiles,
                 xrow0, srow0, gcol0):
    sb = lambda n, sh, dt: st.enter_context(nc.sbuf_tensor(_u(n), sh, dt))
    xt = sb("cb_x", [128, D], F32)
    st_ = sb("cb_s", [128, D], BF16)
    gt = sb("cb_g", [128, D], BF16)
    pt = sb("cb_p", [128, D], F32)
    for j in range(ntiles):
        @ch.step("sync")
        def _(sync, sem=None):
            sync.dma_start(xt[:, :],
                           xsrc[xrow0 + 128 * j:xrow0 + 128 * (j + 1), :])
            sync.dma_start(st_[:, :],
                           s_scr[srow0 + 128 * j:srow0 + 128 * (j + 1), :])
            sync.dma_start(gt[:, :],
                           gT_scr[:, gcol0 + 128 * j:gcol0 + 128 * (j + 1)],
                           transpose=True)

        @ch.step("vector")
        def _(v, sem=None):
            v.tensor_tensor(pt[:, :], st_[:, :], gt[:, :], ALU.mult)
            v.tensor_tensor(xt[:, :], xt[:, :], pt[:, :], ALU.add)

        @ch.step("sync")
        def _(sync, sem=None):
            sync.dma_start(xdst[128 * j:128 * (j + 1), :], xt[:, :])


def emit_qkv(nc, ch, st, dr):
    sb = lambda n, sh, dt: st.enter_context(nc.sbuf_tensor(_u(n), sh, dt))
    NV = 32
    xsb = [sb(f"qk_x{v}", [128, TH], BF16) for v in range(NV)]
    wc = sb("qk_wc", [128, D], BF16)
    qout = sb("qk_qout", [128, TH], BF16)
    q_ps = st.enter_context(nc.psum_tensor(_u("qk_ps"), [128, TH], F32))
    for th in range(2):
        t0 = th * TH

        @ch.step("sync")
        def _(sync, sem=None):
            for v in range(NV):
                sync.dma_start(xsb[v][:, :],
                               dr["xn1_scr"][t0:t0 + TH,
                                             128 * v:128 * (v + 1)],
                               transpose=True)

        ocs = [32, 33] if th == 0 else [32, 33] + list(range(32))
        for oc in ocs:
            @ch.step("gpsimd")
            def _(gp, sem=None):
                if oc < 32:
                    gp.dma_start(ap(wc, 0, (D, 128), (128, 32), (1, 128)),
                                 ap(dr["Wq"], oc * 128, (D, 128),
                                    (128 * D, 32), (1, 128)))
                else:
                    o2 = oc - 32
                    gp.dma_start(ap(wc, 0, (D, 128), (128, 32), (1, 128)),
                                 ap(dr["Wkv"], o2 * 128, (256, 128),
                                    (128 * 256, 32), (1, 128)))

            @ch.step("tensor")
            def _(te, sem=None):
                for tq in range(4):
                    for g in range(NV):
                        mi = te.matmul_q(q_ps[:, 512 * tq:512 * (tq + 1)],
                                         wc[:, 128 * g:128 * (g + 1)],
                                         xsb[g][:, 512 * tq:512 * (tq + 1)],
                                         start=(g == 0), stop=(g == NV - 1),
                                         skip_group_check=True)
                te.mark(mi)

            @ch.step("vector")
            def _(v, sem=None):
                v.tensor_copy(qout[:, :], q_ps[:, :])

            @ch.step("sync")
            def _(sync, sem=None):
                if oc < 32:
                    sync.dma_start(dr["qt_scr"][128 * oc:128 * (oc + 1), :],
                                   qout[:, :])
                else:
                    o2 = oc - 32
                    sync.dma_start(dr["kvt_scr"][128 * o2:128 * (o2 + 1),
                                                 t0:t0 + TH], qout[:, :])


def emit_attn(nc, ch, st, dr, slopes_np):
    sb = lambda n, sh, dt: st.enter_context(nc.sbuf_tensor(_u(n), sh, dt))
    heads = 32
    qsb = [sb(f"at_q{h}", [128, TH], BF16) for h in range(heads)]
    ksb = sb("at_k", [128, T], BF16)
    vtk = [sb(f"at_v{kt}", [128, 128], BF16) for kt in range(NT)]
    slrep = sb("at_sl", [128, heads], F32)
    pmrep = sb("at_pm", [128, NT], F32)
    iota = sb("at_iota", [128, 1], F32)
    onescol = sb("at_ones", [128, 1], BF16)
    bias_v = sb("at_bias", [128, 1], F32)
    probs = sb("at_probs", [128, 512], BF16)
    msc = sb("at_msc", [128, 512], F32)
    maskfull = [sb(f"at_mf{i}", [128, 512], F32) for i in range(4)]
    onum_bf = sb("at_onum", [128, 512], BF16)
    den_sb = sb("at_den", [1, 512], F32)
    sc_ps = [st.enter_context(nc.psum_tensor(_u("at_scps"), [128, 512], F32))
             for _i in range(2)]
    av_ps = st.enter_context(nc.psum_tensor(_u("at_avps"), [128, 512], F32))
    den_ps = st.enter_context(nc.psum_tensor(_u("at_dps"), [1, 512], F32))

    iota_d = nc.inline_tensor(np.arange(128, dtype=np.float32)[:, None],
                              "at_iota_d")
    ones_d = nc.inline_tensor(np.asarray(np.ones((128, 1), dtype=BF)),
                              "at_ones_d")
    mf_np = []
    for d0 in (0, 128, 256, 384):
        p = np.arange(128)[:, None]
        c = np.arange(512)[None, :]
        keep = (c >= d0 + 128) | ((c >= d0) & (p <= c - d0))
        mf_np.append(np.where(keep, 0.0, NEG).astype(np.float32))
    mf_d = [nc.inline_tensor(m, f"at_mf_d{i}") for i, m in enumerate(mf_np)]

    @ch.step("sync")
    def _(sync, sem=None):
        for h in range(heads):
            sync.dma_start(qsb[h][:, :],
                           dr["qt_scr"][128 * h:128 * (h + 1), :])
        sync.dma_start(ksb[:, :], dr["kvt_scr"][0:128, :])
        for kt in range(NT):
            sync.dma_start(vtk[kt][:, :],
                           dr["kvt_scr"][128:256, 128 * kt:128 * (kt + 1)],
                           transpose=True)
        sync.dma_start(slrep[:, :],
                       bass.AP(dr["slopes"], 0, [[0, 128], [1, heads]]))
        sync.dma_start(pmrep[:, :], dr["pmask"][:, :])
        sync.dma_start(iota[:, :], iota_d[:, :])
        sync.dma_start(onescol[:, :], ones_d[:, :])
        for i in range(4):
            sync.dma_start(maskfull[i][:, :], mf_d[i][:, :])

    for h in range(heads):
        sl = float(slopes_np[h])
        ebw = 512 if sl * 511.0 <= WTHRESH else 64
        nex = 512 // ebw
        for qc in range(4):
            q0 = qc * 512
            qglob0 = TH + q0
            kts = []
            for kt in range(NT):
                kb = kt * 128
                if kb > qglob0 + 511:
                    continue
                if sl * (qglob0 - (kb + 127)) > WTHRESH:
                    continue
                kts.append(kt)
            for ki, kt in enumerate(kts):
                kb = kt * 128
                first = (ki == 0)
                diag = kb >= qglob0
                d0 = kb - qglob0 if diag else None
                scp = sc_ps[ki % 2]

                @ch.step("tensor")
                def _(te, sem=None):
                    te.matmul(scp[:, :], ksb[:, kb:kb + 128],
                              qsb[h][:, q0:q0 + 512],
                              start=True, stop=True, skip_group_check=True)

                if diag:
                    @ch.step("vector")
                    def _(v, sem=None):
                        v.tensor_tensor(msc[:, :], scp[:, :],
                                        maskfull[d0 // 128][:, :], ALU.add)

                for ce in range(nex):
                    c = ce * ebw

                    @ch.step("vector")
                    def _(v, sem=None):
                        v.tensor_scalar(bias_v[:, :], iota[:, :],
                                        float(kb - qglob0 - c),
                                        slrep[:, h:h + 1], ALU.add, ALU.mult)
                        v.tensor_tensor(bias_v[:, :], bias_v[:, :],
                                        pmrep[:, kt:kt + 1], ALU.add)

                    @ch.step("scalar")
                    def _(s, sem=None):
                        src = msc if diag else scp
                        s.activation(probs[:, c:c + ebw], src[:, c:c + ebw],
                                     AF.Exp, bias=bias_v[:, 0:1],
                                     scale=ATT_SCALE)

                @ch.step("tensor")
                def _(te, sem=None):
                    te.matmul_q(av_ps[:, :], vtk[kt][:, :], probs[:, :],
                                start=first, stop=False,
                                skip_group_check=True)
                    te.matmul(den_ps[0:1, :], onescol[:, :], probs[:, :],
                              start=first, stop=False, skip_group_check=True)

            @ch.step("vector")
            def _(v, sem=None):
                v.tensor_copy(den_sb[0:1, :], den_ps[0:1, :])
                v.tensor_copy(onum_bf[:, :], av_ps[:, :])

            @ch.step("sync")
            def _(sync, sem=None):
                sync.dma_start(dr["onum_scr"][128 * h:128 * (h + 1),
                                              q0:q0 + 512], onum_bf[:, :])
                sync.dma_start(dr["oden_scr"][h:h + 1, q0:q0 + 512],
                               den_sb[0:1, :])


def emit_wo(nc, ch, st, dr):
    sb = lambda n, sh, dt: st.enter_context(nc.sbuf_tensor(_u(n), sh, dt))
    atsb = [sb(f"wo_a{h}", [128, TH], BF16) for h in range(32)]
    drow = sb("wo_drow", [1, TH], F32)
    rrow = sb("wo_rrow", [1, TH], F32)
    onesrow = sb("wo_ones", [1, 128], F32)
    drep = sb("wo_drep", [128, TH], F32)
    numf = sb("wo_numf", [128, TH], F32)
    wosb = [sb(f"wo_w{cb}", [128, 512], BF16) for cb in range(32)]
    sout = sb("wo_sout", [128, 512], BF16)
    dr_ps = st.enter_context(nc.psum_tensor(_u("wo_dps"), [128, TH], F32))
    s_ps = [st.enter_context(nc.psum_tensor(_u("wo_sps"), [128, 512], F32))
            for _i in range(2)]
    onesr_d = nc.inline_tensor(np.ones((1, 128), dtype=np.float32), "wo_or_d")

    @ch.step("sync")
    def _(sync, sem=None):
        sync.dma_start(onesrow[:, :], onesr_d[:, :])

    for h in range(32):
        @ch.step("gpsimd")
        def _(gp, sem=None):
            gp.dma_start(numf[:, :],
                         dr["onum_scr"][128 * h:128 * (h + 1), :])

        @ch.step("sync")
        def _(sync, sem=None):
            sync.dma_start(drow[0:1, :], dr["oden_scr"][h:h + 1, :])

        @ch.step("vector")
        def _(v, sem=None):
            v.reciprocal(rrow[0:1, :], drow[0:1, :])

        @ch.step("tensor")
        def _(te, sem=None):
            for c in range(0, TH, 512):
                mi = te.matmul_q(dr_ps[:, c:c + 512], onesrow[:, :],
                                 rrow[0:1, c:c + 512], start=True, stop=True,
                                 skip_group_check=True)
            te.mark(mi)

        @ch.step("vector")
        def _(v, sem=None):
            v.tensor_copy(drep[:, :], dr_ps[:, :])
            v.tensor_tensor(atsb[h][:, :], numf[:, :], drep[:, :], ALU.mult)

    for oc in range(8):
        @ch.step("gpsimd")
        def _(gp, sem=None):
            for cb in range(32):
                gp.dma_start(wosb[cb][:, :],
                             dr["Wo"][128 * cb:128 * (cb + 1),
                                      512 * oc:512 * (oc + 1)])

        for t in range(NTH):
            @ch.step("tensor")
            def _(te, sem=None):
                for cb in range(32):
                    mi = te.matmul_q(s_ps[t % 2][:, :],
                                     atsb[cb][:, 128 * t:128 * (t + 1)],
                                     wosb[cb][:, :],
                                     start=(cb == 0), stop=(cb == 31))
                te.mark(mi)

            @ch.step("vector")
            def _(v, sem=None):
                v.tensor_copy(sout[:, :], s_ps[t % 2][:, :])

            @ch.step("sync")
            def _(sync, sem=None):
                sync.dma_start(dr["s1_scr"][128 * t:128 * (t + 1),
                                            512 * oc:512 * (oc + 1)],
                               sout[:, :])


def emit_ffn1(nc, ch, st, dr):
    sb = lambda n, sh, dt: st.enter_context(nc.sbuf_tensor(_u(n), sh, dt))
    NV = 32
    xsb = [sb(f"f1_x{v}", [128, TH], BF16) for v in range(NV)]
    wc = sb("f1_wc", [128, D], BF16)
    atsb = sb("f1_at", [128, 48 * 32], BF16)
    bssb = sb("f1_bs", [48, D], BF16)
    bsf = sb("f1_bsf", [128, 32], F32)
    lsirep = sb("f1_lsi", [48, 4], I32)
    lsif = sb("f1_lsif", [48, 4], F32)
    brep = sb("f1_bv", [48, 1], F32)
    arsb = sb("f1_ar", [48, 1], F32)
    cnt = sb("f1_cnt", [48, 1], F32)
    seg = sb("f1_seg", [48, 1], F32)
    use = sb("f1_use", [48, 1], F32)
    aidx = sb("f1_aidx", [48, 1], F32)
    msk = sb("f1_msk", [48, 1], F32)
    rT = sb("f1_rT", [48, TH], BF16)
    gout = sb("f1_gout", [128, TH], BF16)
    r_ps = st.enter_context(nc.psum_tensor(_u("f1_rps"), [48, TH], F32))
    g_ps = st.enter_context(nc.psum_tensor(_u("f1_gps"), [128, TH], F32))

    @ch.step("sync")
    def _(sync, sem=None):
        for v in range(NV):
            sync.dma_start(xsb[v][:, :],
                           dr["xn1_scr"][TH:T, 128 * v:128 * (v + 1)],
                           transpose=True)
        sync.dma_start(lsirep[:, :], bass.AP(dr["lsi"], 0, [[0, 48], [1, 4]]))
        sync.dma_start(brep[:, :], bass.AP(dr["bval"], 0, [[0, 48], [1, 1]]))
        sync.dma_start(arsb[:, :], dr["arow"][:, :])
        sync.dma_start(bsf[:, :], dr["biasc1"][:, :])

    @ch.step("gpsimd")
    def _(gp, sem=None):
        for v in range(NV):
            gp.dma_start(ap(atsb, v * 48, (48 * 32, 128), (1, 48)),
                         dr["AT"][128 * v:128 * (v + 1), :])
        gp.dma_start(bssb[0:48, :], dr["BS"][:, :])

    @ch.step("vector")
    def _(v, sem=None):
        v.tensor_copy(lsif[0:48, :], lsirep[0:48, :])
        v.tensor_scalar(lsif[0:48, :], lsif[0:48, :], brep[0:48, 0:1], None,
                        ALU.is_le)
        v.tensor_reduce(cnt[0:48, :], lsif[0:48, :], mybir.AxisListType.X,
                        ALU.add)
        v.tensor_scalar(seg[0:48, :], cnt[0:48, :], -1.0, None, ALU.add)
        v.tensor_scalar(use[0:48, :], seg[0:48, :], 0.0, None, ALU.is_gt)
        v.tensor_scalar(aidx[0:48, :], seg[0:48, :], -1.0, 0.0, ALU.add,
                        ALU.max)
        v.tensor_scalar(aidx[0:48, :], aidx[0:48, :], 2.0, None, ALU.min)
        v.tensor_tensor(msk[0:48, :], aidx[0:48, :], arsb[0:48, :],
                        ALU.is_equal)
        v.tensor_tensor(msk[0:48, :], msk[0:48, :], use[0:48, :], ALU.mult)

    @ch.step("tensor")
    def _(te, sem=None):
        for tq in range(4):
            for v in range(NV):
                mi = te.matmul_q(r_ps[0:48, 512 * tq:512 * (tq + 1)],
                                 ap(atsb, v * 48, (48 * 32, 128), (1, 48)),
                                 xsb[v][:, 512 * tq:512 * (tq + 1)],
                                 start=(v == 0), stop=(v == NV - 1),
                                 skip_group_check=True)
        te.mark(mi)

    @ch.step("vector")
    def _(v, sem=None):
        v.tensor_scalar(rT[0:48, :], r_ps[0:48, :], msk[0:48, 0:1], None,
                        ALU.mult)

    for oc in range(32):
        @ch.step("gpsimd")
        def _(gp, sem=None):
            gp.dma_start(ap(wc, 0, (D, 128), (128, 32), (1, 128)),
                         ap(dr["WT"], oc * 128, (D, 128), (128 * D, 32),
                            (1, 128)))

        @ch.step("tensor")
        def _(te, sem=None):
            for tq in range(4):
                for g in range(NV):
                    te.matmul_q(g_ps[:, 512 * tq:512 * (tq + 1)],
                                wc[:, 128 * g:128 * (g + 1)],
                                xsb[g][:, 512 * tq:512 * (tq + 1)],
                                start=(g == 0), stop=False,
                                skip_group_check=True)
                mi = te.matmul_q(g_ps[:, 512 * tq:512 * (tq + 1)],
                                 bssb[0:48, 128 * oc:128 * (oc + 1)],
                                 rT[0:48, 512 * tq:512 * (tq + 1)],
                                 start=False, stop=True,
                                 skip_group_check=True)
            te.mark(mi)

        @ch.step("scalar")
        def _(s, sem=None):
            s.activation(gout[:, :], g_ps[:, :], AF.Silu,
                         bias=bsf[:, oc:oc + 1])

        @ch.step("sync")
        def _(sync, sem=None):
            sync.dma_start(dr["g1T_scr"][128 * oc:128 * (oc + 1), :],
                           gout[:, :])


def build_mega(slopes_np):
    nc = bass.Bass("TRN2", target_bir_lowering=False)
    dt_ = lambda n, sh, dty, kind="Internal": nc.dram_tensor(n, sh, dty,
                                                             kind=kind)
    dr = {}
    EI = "ExternalInput"
    dr["xb"] = dt_("xb", [T, D], F32, EI)
    dr["WinT"] = dt_("WinT", [D, MD], F32, EI)
    dr["binm"] = dt_("binm", [1, MD], F32, EI)
    dr["qw"] = dt_("qw", [512, D], I32, EI)
    dr["qz"] = dt_("qz", [32, 512], I32, EI)
    dr["scales"] = dt_("scales", [32, D], F16, EI)
    dr["biasc0"] = dt_("biasc0", [128, 32], F16, EI)
    dr["shamt"] = dt_("shamt", [128, 1], I32, EI)
    dr["cmask"] = dt_("cmask", [128, 8], F32, EI)
    for dd in range(4):
        for nm, sh in [(f"lnw{dd}", [1, MD]), (f"lnb{dd}", [1, MD]),
                       (f"scl{dd}", [1, MD]), (f"phzs{dd}", [128, 8]),
                       (f"phzrow{dd}", [2, MD]), (f"cinit{dd}", [128, 8]),
                       (f"cinj{dd}", [128, 8]), (f"ln2w{dd}", [1, MD]),
                       (f"ln2b{dd}", [1, MD]), (f"W1T{dd}", [MD, MF]),
                       (f"b1c{dd}", [128, 16]), (f"W2T{dd}", [MF, MD]),
                       (f"b2_{dd}", [1, MD])]:
            dr[nm] = dt_(nm, sh, F32, EI)
    dr["lnlw"] = dt_("lnlw", [1, MD], F32, EI)
    dr["lnlb"] = dt_("lnlb", [1, MD], F32, EI)
    dr["WoutT"] = dt_("WoutT", [MD, D], F32, EI)
    dr["bout"] = dt_("bout", [1, D], F32, EI)
    dr["Wq"] = dt_("Wq", [D, D], F32, EI)
    dr["Wkv"] = dt_("Wkv", [D, 256], F32, EI)
    dr["slopes"] = dt_("slopes", [1, 32], F32, EI)
    dr["pmask"] = dt_("pmask", [128, NT], F32, EI)
    dr["Wo"] = dt_("Wo", [D, D], F32, EI)
    dr["WT"] = dt_("WT", [D, D], F32, EI)
    dr["biasc1"] = dt_("biasc1", [128, 32], F32, EI)
    dr["AT"] = dt_("AT", [D, 48], F32, EI)
    dr["BS"] = dt_("BS", [48, D], F32, EI)
    dr["lsi"] = dt_("lsi", [1, 4], I32, EI)
    dr["bval"] = dt_("bval", [1, 1], F32, EI)
    dr["arow"] = dt_("arow", [48, 1], F32, EI)
    dr["out"] = dt_("out", [TH, D], F32, "ExternalOutput")
    dr["xn_scr"] = dt_("xn_scr", [T, D], BF16)
    dr["h_scr"] = dt_("h_scr", [T, MD], F32)
    dr["g0T_scr"] = dt_("g0T_scr", [D, T], BF16)
    dr["s0_scr"] = dt_("s0_scr", [T, D], BF16)
    dr["x1_scr"] = dt_("x1_scr", [T, D], F32)
    dr["xn1_scr"] = dt_("xn1_scr", [T, D], BF16)
    dr["qt_scr"] = dt_("qt_scr", [D, TH], BF16)
    dr["kvt_scr"] = dt_("kvt_scr", [256, T], BF16)
    dr["onum_scr"] = dt_("onum_scr", [D, TH], BF16)
    dr["oden_scr"] = dt_("oden_scr", [32, TH], F32)
    dr["s1_scr"] = dt_("s1_scr", [TH, D], BF16)
    dr["g1T_scr"] = dt_("g1T_scr", [D, TH], BF16)
    dr["lny_scr"] = dt_("lny_scr", [T, MD], BF16)
    dr["w4_scr"] = dt_("w4_scr", [D, D], BF16)
    dr["crow_scr"] = dt_("crow_scr", [2 * 16 * MD], F32)
    dr["prow_scr"] = dt_("prow_scr", [2, MD], F32)

    ch = Chain(nc)
    if True:
        with ExitStack() as st:
            emit_ln(nc, ch, st, dr["xb"], dr["xn_scr"], NT)
        with ExitStack() as st:
            emit_h0_gate0(nc, ch, st, dr)
        with ExitStack() as st:
            emit_mini_stack(nc, ch, st, dr)
        with ExitStack() as st:
            emit_combine(nc, ch, st, dr["xb"], dr["s0_scr"], dr["g0T_scr"],
                         dr["x1_scr"], NT, 0, 0, 0)
        with ExitStack() as st:
            emit_ln(nc, ch, st, dr["x1_scr"], dr["xn1_scr"], NT)
        with ExitStack() as st:
            emit_qkv(nc, ch, st, dr)
        with ExitStack() as st:
            emit_attn(nc, ch, st, dr, slopes_np)
        with ExitStack() as st:
            emit_wo(nc, ch, st, dr)
        with ExitStack() as st:
            emit_ffn1(nc, ch, st, dr)
        with ExitStack() as st:
            emit_combine(nc, ch, st, dr["x1_scr"], dr["s1_scr"],
                         dr["g1T_scr"], dr["out"], NTH, TH, 0, 0)
    ch.finish()
    return nc


_PROG = {}


def kernel(**inp):
    C = np.ascontiguousarray
    f32 = np.float32
    sl_np = alibi_slopes(32)
    if "mega" not in _PROG:
        _PROG["mega"] = build_mega(tuple(float(v) for v in sl_np))
    nc = _PROG["mega"]

    x = np.asarray(inp["L0_attn_x"], f32)
    scan_layout = lambda a: C(np.asarray(a, f32).reshape(4, 128).T)
    zeros8 = np.zeros((128, 8), f32)
    ones8 = np.ones((128, 8), f32)

    shamt = ((np.arange(128) % 8) * 4).astype(np.int32)[:, None]
    biasc0 = C(np.asarray(inp["L0_ffn_bias"]).reshape(32, 128).T)
    WinT = C(np.asarray(inp["L0_attn_Win"], f32).T)
    WoutT = C(np.asarray(inp["L0_attn_Wout"], f32).T)
    WT = C(np.asarray(inp["L1_ffn_weight"], f32).T)
    biasc1 = C(np.asarray(inp["L1_ffn_bias"], f32).reshape(32, 128).T)
    AT = C(np.asarray(inp["L1_ffn_lora_A"],
                      f32).transpose(2, 0, 1).reshape(D, 48))
    BS = C(np.asarray(inp["L1_ffn_lora_B"],
                      f32).transpose(0, 2, 1).reshape(48, D))
    arow = np.repeat(np.arange(3, dtype=f32), 16)[:, None]
    lsi = np.asarray(inp["L1_ffn_lora_start_indices"], np.int32)[None]

    in_maps = []
    for c in range(NCORE):
        b, hf = c // 2, c % 2
        if hf == 0:
            xb = np.concatenate([np.zeros((TH, D), f32),
                                 np.asarray(x[b, 0:TH], f32)], 0)
        else:
            xb = np.asarray(x[b], f32)
        m = {
            "xb": C(xb), "WinT": WinT,
            "binm": np.asarray(inp["L0_attn_bin"], f32)[None],
            "qw": np.asarray(inp["L0_ffn_qweight"]),
            "qz": np.asarray(inp["L0_ffn_qzeros"]),
            "scales": np.asarray(inp["L0_ffn_scales"]),
            "biasc0": biasc0, "shamt": shamt,
            "cmask": ones8 if hf == 1 else zeros8,
            "lnlw": np.asarray(inp["L0_attn_lnl_w"], f32)[None],
            "lnlb": np.asarray(inp["L0_attn_lnl_b"], f32)[None],
            "WoutT": WoutT,
            "bout": np.asarray(inp["L0_attn_bout"], f32)[None],
            "Wq": np.asarray(inp["L1_attn_Wq"], f32),
            "Wkv": np.asarray(inp["L1_attn_Wkv"], f32),
            "slopes": sl_np[None],
            "pmask": np.concatenate(
                [np.full((128, 16), NEG if hf == 0 else 0.0, f32),
                 np.zeros((128, 16), f32)], 1),
            "Wo": np.asarray(inp["L1_attn_Wo"], f32),
            "WT": WT, "biasc1": biasc1, "AT": AT, "BS": BS,
            "lsi": lsi, "bval": np.full((1, 1), float(b), f32),
            "arow": arow,
        }
        for dd in range(4):
            lc8 = np.concatenate([scan_layout(inp["L0_attn_lc_re"][dd]),
                                  scan_layout(inp["L0_attn_lc_im"][dd])], 1)
            m[f"lnw{dd}"] = np.asarray(inp["L0_attn_ln1_w"][dd], f32)[None]
            m[f"lnb{dd}"] = np.asarray(inp["L0_attn_ln1_b"][dd], f32)[None]
            m[f"scl{dd}"] = np.asarray(inp["L0_attn_sc_lin"][dd], f32)[None]
            m[f"phzs{dd}"] = np.concatenate(
                [scan_layout(inp["L0_attn_phz_re"][dd]),
                 scan_layout(inp["L0_attn_phz_im"][dd])], 1)
            m[f"phzrow{dd}"] = C(np.stack(
                [np.asarray(inp["L0_attn_phz_re"][dd], f32),
                 np.asarray(inp["L0_attn_phz_im"][dd], f32)]))
            m[f"cinit{dd}"] = lc8 if hf == 1 else zeros8.copy()
            m[f"cinj{dd}"] = lc8 if hf == 0 else zeros8.copy()
            m[f"ln2w{dd}"] = np.asarray(inp["L0_attn_ln2_w"][dd], f32)[None]
            m[f"ln2b{dd}"] = np.asarray(inp["L0_attn_ln2_b"][dd], f32)[None]
            m[f"W1T{dd}"] = C(np.asarray(inp["L0_attn_W1"][dd], f32).T)
            m[f"b1c{dd}"] = C(np.asarray(inp["L0_attn_b1"][dd],
                                         f32).reshape(16, 128).T)
            m[f"W2T{dd}"] = C(np.asarray(inp["L0_attn_W2"][dd], f32).T)
            m[f"b2_{dd}"] = np.asarray(inp["L0_attn_b2"][dd], f32)[None]
        in_maps.append(m)

    res = run_bass_kernel_spmd(nc, in_maps, core_ids=list(range(NCORE)))
    out = np.empty((4, 4096, D), f32)
    for c in range(NCORE):
        b, hf = c // 2, c % 2
        out[b, hf * TH:(hf + 1) * TH] = np.asarray(res.results[c]["out"], f32)
    return out


# version 03
# speedup vs baseline: 1.0269x; 1.0269x over previous
import math
from contextlib import ExitStack
import numpy as np
import ml_dtypes
import concourse.bass as bass
import concourse.mybir as mybir
from concourse.bass_utils import run_bass_kernel_spmd

F32 = mybir.dt.float32
BF16 = mybir.dt.bfloat16
F16 = mybir.dt.float16
I32 = mybir.dt.int32
AF = mybir.ActivationFunctionType
ALU = mybir.AluOpType
BF = ml_dtypes.bfloat16

NCORE = 8
T = 4096
TH = 2048
D = 4096
MD = 512
MF = 2048
NT = T // 128
NTH = TH // 128
EPS = 1e-5
ATT_SCALE = float(128 ** -0.5)
WTHRESH = 60.0
NEG = -1e30

_UN = [0]


def _u(n):
    _UN[0] += 1
    return f"{n}_{_UN[0]}"


def alibi_slopes(n_heads=32):
    cp2 = 2 ** math.floor(math.log2(n_heads))
    base = 2.0 ** (-(2.0 ** (-(math.log2(cp2) - 3))))
    slopes = [base ** k for k in range(1, cp2 + 1)]
    if cp2 != n_heads:
        eb = 2.0 ** (-(2.0 ** (-(math.log2(2 * cp2) - 3))))
        ne = min(cp2, n_heads - cp2)
        slopes += [eb ** k for k in range(1, 2 * ne + 1, 2)]
    return np.asarray(slopes, dtype=np.float32)


def ap(tsr, off, *dims):
    return bass.AP(tsr, off, [list(d) for d in dims])


def smax(*snaps):
    snaps = [s for s in snaps if s is not None]
    if not snaps:
        return None
    return tuple(max(v) for v in zip(*snaps))


_NOSYNC = {"wait_ge", "nop", "engine_nop", "register", "reg_mov", "print"}


class _Auto:

    def __init__(self, ch, eng):
        self._ch = ch
        self._eng = eng

    def matmul_q(self, *a, **k):
        return self._eng.matmul(*a, **k)

    def mark(self, inst):
        inst.then_inc(self._ch.csem, 1)
        self._ch.ct += 1
        return inst

    def __getattr__(self, name):
        target = getattr(self._eng, name)
        if name.startswith("_") or not callable(target) or name in _NOSYNC:
            return target

        def wrap(*a, **k):
            inst = target(*a, **k)
            if inst is None or not hasattr(inst, "then_inc"):
                return inst
            if name in ("dma_start", "indirect_dma_start"):
                from concourse.bass import BassGpSimd
                if isinstance(self._eng, BassGpSimd):
                    inst.then_inc(self._ch.gsem, 16)
                    self._ch.gt += 16
                else:
                    inst.then_inc(self._ch.dsem, 16)
                    self._ch.dt += 16
            else:
                inst.then_inc(self._ch.csem, 1)
                self._ch.ct += 1
                self._eng.wait_ge(self._ch.csem, self._ch.ct)
            return inst
        return wrap


class Chain:

    def __init__(self, nc):
        self.nc = nc
        self.block = nc.Block()
        self.block.__enter__()
        self.csem = nc.alloc_semaphore("ch_c")
        self.dsem = nc.alloc_semaphore("ch_d")
        self.gsem = nc.alloc_semaphore("ch_g")
        self.ct = 0
        self.dt = 0
        self.gt = 0
        self.last = {}

    def snap(self):
        return (self.ct, self.dt, self.gt)

    def step(self, engine, at=None):
        def deco(fn):
            if at is None:
                ct0, dt0, gt0 = self.ct, self.dt, self.gt
            else:
                ct0, dt0, gt0 = at

            def body(eng):
                if ct0 > 0 and self.last.get("c") != (engine, ct0):
                    eng.wait_ge(self.csem, ct0)
                if dt0 > 0 and self.last.get("d") != (engine, dt0):
                    eng.wait_ge(self.dsem, dt0)
                if gt0 > 0 and self.last.get("g") != (engine, gt0):
                    eng.wait_ge(self.gsem, gt0)
                fn(_Auto(self, eng), None)
            getattr(self.block, engine)(body)
            self.last["c"] = (engine, ct0)
            self.last["d"] = (engine, dt0)
            self.last["g"] = (engine, gt0)
            return fn
        return deco

    def finish(self):
        ct, dt, gt = self.ct, self.dt, self.gt

        def fin(g):
            g.wait_ge(self.csem, ct)
            g.wait_ge(self.dsem, dt)
            g.wait_ge(self.gsem, gt)
        self.block.gpsimd(fin)
        self.block.__exit__(None, None, None)


def emit_ln(nc, ch, st, x_dram, xn_dram, ntiles):
    sb = lambda n, sh, dt: st.enter_context(nc.sbuf_tensor(_u(n), sh, dt))
    NB = 3
    xt = [sb("ln_xt", [128, D], F32) for _i in range(NB)]
    ot = [sb("ln_ot", [128, D], BF16) for _i in range(NB)]
    scr = [sb("ln_scr", [128, D], F32) for _i in range(2)]
    sums = [sb("ln_sums", [128, 1], F32) for _i in range(2)]
    sumsq = [sb("ln_sumsq", [128, 1], F32) for _i in range(2)]
    nmu = [sb("ln_nmu", [128, 1], F32) for _i in range(2)]
    ve = [sb("ln_ve", [128, 1], F32) for _i in range(2)]
    rstd = [sb("ln_rstd", [128, 1], F32) for _i in range(2)]
    scr0 = [sb("ln_scr0", [128, 1], F32) for _i in range(2)]
    ld = {}
    stats = {}
    sq = {}
    aff = {}
    stq = {}
    for j in range(ntiles):
        b = j % NB
        p = j % 2

        @ch.step("sync", at=smax(aff.get(j - NB), stq.get(j - NB)))
        def _(sync, sem=None):
            sync.dma_start(xt[b][:, :], x_dram[128 * j:128 * (j + 1), :])
        ld[j] = ch.snap()

        @ch.step("vector", at=smax(ld[j], sq.get(j - 2)))
        def _(v, sem=None):
            v.tensor_scalar(scr[p][:, :], xt[b][:, :], 1.0, 0.0, ALU.mult,
                            ALU.add, accum_out=sums[p][:, :])
            v.scalar_tensor_tensor(scr[p][:, :], xt[b][:, :], 1.0, xt[b][:, :],
                                   ALU.mult, ALU.mult,
                                   accum_out=sumsq[p][:, :])
            v.tensor_scalar(nmu[p][:, :], sums[p][:, :], -1.0 / D, None,
                            ALU.mult)
            v.tensor_scalar(ve[p][:, :], sumsq[p][:, :], 1.0 / D, EPS,
                            ALU.mult, ALU.add)
            v.tensor_tensor(scr0[p][:, :], nmu[p][:, :], nmu[p][:, :],
                            ALU.mult)
            v.tensor_tensor(ve[p][:, :], ve[p][:, :], scr0[p][:, :],
                            ALU.subtract)
            v.reciprocal(ve[p][:, :], ve[p][:, :])
        stats[j] = ch.snap()

        @ch.step("scalar", at=stats[j])
        def _(s, sem=None):
            s.activation(rstd[p][:, :], ve[p][:, :], AF.Sqrt)
        sq[j] = ch.snap()

        @ch.step("vector", at=smax(sq[j], stq.get(j - NB)))
        def _(v, sem=None):
            v.tensor_scalar(ot[b][:, :], xt[b][:, :], nmu[p][:, 0:1],
                            rstd[p][:, 0:1], ALU.add, ALU.mult)
        aff[j] = ch.snap()

        @ch.step("gpsimd", at=aff[j])
        def _(g, sem=None):
            g.dma_start(xn_dram[128 * j:128 * (j + 1), :], ot[b][:, :])
        stq[j] = ch.snap()


def emit_wln_chunk(ch, hsb_j, lwrep, lbrep, out_bf, tmp, width=MD):
    scr, sums, sumsq, nmu, ve, rstd, scr0, lnt = tmp

    @ch.step("vector")
    def _(v, sem=None):
        v.tensor_scalar(scr[:, :], hsb_j[:, :], 1.0, 0.0, ALU.mult, ALU.add,
                        accum_out=sums[:, :])
        v.scalar_tensor_tensor(scr[:, :], hsb_j[:, :], 1.0, hsb_j[:, :],
                               ALU.mult, ALU.mult, accum_out=sumsq[:, :])
        v.tensor_scalar(nmu[:, :], sums[:, :], -1.0 / width, None, ALU.mult)
        v.tensor_scalar(ve[:, :], sumsq[:, :], 1.0 / width, EPS,
                        ALU.mult, ALU.add)
        v.tensor_tensor(scr0[:, :], nmu[:, :], nmu[:, :], ALU.mult)
        v.tensor_tensor(ve[:, :], ve[:, :], scr0[:, :], ALU.subtract)
        v.reciprocal(ve[:, :], ve[:, :])

    @ch.step("scalar")
    def _(s, sem=None):
        s.activation(rstd[:, :], ve[:, :], AF.Sqrt)

    @ch.step("vector")
    def _(v, sem=None):
        v.tensor_scalar(lnt[:, :], hsb_j[:, :], nmu[:, 0:1], rstd[:, 0:1],
                        ALU.add, ALU.mult)
        v.tensor_tensor(lnt[:, :], lnt[:, :], lwrep[:, :], ALU.mult)
        v.tensor_tensor(out_bf[:, :], lnt[:, :], lbrep[:, :], ALU.add)


def emit_h0_gate0(nc, ch, st, dr):
    sb = lambda n, sh, dt: st.enter_context(nc.sbuf_tensor(_u(n), sh, dt))
    NV = D // 128
    zsn = sb("g0_zsn", [32, D], BF16)
    with ExitStack() as dst:
        dsb = lambda n, sh, dt: dst.enter_context(
            nc.sbuf_tensor(_u(n), sh, dt))
        ssb = dsb("g0_ssb", [32, D], BF16)
        qwrep = [dsb("g0_qwrep", [128, D], I32) for _i in range(2)]
        srep = [dsb("g0_srep", [128, D], BF16) for _i in range(2)]
        w4g = [dsb("g0_w4g", [128, D], BF16) for _i in range(2)]
        sh_sb = dsb("g0_sh", [128, 1], I32)
        ztmp = dsb("g0_ztmp", [32, 512], I32)
        qzsb = dsb("g0_qz", [32, 512], I32)

        @ch.step("sync")
        def _(sync, sem=None):
            sync.dma_start(qzsb[:, :], dr["qz"][:, :])
            sync.dma_start(sh_sb[:, :], dr["shamt"][:, :])

        @ch.step("gpsimd")
        def _(gp, sem=None):
            gp.dma_start(ssb[0:32, :], dr["scales"][:, :])

        @ch.step("vector")
        def _(v, sem=None):
            for nib in range(8):
                v.tensor_scalar(ztmp[0:32, :], qzsb[0:32, :], 4 * nib, 15,
                                ALU.logical_shift_right, ALU.bitwise_and)
                v.tensor_copy(ap(zsn, nib, (D, 32), (8, 512)), ztmp[0:32, :])
            v.tensor_tensor(zsn[0:32, :], zsn[0:32, :], ssb[0:32, :], ALU.mult)
            v.tensor_scalar(zsn[0:32, :], zsn[0:32, :], -1.0, None, ALU.mult)

        dld = {}
        dcp = {}
        dst_ = {}
        for g in range(32):
            b = g % 2

            @ch.step("sync", at=dcp.get(g - 2))
            def _(sync, sem=None):
                sync.dma_start(qwrep[b][:, :],
                               bass.AP(dr["qw"], 16 * g * D,
                                       [[D, 16], [0, 8], [1, D]]))

            @ch.step("gpsimd", at=dcp.get(g - 2))
            def _(gp, sem=None):
                gp.dma_start(srep[b][:, :],
                             bass.AP(dr["scales"], g * D, [[0, 128], [1, D]]))
            dld[g] = ch.snap()

            @ch.step("vector", at=smax(dld[g], dst_.get(g - 2)))
            def _(v, sem=None):
                v.tensor_scalar(qwrep[b][:, :], qwrep[b][:, :], sh_sb[:, 0:1],
                                15, ALU.logical_shift_right, ALU.bitwise_and)
                v.tensor_copy(w4g[b][:, :], qwrep[b][:, :])
                v.tensor_tensor(w4g[b][:, :], w4g[b][:, :], srep[b][:, :],
                                ALU.mult)
            dcp[g] = ch.snap()

            @ch.step("gpsimd", at=dcp[g])
            def _(gp, sem=None):
                gp.dma_start(dr["w4_scr"][128 * g:128 * (g + 1), :],
                             w4g[b][:, :])
            dst_[g] = ch.snap()

    xsb = [sb(f"hg_x{v}", [128, TH], BF16) for v in range(NV)]
    winsb = sb("hg_win", [128, NV * MD], BF16)
    brep = sb("hg_bin", [128, MD], F32)
    b0rep = sb("hg_b0", [128, 32], F32)
    b0h = sb("hg_b0h", [128, 32], F16)
    hout = [sb("hg_hout", [128, MD], F32) for _i in range(2)]
    w4c = [sb("hg_w4c", [128, D], BF16) for _i in range(2)]
    gout = [sb("hg_gout", [128, TH], BF16) for _i in range(2)]
    xs_sb = sb("hg_xs", [32, TH], BF16)
    OHsb = sb("hg_OH", [128, 32 * 32], BF16)
    OH32_np = np.zeros((128, 32 * 32), dtype=BF)
    for g in range(32):
        OH32_np[:, g * 32 + g] = 1.0
    OH32_d = nc.inline_tensor(np.asarray(OH32_np), "hg_OH_d")

    @ch.step("sync")
    def _(sync, sem=None):
        sync.dma_start(OHsb[:, :], OH32_d[:, :])
        sync.dma_start(brep[:, :], bass.AP(dr["binm"], 0, [[0, 128], [1, MD]]))
        sync.dma_start(b0h[:, :], dr["biasc0"][:, :])

    @ch.step("gpsimd")
    def _(gp, sem=None):
        for v in range(NV):
            gp.dma_start(winsb[:, v * MD:(v + 1) * MD],
                         dr["WinT"][128 * v:128 * (v + 1), :])

    @ch.step("vector")
    def _(v, sem=None):
        v.tensor_copy(b0rep[:, :], b0h[:, :])

    for th in range(2):
        t0 = th * TH

        @ch.step("sync")
        def _(sync, sem=None):
            for v in range(NV):
                sync.dma_start(xsb[v][:, :],
                               dr["xn_scr"][t0:t0 + TH,
                                            128 * v:128 * (v + 1)],
                               transpose=True)

        with ExitStack() as hps_st:
            h_ps = [hps_st.enter_context(
                nc.psum_tensor(_u("hg_hps"), [128, MD], F32))
                for _i in range(2)]
            xmm = {}
            xcp = {}
            for tq in range(4):
                @ch.step("tensor", at=(None if tq < 2 else xcp.get(tq - 2))
                         if tq else None)
                def _(te, sem=None):
                    for g in range(NV):
                        mi = te.matmul_q(h_ps[tq % 2][0:32, :],
                                         OHsb[:, 32 * g:32 * (g + 1)],
                                         xsb[g][:, 512 * tq:512 * (tq + 1)],
                                         start=(g == 0), stop=(g == NV - 1),
                                         skip_group_check=True)
                    te.mark(mi)
                xmm[tq] = ch.snap()

                @ch.step("vector", at=xmm[tq])
                def _(v, sem=None):
                    v.tensor_copy(xs_sb[0:32, 512 * tq:512 * (tq + 1)],
                                  h_ps[tq % 2][0:32, :])
                xcp[tq] = ch.snap()

            hmm = {}
            hcp = {}
            hst = {}
            for t in range(NTH):
                @ch.step("tensor", at=smax(hcp.get(t - 2),
                                           xcp[3] if t == 0 else None))
                def _(te, sem=None):
                    for v in range(NV):
                        mi = te.matmul_q(h_ps[t % 2][:, :],
                                         xsb[v][:, 128 * t:128 * (t + 1)],
                                         winsb[:, v * MD:(v + 1) * MD],
                                         start=(v == 0), stop=(v == NV - 1))
                    te.mark(mi)
                hmm[t] = ch.snap()

                @ch.step("vector", at=smax(hmm[t], hst.get(t - 2)))
                def _(v, sem=None):
                    v.tensor_tensor(hout[t % 2][:, :], h_ps[t % 2][:, :],
                                    brep[:, :], ALU.add)
                hcp[t] = ch.snap()

                @ch.step("gpsimd", at=hcp[t])
                def _(g, sem=None):
                    g.dma_start(
                        dr["h_scr"][t0 + 128 * t:t0 + 128 * (t + 1), :],
                        hout[t % 2][:, :])
                hst[t] = ch.snap()

        with ExitStack() as gps_st:
            g_ps = [gps_st.enter_context(
                nc.psum_tensor(_u("hg_gps"), [128, TH], F32))
                for _i in range(2)]
            gld = {}
            gmm = {}
            gsl = {}
            gst_ = {}
            for oc in range(32):
                b = oc % 2

                @ch.step("sync", at=gmm.get(oc - 2))
                def _(sync, sem=None):
                    sync.dma_start(
                        ap(w4c[b], 0, (D, 128), (128, 32), (1, 128)),
                        ap(dr["w4_scr"], oc * 128, (D, 128), (128 * D, 32),
                           (1, 128)))
                gld[oc] = ch.snap()

                @ch.step("tensor", at=smax(gld[oc], gsl.get(oc - 2)))
                def _(te, sem=None):
                    for tq in range(4):
                        for g in range(NV):
                            te.matmul_q(g_ps[b][:, 512 * tq:512 * (tq + 1)],
                                        w4c[b][:, 128 * g:128 * (g + 1)],
                                        xsb[g][:, 512 * tq:512 * (tq + 1)],
                                        start=(g == 0), stop=False,
                                        skip_group_check=True)
                        mi = te.matmul_q(g_ps[b][:, 512 * tq:512 * (tq + 1)],
                                         zsn[0:32, 128 * oc:128 * (oc + 1)],
                                         xs_sb[0:32,
                                               512 * tq:512 * (tq + 1)],
                                         start=False, stop=True,
                                         skip_group_check=True)
                    te.mark(mi)
                gmm[oc] = ch.snap()

                @ch.step("scalar", at=smax(gmm[oc], gst_.get(oc - 2)))
                def _(s, sem=None):
                    s.activation(gout[b][:, :], g_ps[b][:, :], AF.Silu,
                                 bias=b0rep[:, oc:oc + 1])
                gsl[oc] = ch.snap()

                @ch.step("gpsimd", at=gsl[oc])
                def _(g, sem=None):
                    g.dma_start(dr["g0T_scr"][128 * oc:128 * (oc + 1),
                                              t0:t0 + TH], gout[b][:, :])
                gst_[oc] = ch.snap()


def _dft_consts_np():
    i = np.arange(128)
    f = np.arange(128)
    ang = 2 * np.pi * np.outer(i, f) / 256.0
    WfrT = np.cos(ang).astype(BF)
    WfiT = (-np.sin(ang)).astype(BF)
    alt = ((-1.0) ** i).astype(BF)[:, None]
    w = np.full(128, 2.0)
    w[0] = 1.0
    angl = 2 * np.pi * np.outer(f, np.arange(128)) / 256.0
    VcT = (w[:, None] * np.cos(angl) / 256.0).astype(BF)
    VsT = (-w[:, None] * np.sin(angl) / 256.0).astype(BF)
    altl = ((((-1.0) ** np.arange(128)) / 256.0).astype(BF))[None, :]
    ident = np.eye(128, dtype=np.float32)
    OH = np.zeros((128, 256), dtype=BF)
    for j in range(16):
        OH[:, j * 16 + j] = 1.0
    return WfrT, WfiT, alt, VcT, VsT, altl, ident, OH


def emit_mini_stack(nc, ch, st, dr):
    sb = lambda n, sh, dt: st.enter_context(nc.sbuf_tensor(_u(n), sh, dt))
    consts = _dft_consts_np()
    names = ["WfrT", "WfiT", "alt", "VcT", "VsT", "altl", "ident", "OH"]
    shapes = [[128, 128], [128, 128], [128, 1], [128, 128], [128, 128],
              [1, 128], [128, 128], [128, 256]]
    dts = [BF16, BF16, BF16, BF16, BF16, BF16, F32, BF16]
    csb = {}
    cds = {}
    for nm, shp, dty, arr in zip(names, shapes, dts, consts):
        csb[nm] = sb("m_" + nm, shp, dty)
        cds[nm] = nc.inline_tensor(np.asarray(arr), "m_d_" + nm)
    hsb = [sb(f"m_h{t}", [128, MD], F32) for t in range(NT)]
    cmask = sb("m_cmask", [128, 8], F32)

    @ch.step("sync")
    def _(sync, sem=None):
        for nm in names:
            sync.dma_start(csb[nm][:, :], cds[nm][:, :])
        sync.dma_start(cmask[:, :], dr["cmask"][:, :])
        for t in range(NT):
            sync.dma_start(hsb[t][:, :], dr["h_scr"][128 * t:128 * (t + 1), :])

    for dd in range(4):
        _emit_mini_layer(nc, ch, st, dr, dd, hsb, csb, cmask)

    with ExitStack() as fst:
        fsb = lambda n, sh, dt: fst.enter_context(
            nc.sbuf_tensor(_u(n), sh, dt))
        lwrep = fsb("sf_lw", [128, MD], F32)
        lbrep = fsb("sf_lb", [128, MD], F32)
        borep = fsb("sf_bo", [128, D], F32)
        lny = fsb("sf_lny", [128, MD], BF16)
        hnT = [fsb(f"sf_hnT{cb}", [128, T], BF16) for cb in range(4)]
        wsb = [fsb(f"sf_w{cb}", [128, D], BF16) for cb in range(4)]
        sout = fsb("sf_sout", [128, D], BF16)
        tmp = [fsb("sf_scr", [128, MD], F32), fsb("sf_su", [128, 1], F32),
               fsb("sf_sq", [128, 1], F32), fsb("sf_nm", [128, 1], F32),
               fsb("sf_ve", [128, 1], F32), fsb("sf_rs", [128, 1], F32),
               fsb("sf_s0", [128, 1], F32), fsb("sf_lnt", [128, MD], F32)]
        psum = fst.enter_context(nc.psum_tensor(_u("sf_ps"), [128, D], F32))

        @ch.step("sync")
        def _(sync, sem=None):
            sync.dma_start(lwrep[:, :],
                           bass.AP(dr["lnlw"], 0, [[0, 128], [1, MD]]))
            sync.dma_start(lbrep[:, :],
                           bass.AP(dr["lnlb"], 0, [[0, 128], [1, MD]]))
            sync.dma_start(borep[:, :],
                           bass.AP(dr["bout"], 0, [[0, 128], [1, D]]))

        @ch.step("gpsimd")
        def _(gp, sem=None):
            for cb in range(4):
                gp.dma_start(wsb[cb][:, :],
                             dr["WoutT"][128 * cb:128 * (cb + 1), :])

        for j in range(NT):
            emit_wln_chunk(ch, hsb[j], lwrep, lbrep, lny, tmp)

            @ch.step("sync")
            def _(sync, sem=None):
                sync.dma_start(dr["lny_scr"][128 * j:128 * (j + 1), :],
                               lny[:, :])

        @ch.step("sync")
        def _(sync, sem=None):
            for cb in range(4):
                sync.dma_start(hnT[cb][:, :],
                               dr["lny_scr"][:, 128 * cb:128 * (cb + 1)],
                               transpose=True)

        for t in range(NT):
            @ch.step("tensor")
            def _(te, sem=None):
                for oq in range(8):
                    for cb in range(4):
                        mi = te.matmul_q(psum[:, 512 * oq:512 * (oq + 1)],
                                         hnT[cb][:, 128 * t:128 * (t + 1)],
                                         wsb[cb][:, 512 * oq:512 * (oq + 1)],
                                         start=(cb == 0), stop=(cb == 3))
                te.mark(mi)

            @ch.step("vector")
            def _(v, sem=None):
                v.tensor_tensor(sout[:, :], psum[:, :], borep[:, :], ALU.add)

            @ch.step("sync")
            def _(sync, sem=None):
                sync.dma_start(dr["s0_scr"][128 * t:128 * (t + 1), :],
                               sout[:, :])


def _emit_mini_layer(nc, ch, st, dr, dd, hsb, cc, cmask):
    with ExitStack() as lst:
        sb = lambda n, sh, dt: lst.enter_context(nc.sbuf_tensor(_u(n), sh, dt))
        WfrT, WfiT, alt = cc["WfrT"], cc["WfiT"], cc["alt"]
        VcT, VsT, altl = cc["VcT"], cc["VsT"], cc["altl"]
        ident, OH = cc["ident"], cc["OH"]
        PIT = lambda t_: t_.shape[1]

        lnt = sb("ml_lnt", [128, MD], F32)
        sums = sb("ml_sums", [128, 1], F32)
        sumsq = sb("ml_sumsq", [128, 1], F32)
        nmu = sb("ml_nmu", [128, 1], F32)
        ve = sb("ml_ve", [128, 1], F32)
        rstd = sb("ml_rstd", [128, 1], F32)
        scr = sb("ml_scr", [128, MD], F32)
        scr0 = sb("ml_scr0", [128, 1], F32)
        pst = ExitStack()
        sv = lambda n, sh, dt: pst.enter_context(
            nc.sbuf_tensor(_u(n), sh, dt))
        lwrep = sv("ml_lw", [128, MD], F32)
        lbrep = sv("ml_lb", [128, MD], F32)
        sclrep = sv("ml_scl", [128, MD], F32)
        prep_r = sv("ml_pr", [128, MD], F32)
        prep_i = sv("ml_pi", [128, MD], F32)
        pz = sv("ml_pz", [128, 8], F32)
        psc = sv("ml_psc", [128, 8], F32)
        przr = sv("ml_przr", [1, MD], F32)
        przi = sv("ml_przi", [1, MD], F32)
        rowsq = sv("ml_rowsq", [1, MD], F32)
        rowtmp = sv("ml_rowtmp", [1, MD], F32)
        prowpr = sv("ml_prowpr", [1, MD], F32)
        prowpi = sv("ml_prowpi", [1, MD], F32)
        tr8 = sv("ml_tr8", [128, 8], F32)
        Pcm_r = sv("ml_Pcmr", [128, 4 * 132], F32)
        Pcm_i = sv("ml_Pcmi", [128, 4 * 132], F32)
        tmpA = sv("ml_tmpA", [128, 4 * 64], F32)
        tmpB = sv("ml_tmpB", [128, 4 * 64], F32)
        Pmc_r = sv("ml_Pmcr", [128, MD], F32)
        Pmc_i = sv("ml_Pmci", [128, MD], F32)
        Pmc_rbf = sv("ml_Pmcrb", [128, MD], BF16)
        Prev_r = sv("ml_Prevr", [128, MD], F32)
        Prev_i = sv("ml_Previ", [128, MD], F32)
        frrev = sv("ml_frrev", [128, MD], BF16)
        firev = sv("ml_firev", [128, MD], BF16)
        Frr = sv("ml_Frr", [128, MD], BF16)
        Fri = sv("ml_Fri", [128, MD], BF16)
        Frr128 = sv("ml_Frr128", [1, MD], BF16)
        Pp1r = sv("ml_Pp1r", [128, MD], BF16)
        Pp1i = sv("ml_Pp1i", [128, MD], BF16)
        tbig1 = sv("ml_tb1", [128, MD], F32)
        tbig2 = sv("ml_tb2", [128, MD], F32)
        P128 = sv("ml_P128", [128, 8], F32)
        P128p = [sv(f"ml_P128p{k}", [128, 8], F32) for k in range(4)]
        conv_sb = sv("ml_conv", [128, MD], F32)
        ysb = sv("ml_ysb", [128, MD], BF16)
        Yr_sb = sv("ml_Yr", [128, MD], BF16)
        Yi_sb = sv("ml_Yi", [128, MD], BF16)
        yr128 = sv("ml_yr128", [1, MD], BF16)
        Hr_sb = sv("ml_Hr", [128, MD], BF16)
        Hi_sb = sv("ml_Hi", [128, MD], BF16)
        Hr128 = sv("ml_Hr128", [1, MD], BF16)
        tre = sv("ml_tre", [128, MD], BF16)
        tim = sv("ml_tim", [128, MD], BF16)
        tA = sv("ml_tA", [128, MD], BF16)
        tB = sv("ml_tB", [128, MD], BF16)
        U_r = sv("ml_Ur", [128, 64], F32)
        U_i = sv("ml_Ui", [128, 64], F32)
        Urow_r = sv("ml_Urowr", [16, MD], F32)
        Urow_i = sv("ml_Urowi", [16, MD], F32)
        PPc_r = sv("ml_PPcr", [128, 64], F32)
        PPc_i = sv("ml_PPci", [128, 64], F32)
        ts1 = sv("ml_ts1", [128, 64], F32)
        ts2 = sv("ml_ts2", [128, 64], F32)
        ts3 = sv("ml_ts3", [128, 64], F32)
        ts4 = sv("ml_ts4", [128, 64], F32)
        csh_r = sv("ml_cshr", [128, 64], F32)
        csh_i = sv("ml_cshi", [128, 64], F32)
        crow_sb = sv("ml_crow", [16, 2 * MD], F32)
        ci_sb = sv("ml_ci", [128, 8], F32)
        ci2_sb = sv("ml_ci2", [128, 8], F32)
        cfin_sb = sv("ml_cfin", [128, 8], F32)
        cinj_sb = sv("ml_cinj", [128, 8], F32)
        crj_r = sv("ml_crjr", [128, MD], BF16)
        crj_i = sv("ml_crji", [128, MD], BF16)

        @ch.step("sync")
        def _(sync, sem=None):
            for dst_, src_ in [(lwrep, dr[f"lnw{dd}"]),
                               (lbrep, dr[f"lnb{dd}"]),
                               (sclrep, dr[f"scl{dd}"])]:
                sync.dma_start(dst_[:, :],
                               bass.AP(src_, 0, [[0, 128], [1, MD]]))
            sync.dma_start(pz[:, :], dr[f"phzs{dd}"][:, :])
            sync.dma_start(przr[:, :], dr[f"phzrow{dd}"][0:1, :])
            sync.dma_start(przi[:, :], dr[f"phzrow{dd}"][1:2, :])
            sync.dma_start(ci_sb[:, :], dr[f"cinit{dd}"][:, :])
            sync.dma_start(cinj_sb[:, :], dr[f"cinj{dd}"][:, :])

        psm = lambda n, shp: pst.enter_context(
            nc.psum_tensor(_u(n), shp, F32))
        Yr_ps = psm("ml_Yrps", [128, MD])
        Yi_ps = psm("ml_Yips", [128, MD])
        yrow_ps = psm("ml_yrps", [1, MD])
        conv_ps = psm("ml_cps", [128, MD])
        Ur_ps = psm("ml_Urps", [16, MD])
        Ui_ps = psm("ml_Uips", [16, MD])
        tr_ps = psm("ml_trps", [128, 128])

        @ch.step("vector")
        def _(v, sem=None):
            v.tensor_tensor(tr8[:, 0:4], pz[:, 0:4], pz[:, 0:4], ALU.mult)
            v.tensor_tensor(tr8[:, 4:8], pz[:, 4:8], pz[:, 4:8], ALU.mult)
            v.tensor_tensor(tr8[:, 0:4], tr8[:, 0:4], tr8[:, 4:8],
                            ALU.add)
            v.tensor_tensor(rowsq[:, :], przr[:, :], przr[:, :], ALU.mult)
            v.tensor_tensor(rowtmp[:, :], przi[:, :], przi[:, :],
                            ALU.mult)
            v.tensor_tensor(rowsq[:, :], rowsq[:, :], rowtmp[:, :],
                            ALU.add)

        @ch.step("scalar")
        def _(s, sem=None):
            s.activation(tr8[:, 4:8], tr8[:, 0:4], AF.Sqrt)
            s.activation(tr8[:, 0:4], tr8[:, 4:8], AF.Tanh)
            s.activation(rowsq[:, :], rowsq[:, :], AF.Sqrt)
            s.activation(rowtmp[:, :], rowsq[:, :], AF.Tanh)

        @ch.step("vector")
        def _(v, sem=None):
            v.reciprocal(tr8[:, 4:8], tr8[:, 4:8])
            v.tensor_tensor(tr8[:, 4:8], tr8[:, 4:8], tr8[:, 0:4],
                            ALU.mult)
            v.tensor_tensor(psc[:, 0:4], pz[:, 0:4], tr8[:, 4:8],
                            ALU.mult)
            v.tensor_tensor(psc[:, 4:8], pz[:, 4:8], tr8[:, 4:8],
                            ALU.mult)
            v.reciprocal(rowsq[:, :], rowsq[:, :])
            v.tensor_tensor(rowsq[:, :], rowsq[:, :], rowtmp[:, :],
                            ALU.mult)
            v.tensor_tensor(prowpr[:, :], przr[:, :], rowsq[:, :],
                            ALU.mult)
            v.tensor_tensor(prowpi[:, :], przi[:, :], rowsq[:, :],
                            ALU.mult)

        @ch.step("gpsimd")
        def _(g, sem=None):
            g.dma_start(dr["prow_scr"][0:1, :], prowpr[:, :])
            g.dma_start(dr["prow_scr"][1:2, :], prowpi[:, :])

        @ch.step("gpsimd")
        def _(g, sem=None):
            g.dma_start(prep_r[:, :],
                        bass.AP(dr["prow_scr"], 0, [[0, 128], [1, MD]]))
            g.dma_start(prep_i[:, :],
                        bass.AP(dr["prow_scr"], MD, [[0, 128], [1, MD]]))

        @ch.step("vector")
        def _(v, sem=None):
            for cb in range(4):
                o = cb * 132
                v.memset(Pcm_r[:, o:o + 1], 1.0)
                v.memset(Pcm_i[:, o:o + 1], 0.0)
                v.tensor_copy(Pcm_r[:, o + 1:o + 2], psc[:, cb:cb + 1])
                v.tensor_copy(Pcm_i[:, o + 1:o + 2],
                              psc[:, 4 + cb:5 + cb])
            w = 1
            for k in range(7):
                sl = lambda tsr, off, cnt: ap(tsr, off, (PIT(tsr), 128),
                                              (132, 4), (1, cnt))
                Ar, Ai = sl(Pcm_r, 1, w), sl(Pcm_i, 1, w)
                Br = ap(Pcm_r, w, (PIT(Pcm_r), 128), (132, 4), (0, w))
                Bi = ap(Pcm_i, w, (PIT(Pcm_i), 128), (132, 4), (0, w))
                Or, Oi = sl(Pcm_r, w + 1, w), sl(Pcm_i, w + 1, w)
                t1 = ap(tmpA, 0, (PIT(tmpA), 128), (64, 4), (1, w))
                t2 = ap(tmpB, 0, (PIT(tmpB), 128), (64, 4), (1, w))
                v.tensor_tensor(t1, Ar, Br, ALU.mult)
                v.tensor_tensor(t2, Ai, Bi, ALU.mult)
                v.tensor_tensor(Or, t1, t2, ALU.subtract)
                v.tensor_tensor(t1, Ar, Bi, ALU.mult)
                v.tensor_tensor(t2, Ai, Br, ALU.mult)
                v.tensor_tensor(Oi, t1, t2, ALU.add)
                w *= 2

        for cb in range(4):
            for ri in range(2):
                @ch.step("tensor")
                def _(te, sem=None):
                    src = [Pcm_r, Pcm_i][ri]
                    te.transpose(tr_ps[:, :],
                                 src[:, cb * 132:cb * 132 + 128],
                                 ident[:, :])

                @ch.step("vector")
                def _(v, sem=None):
                    dst = [Pmc_r, Pmc_i][ri]
                    v.tensor_copy(dst[:, cb * 128:(cb + 1) * 128],
                                  tr_ps[:, :])

        @ch.step("vector")
        def _(v, sem=None):
            v.tensor_copy(ap(Prev_r, 0, (PIT(Prev_r), 128), (128, 4),
                             (1, 128)),
                          ap(Pcm_r, 127, (PIT(Pcm_r), 128), (132, 4),
                             (-1, 128)))
            v.tensor_copy(ap(Prev_i, 0, (PIT(Prev_i), 128), (128, 4),
                             (1, 128)),
                          ap(Pcm_i, 127, (PIT(Pcm_i), 128), (132, 4),
                             (-1, 128)))

        for cb in range(4):
            for ri in range(2):
                @ch.step("tensor")
                def _(te, sem=None):
                    src = [Prev_r, Prev_i][ri]
                    te.transpose(tr_ps[:, :],
                                 src[:, cb * 128:(cb + 1) * 128],
                                 ident[:, :])

                @ch.step("vector")
                def _(v, sem=None):
                    dst = [frrev, firev][ri]
                    v.tensor_copy(dst[:, cb * 128:(cb + 1) * 128],
                                  tr_ps[:, :])

        @ch.step("vector")
        def _(v, sem=None):
            v.tensor_copy(Pmc_rbf[:, :], Pmc_r[:, :])

        @ch.step("tensor")
        def _(te, sem=None):
            te.matmul_q(Yr_ps[:, :], WfrT[:, :], Pmc_rbf[:, :],
                        start=True, stop=True)
            te.matmul_q(Yi_ps[:, :], WfiT[:, :], Pmc_rbf[:, :],
                        start=True, stop=True)
            te.matmul(yrow_ps[0:1, :], alt[:, :], Pmc_rbf[:, :],
                      start=True, stop=True)

        @ch.step("vector")
        def _(v, sem=None):
            v.tensor_tensor(Frr[:, :], Yr_ps[:, :], sclrep[:, :],
                            ALU.mult)
            v.tensor_tensor(Fri[:, :], Yi_ps[:, :], sclrep[:, :],
                            ALU.mult)
            v.tensor_tensor(Frr128[0:1, :], yrow_ps[0:1, :],
                            sclrep[0:1, :], ALU.mult)

        @ch.step("vector")
        def _(v, sem=None):
            v.tensor_tensor(tbig1[:, :], Pmc_r[:, :], prep_r[:, :],
                            ALU.mult)
            v.tensor_tensor(tbig2[:, :], Pmc_i[:, :], prep_i[:, :],
                            ALU.mult)
            v.tensor_tensor(tbig1[:, :], tbig1[:, :], tbig2[:, :],
                            ALU.subtract)
            v.tensor_tensor(scr[:, :], Pmc_r[:, :], prep_i[:, :],
                            ALU.mult)
            v.tensor_tensor(tbig2[:, :], Pmc_i[:, :], prep_r[:, :],
                            ALU.mult)
            v.tensor_tensor(tbig2[:, :], scr[:, :], tbig2[:, :], ALU.add)
            v.tensor_tensor(Pp1r[:, :], tbig1[:, :], sclrep[:, :],
                            ALU.mult)
            v.tensor_tensor(Pp1i[:, :], tbig2[:, :], sclrep[:, :],
                            ALU.mult)
            v.tensor_copy(P128[:, 0:4],
                          ap(Pcm_r, 128, (PIT(Pcm_r), 128), (132, 4)))
            v.tensor_copy(P128[:, 4:8],
                          ap(Pcm_i, 128, (PIT(Pcm_i), 128), (132, 4)))
            v.tensor_copy(P128p[0][:, :], P128[:, :])
            for k in range(1, 4):
                a = P128p[k - 1]
                o = P128p[k]
                v.tensor_tensor(tr8[:, 0:4], a[:, 0:4], a[:, 0:4],
                                ALU.mult)
                v.tensor_tensor(tr8[:, 4:8], a[:, 4:8], a[:, 4:8],
                                ALU.mult)
                v.tensor_tensor(o[:, 0:4], tr8[:, 0:4], tr8[:, 4:8],
                                ALU.subtract)
                v.tensor_tensor(tr8[:, 0:4], a[:, 0:4], a[:, 4:8],
                                ALU.mult)
                v.tensor_tensor(o[:, 4:8], tr8[:, 0:4], tr8[:, 0:4],
                                ALU.add)

        for seg in range(2):
            j0 = seg * 16
            cinit = ci_sb if seg == 0 else ci2_sb
            for jl in range(16):
                j = j0 + jl

                @ch.step("vector")
                def _(v, sem=None):
                    v.tensor_scalar(scr[:, :], hsb[j][:, :], 1.0, 0.0,
                                    ALU.mult, ALU.add,
                                    accum_out=sums[:, :])
                    v.scalar_tensor_tensor(scr[:, :], hsb[j][:, :], 1.0,
                                           hsb[j][:, :], ALU.mult,
                                           ALU.mult,
                                           accum_out=sumsq[:, :])
                    v.tensor_scalar(nmu[:, :], sums[:, :], -1.0 / MD,
                                    None, ALU.mult)
                    v.tensor_scalar(ve[:, :], sumsq[:, :], 1.0 / MD, EPS,
                                    ALU.mult, ALU.add)
                    v.tensor_tensor(scr0[:, :], nmu[:, :], nmu[:, :],
                                    ALU.mult)
                    v.tensor_tensor(ve[:, :], ve[:, :], scr0[:, :],
                                    ALU.subtract)
                    v.reciprocal(ve[:, :], ve[:, :])

                @ch.step("scalar")
                def _(s, sem=None):
                    s.activation(rstd[:, :], ve[:, :], AF.Sqrt)

                @ch.step("vector")
                def _(v, sem=None):
                    v.tensor_scalar(lnt[:, :], hsb[j][:, :], nmu[:, 0:1],
                                    rstd[:, 0:1], ALU.add, ALU.mult)
                    v.tensor_tensor(lnt[:, :], lnt[:, :], lwrep[:, :],
                                    ALU.mult)
                    v.tensor_tensor(lnt[:, :], lnt[:, :], lbrep[:, :],
                                    ALU.add)

                @ch.step("scalar")
                def _(s, sem=None):
                    s.activation(ysb[:, :], lnt[:, :], AF.Silu)

                @ch.step("tensor")
                def _(te, sem=None):
                    te.matmul_q(Yr_ps[:, :], WfrT[:, :], ysb[:, :],
                                start=True, stop=True)
                    te.matmul_q(Yi_ps[:, :], WfiT[:, :], ysb[:, :],
                                start=True, stop=True)
                    te.matmul(yrow_ps[0:1, :], alt[:, :], ysb[:, :],
                              start=True, stop=True)

                @ch.step("scalar")
                def _(s, sem=None):
                    s.copy(Yr_sb[:, :], Yr_ps[:, :])
                    s.copy(Yi_sb[:, :], Yi_ps[:, :])
                    s.copy(yr128[0:1, :], yrow_ps[0:1, :])

                @ch.step("vector")
                def _(v, sem=None):
                    v.tensor_tensor(tA[:, :], Yr_sb[:, :], Frr[:, :],
                                    ALU.mult)
                    v.tensor_tensor(tB[:, :], Yi_sb[:, :], Fri[:, :],
                                    ALU.mult)
                    v.tensor_tensor(Hr_sb[:, :], tA[:, :], tB[:, :],
                                    ALU.subtract)
                    v.tensor_tensor(tA[:, :], Yr_sb[:, :], Fri[:, :],
                                    ALU.mult)
                    v.tensor_tensor(tB[:, :], Yi_sb[:, :], Frr[:, :],
                                    ALU.mult)
                    v.tensor_tensor(Hi_sb[:, :], tA[:, :], tB[:, :],
                                    ALU.add)
                    v.tensor_tensor(Hr128[0:1, :], yr128[0:1, :],
                                    Frr128[0:1, :], ALU.mult)
                    v.tensor_tensor(tre[:, :], ysb[:, :], frrev[:, :],
                                    ALU.mult)
                    v.tensor_tensor(tim[:, :], ysb[:, :], firev[:, :],
                                    ALU.mult)

                @ch.step("tensor")
                def _(te, sem=None):
                    te.matmul_q(Ur_ps[0:16, :],
                                OH[:, jl * 16:(jl + 1) * 16],
                                tre[:, :], start=(jl == 0),
                                stop=(jl == 15), skip_group_check=True)
                    te.matmul_q(Ui_ps[0:16, :],
                                OH[:, jl * 16:(jl + 1) * 16],
                                tim[:, :], start=(jl == 0),
                                stop=(jl == 15), skip_group_check=True)
                    te.matmul_q(conv_ps[:, :], VcT[:, :], Hr_sb[:, :],
                                start=True, stop=False,
                                skip_group_check=True)
                    te.matmul_q(conv_ps[:, :], VsT[:, :], Hi_sb[:, :],
                                start=False, stop=False,
                                skip_group_check=True)
                    te.matmul(conv_ps[:, :], altl[0:1, :], Hr128[0:1, :],
                              start=False, stop=True,
                              skip_group_check=True)

                @ch.step("scalar")
                def _(s, sem=None):
                    s.copy(conv_sb[:, :], conv_ps[:, :])

                @ch.step("vector")
                def _(v, sem=None):
                    v.tensor_tensor(hsb[j][:, :], hsb[j][:, :],
                                    conv_sb[:, :], ALU.add)

            @ch.step("vector")
            def _(v, sem=None):
                v.tensor_copy(Urow_r[:, :], Ur_ps[:, :])
                v.tensor_copy(Urow_i[:, :], Ui_ps[:, :])

            for cb in range(4):
                for ri in range(2):
                    @ch.step("tensor")
                    def _(te, sem=None):
                        src = [Urow_r, Urow_i][ri]
                        te.transpose(tr_ps[0:128, 0:16],
                                     src[0:16, cb * 128:(cb + 1) * 128],
                                     ident[0:16, 0:16])

                    @ch.step("vector")
                    def _(v, sem=None):
                        dst = [U_r, U_i][ri]
                        v.tensor_copy(dst[:, cb * 16:(cb + 1) * 16],
                                      tr_ps[0:128, 0:16])

            @ch.step("vector")
            def _(v, sem=None):
                w = 1
                for k in range(4):
                    n = 16 - w
                    sl = lambda tsr, off: ap(tsr, off, (PIT(tsr), 128),
                                             (16, 4), (1, n))
                    zsr, zsi = sl(U_r, 0), sl(U_i, 0)
                    tgr, tgi = sl(U_r, w), sl(U_i, w)
                    Pr = ap(P128p[k], 0, (8, 128), (1, 4), (0, n))
                    Pi = ap(P128p[k], 4, (8, 128), (1, 4), (0, n))
                    t1 = ap(ts1, 0, (PIT(ts1), 128), (16, 4), (1, n))
                    t2 = ap(ts2, 0, (PIT(ts2), 128), (16, 4), (1, n))
                    t3 = ap(ts3, 0, (PIT(ts3), 128), (16, 4), (1, n))
                    t4 = ap(ts4, 0, (PIT(ts4), 128), (16, 4), (1, n))
                    v.tensor_tensor(t1, zsr, Pr, ALU.mult)
                    v.tensor_tensor(t2, zsi, Pi, ALU.mult)
                    v.tensor_tensor(t3, zsr, Pi, ALU.mult)
                    v.tensor_tensor(t4, zsi, Pr, ALU.mult)
                    v.tensor_tensor(t1, t1, t2, ALU.subtract)
                    v.tensor_tensor(t3, t3, t4, ALU.add)
                    v.tensor_tensor(tgr, tgr, t1, ALU.add)
                    v.tensor_tensor(tgi, tgi, t3, ALU.add)
                    w *= 2
                v.tensor_copy(ap(PPc_r, 0, (PIT(PPc_r), 128), (16, 4)),
                              P128[:, 0:4])
                v.tensor_copy(ap(PPc_i, 0, (PIT(PPc_i), 128), (16, 4)),
                              P128[:, 4:8])
                w = 1
                for k in range(4):
                    n = min(2 * w, 16) - w
                    sl2 = lambda tsr, off: ap(tsr, off, (PIT(tsr), 128),
                                              (16, 4), (1, n))
                    ar, ai = sl2(PPc_r, 0), sl2(PPc_i, 0)
                    tgr, tgi = sl2(PPc_r, w), sl2(PPc_i, w)
                    Pr = ap(P128p[k], 0, (8, 128), (1, 4), (0, n))
                    Pi = ap(P128p[k], 4, (8, 128), (1, 4), (0, n))
                    t1 = ap(ts1, 0, (PIT(ts1), 128), (16, 4), (1, n))
                    t2 = ap(ts2, 0, (PIT(ts2), 128), (16, 4), (1, n))
                    v.tensor_tensor(t1, ar, Pr, ALU.mult)
                    v.tensor_tensor(t2, ai, Pi, ALU.mult)
                    v.tensor_tensor(tgr, t1, t2, ALU.subtract)
                    v.tensor_tensor(t1, ar, Pi, ALU.mult)
                    v.tensor_tensor(t2, ai, Pr, ALU.mult)
                    v.tensor_tensor(tgi, t1, t2, ALU.add)
                    w *= 2
                cir = ap(cinit, 0, (8, 128), (1, 4), (0, 16))
                cii = ap(cinit, 4, (8, 128), (1, 4), (0, 16))
                sl3 = lambda tsr: ap(tsr, 0, (PIT(tsr), 128), (16, 4),
                                     (1, 16))
                v.tensor_tensor(sl3(ts1), sl3(PPc_r), cir, ALU.mult)
                v.tensor_tensor(sl3(ts2), sl3(PPc_i), cii, ALU.mult)
                v.tensor_tensor(sl3(ts1), sl3(ts1), sl3(ts2),
                                ALU.subtract)
                v.tensor_tensor(sl3(U_r), sl3(U_r), sl3(ts1), ALU.add)
                v.tensor_tensor(sl3(ts1), sl3(PPc_r), cii, ALU.mult)
                v.tensor_tensor(sl3(ts2), sl3(PPc_i), cir, ALU.mult)
                v.tensor_tensor(sl3(ts1), sl3(ts1), sl3(ts2), ALU.add)
                v.tensor_tensor(sl3(U_i), sl3(U_i), sl3(ts1), ALU.add)
                v.tensor_copy(cfin_sb[:, 0:4],
                              ap(U_r, 15, (PIT(U_r), 128), (16, 4)))
                v.tensor_copy(cfin_sb[:, 4:8],
                              ap(U_i, 15, (PIT(U_i), 128), (16, 4)))
                v.tensor_copy(ap(csh_r, 0, (PIT(csh_r), 128), (16, 4)),
                              cinit[:, 0:4])
                v.tensor_copy(ap(csh_i, 0, (PIT(csh_i), 128), (16, 4)),
                              cinit[:, 4:8])
                v.tensor_copy(ap(csh_r, 1, (PIT(csh_r), 128), (16, 4),
                                 (1, 15)),
                              ap(U_r, 0, (PIT(U_r), 128), (16, 4),
                                 (1, 15)))
                v.tensor_copy(ap(csh_i, 1, (PIT(csh_i), 128), (16, 4),
                                 (1, 15)),
                              ap(U_i, 0, (PIT(U_i), 128), (16, 4),
                                 (1, 15)))
                if seg == 0:
                    v.tensor_tensor(ci2_sb[:, :], cfin_sb[:, :],
                                    cmask[:, :], ALU.mult)
                    v.tensor_tensor(ci2_sb[:, :], ci2_sb[:, :],
                                    cinj_sb[:, :], ALU.add)

            for cb in range(4):
                for ri in range(2):
                    @ch.step("tensor")
                    def _(te, sem=None):
                        src = [csh_r, csh_i][ri]
                        te.transpose(tr_ps[0:16, 0:128],
                                     src[:, cb * 16:(cb + 1) * 16],
                                     ident[:, :])

                    @ch.step("vector")
                    def _(v, sem=None):
                        v.tensor_copy(
                            crow_sb[0:16, ri * MD + cb * 128:
                                    ri * MD + (cb + 1) * 128],
                            tr_ps[0:16, 0:128])

            @ch.step("sync")
            def _(sync, sem=None):
                sync.dma_start(
                    ap(dr["crow_scr"], 0, (512, 16), (8192, 2), (1, 512)),
                    ap(crow_sb, 0, (PIT(crow_sb), 16), (MD, 2), (1, MD)))

            for jl in range(16):
                j = j0 + jl

                @ch.step("gpsimd")
                def _(g, sem=None):
                    g.dma_start(crj_r[:, :],
                                bass.AP(dr["crow_scr"], jl * MD,
                                        [[0, 128], [1, MD]]))
                    g.dma_start(crj_i[:, :],
                                bass.AP(dr["crow_scr"], 8192 + jl * MD,
                                        [[0, 128], [1, MD]]))

                @ch.step("vector")
                def _(v, sem=None):
                    v.tensor_tensor(tbig1[:, :], crj_r[:, :], Pp1r[:, :],
                                    ALU.mult)
                    v.tensor_tensor(tbig2[:, :], crj_i[:, :], Pp1i[:, :],
                                    ALU.mult)
                    v.tensor_tensor(tbig1[:, :], tbig1[:, :], tbig2[:, :],
                                    ALU.subtract)
                    v.tensor_tensor(hsb[j][:, :], hsb[j][:, :],
                                    tbig1[:, :], ALU.add)

        pst.close()

        lw2 = sb("ml_lw2", [128, MD], F32)
        lb2 = sb("ml_lb2", [128, MD], F32)
        b2rep = sb("ml_b2", [128, MD], F32)
        b1sb = sb("ml_b1", [128, 16], F32)
        lny = sb("ml_lny", [128, MD], BF16)
        lnyT = [sb(f"ml_lnyT{cb}", [128, TH], BF16) for cb in range(4)]
        w1sb = [sb(f"ml_w1{cb}", [128, MF], BF16) for cb in range(4)]
        w2sb = [sb(f"ml_w2{f}", [128, MD], BF16) for f in range(16)]
        y2sb = [sb(f"ml_y2{f}", [128, 512], BF16) for f in range(16)]
        ffn_ps = [lst.enter_context(nc.psum_tensor(_u("ml_fps"), [128, 512],
                                                   F32)) for _i in range(2)]
        ffh_ps = [lst.enter_context(nc.psum_tensor(_u("ml_fhps"), [128, MD],
                                                   F32)) for _i in range(2)]
        tmp = (scr, sums, sumsq, nmu, ve, rstd, scr0, lnt)

        @ch.step("sync")
        def _(sync, sem=None):
            for dst_, src_ in [(lw2, dr[f"ln2w{dd}"]), (lb2, dr[f"ln2b{dd}"]),
                               (b2rep, dr[f"b2_{dd}"])]:
                sync.dma_start(dst_[:, :],
                               bass.AP(src_, 0, [[0, 128], [1, MD]]))
            sync.dma_start(b1sb[:, :], dr[f"b1c{dd}"][:, :])

        @ch.step("gpsimd")
        def _(g, sem=None):
            for cb in range(4):
                g.dma_start(w1sb[cb][:, :],
                            dr[f"W1T{dd}"][128 * cb:128 * (cb + 1), :])
            for f in range(16):
                g.dma_start(w2sb[f][:, :],
                            dr[f"W2T{dd}"][128 * f:128 * (f + 1), :])

        for j in range(NT):
            emit_wln_chunk(ch, hsb[j], lw2, lb2, lny, tmp)

            @ch.step("sync")
            def _(sync, sem=None):
                sync.dma_start(dr["lny_scr"][128 * j:128 * (j + 1), :],
                               lny[:, :])

        for th2 in range(2):
            t0 = th2 * TH

            @ch.step("sync")
            def _(sync, sem=None):
                for cb in range(4):
                    sync.dma_start(lnyT[cb][:, :],
                                   dr["lny_scr"][t0:t0 + TH,
                                                 128 * cb:128 * (cb + 1)],
                                   transpose=True)

            for tb in range(4):
                c0 = tb * 512
                for f in range(16):
                    @ch.step("tensor")
                    def _(te, sem=None):
                        for cb in range(4):
                            mi = te.matmul_q(
                                ffn_ps[f % 2][:, :],
                                w1sb[cb][:, 128 * f:128 * (f + 1)],
                                lnyT[cb][:, c0:c0 + 512],
                                start=(cb == 0), stop=(cb == 3))
                        te.mark(mi)

                    @ch.step("scalar")
                    def _(s, sem=None):
                        s.activation(y2sb[f][:, :], ffn_ps[f % 2][:, :],
                                     AF.Silu, bias=b1sb[:, f:f + 1])

                for tt in range(4):
                    j = th2 * 16 + tb * 4 + tt

                    @ch.step("tensor")
                    def _(te, sem=None):
                        for f in range(16):
                            mi = te.matmul_q(
                                ffh_ps[tt % 2][:, :],
                                y2sb[f][:, 128 * tt:128 * (tt + 1)],
                                w2sb[f][:, :],
                                start=(f == 0), stop=(f == 15))
                        te.mark(mi)

                    @ch.step("vector")
                    def _(v, sem=None):
                        v.tensor_tensor(scr[:, :], ffh_ps[tt % 2][:, :],
                                        b2rep[:, :], ALU.add)
                        v.tensor_tensor(hsb[j][:, :], hsb[j][:, :], scr[:, :],
                                        ALU.add)


def emit_combine(nc, ch, st, xsrc, s_scr, gT_scr, xdst, ntiles,
                 xrow0, srow0, gcol0):
    sb = lambda n, sh, dt: st.enter_context(nc.sbuf_tensor(_u(n), sh, dt))
    NB = 3
    xt = [sb("cb_x", [128, D], F32) for _i in range(NB)]
    st_ = [sb("cb_s", [128, D], BF16) for _i in range(NB)]
    gt = [sb("cb_g", [128, D], BF16) for _i in range(NB)]
    pt = sb("cb_p", [128, D], F32)
    ld = {}
    cmp_ = {}
    stq = {}
    for j in range(ntiles):
        b = j % NB

        @ch.step("sync", at=smax(cmp_.get(j - NB), stq.get(j - NB)))
        def _(sync, sem=None):
            sync.dma_start(gt[b][:, :],
                           gT_scr[:, gcol0 + 128 * j:gcol0 + 128 * (j + 1)],
                           transpose=True)
            sync.dma_start(st_[b][:, :],
                           s_scr[srow0 + 128 * j:srow0 + 128 * (j + 1), :])

        @ch.step("gpsimd", at=smax(cmp_.get(j - NB), stq.get(j - NB)))
        def _(g, sem=None):
            g.dma_start(xt[b][:, :],
                        xsrc[xrow0 + 128 * j:xrow0 + 128 * (j + 1), :])
        ld[j] = ch.snap()

        @ch.step("vector", at=ld[j])
        def _(v, sem=None):
            v.tensor_tensor(pt[:, :], st_[b][:, :], gt[b][:, :], ALU.mult)
            v.tensor_tensor(xt[b][:, :], xt[b][:, :], pt[:, :], ALU.add)
        cmp_[j] = ch.snap()

        @ch.step("gpsimd", at=cmp_[j])
        def _(g, sem=None):
            g.dma_start(xdst[128 * j:128 * (j + 1), :], xt[b][:, :])
        stq[j] = ch.snap()


def emit_qkv(nc, ch, st, dr):
    sb = lambda n, sh, dt: st.enter_context(nc.sbuf_tensor(_u(n), sh, dt))
    NV = 32
    xsb = [sb(f"qk_x{v}", [128, TH], BF16) for v in range(NV)]
    wc = sb("qk_wc", [128, D], BF16)
    qout = sb("qk_qout", [128, TH], BF16)
    q_ps = st.enter_context(nc.psum_tensor(_u("qk_ps"), [128, TH], F32))
    for th in range(2):
        t0 = th * TH

        @ch.step("sync")
        def _(sync, sem=None):
            for v in range(NV):
                sync.dma_start(xsb[v][:, :],
                               dr["xn1_scr"][t0:t0 + TH,
                                             128 * v:128 * (v + 1)],
                               transpose=True)

        ocs = [32, 33] if th == 0 else [32, 33] + list(range(32))
        for oc in ocs:
            @ch.step("gpsimd")
            def _(gp, sem=None):
                if oc < 32:
                    gp.dma_start(ap(wc, 0, (D, 128), (128, 32), (1, 128)),
                                 ap(dr["Wq"], oc * 128, (D, 128),
                                    (128 * D, 32), (1, 128)))
                else:
                    o2 = oc - 32
                    gp.dma_start(ap(wc, 0, (D, 128), (128, 32), (1, 128)),
                                 ap(dr["Wkv"], o2 * 128, (256, 128),
                                    (128 * 256, 32), (1, 128)))

            @ch.step("tensor")
            def _(te, sem=None):
                for tq in range(4):
                    for g in range(NV):
                        mi = te.matmul_q(q_ps[:, 512 * tq:512 * (tq + 1)],
                                         wc[:, 128 * g:128 * (g + 1)],
                                         xsb[g][:, 512 * tq:512 * (tq + 1)],
                                         start=(g == 0), stop=(g == NV - 1),
                                         skip_group_check=True)
                te.mark(mi)

            @ch.step("vector")
            def _(v, sem=None):
                v.tensor_copy(qout[:, :], q_ps[:, :])

            @ch.step("sync")
            def _(sync, sem=None):
                if oc < 32:
                    sync.dma_start(dr["qt_scr"][128 * oc:128 * (oc + 1), :],
                                   qout[:, :])
                else:
                    o2 = oc - 32
                    sync.dma_start(dr["kvt_scr"][128 * o2:128 * (o2 + 1),
                                                 t0:t0 + TH], qout[:, :])


def emit_attn(nc, ch, st, dr, slopes_np):
    sb = lambda n, sh, dt: st.enter_context(nc.sbuf_tensor(_u(n), sh, dt))
    heads = 32
    qsb = [sb(f"at_q{h}", [128, TH], BF16) for h in range(heads)]
    ksb = sb("at_k", [128, T], BF16)
    vtk = [sb(f"at_v{kt}", [128, 128], BF16) for kt in range(NT)]
    slrep = sb("at_sl", [128, heads], F32)
    pmrep = sb("at_pm", [128, NT], F32)
    iota = sb("at_iota", [128, 1], F32)
    onescol = sb("at_ones", [128, 1], BF16)
    bias_v = sb("at_bias", [128, 1], F32)
    probs = sb("at_probs", [128, 512], BF16)
    msc = sb("at_msc", [128, 512], F32)
    maskfull = [sb(f"at_mf{i}", [128, 512], F32) for i in range(4)]
    onum_bf = sb("at_onum", [128, 512], BF16)
    den_sb = sb("at_den", [1, 512], F32)
    sc_ps = [st.enter_context(nc.psum_tensor(_u("at_scps"), [128, 512], F32))
             for _i in range(2)]
    av_ps = st.enter_context(nc.psum_tensor(_u("at_avps"), [128, 512], F32))
    den_ps = st.enter_context(nc.psum_tensor(_u("at_dps"), [1, 512], F32))

    iota_d = nc.inline_tensor(np.arange(128, dtype=np.float32)[:, None],
                              "at_iota_d")
    ones_d = nc.inline_tensor(np.asarray(np.ones((128, 1), dtype=BF)),
                              "at_ones_d")
    mf_np = []
    for d0 in (0, 128, 256, 384):
        p = np.arange(128)[:, None]
        c = np.arange(512)[None, :]
        keep = (c >= d0 + 128) | ((c >= d0) & (p <= c - d0))
        mf_np.append(np.where(keep, 0.0, NEG).astype(np.float32))
    mf_d = [nc.inline_tensor(m, f"at_mf_d{i}") for i, m in enumerate(mf_np)]

    @ch.step("sync")
    def _(sync, sem=None):
        for h in range(heads):
            sync.dma_start(qsb[h][:, :],
                           dr["qt_scr"][128 * h:128 * (h + 1), :])
        sync.dma_start(ksb[:, :], dr["kvt_scr"][0:128, :])
        for kt in range(NT):
            sync.dma_start(vtk[kt][:, :],
                           dr["kvt_scr"][128:256, 128 * kt:128 * (kt + 1)],
                           transpose=True)
        sync.dma_start(slrep[:, :],
                       bass.AP(dr["slopes"], 0, [[0, 128], [1, heads]]))
        sync.dma_start(pmrep[:, :], dr["pmask"][:, :])
        sync.dma_start(iota[:, :], iota_d[:, :])
        sync.dma_start(onescol[:, :], ones_d[:, :])
        for i in range(4):
            sync.dma_start(maskfull[i][:, :], mf_d[i][:, :])

    for h in range(heads):
        sl = float(slopes_np[h])
        ebw = 512 if sl * 511.0 <= WTHRESH else 64
        nex = 512 // ebw
        for qc in range(4):
            q0 = qc * 512
            qglob0 = TH + q0
            kts = []
            for kt in range(NT):
                kb = kt * 128
                if kb > qglob0 + 511:
                    continue
                if sl * (qglob0 - (kb + 127)) > WTHRESH:
                    continue
                kts.append(kt)
            for ki, kt in enumerate(kts):
                kb = kt * 128
                first = (ki == 0)
                diag = kb >= qglob0
                d0 = kb - qglob0 if diag else None
                scp = sc_ps[ki % 2]

                @ch.step("tensor")
                def _(te, sem=None):
                    te.matmul(scp[:, :], ksb[:, kb:kb + 128],
                              qsb[h][:, q0:q0 + 512],
                              start=True, stop=True, skip_group_check=True)

                if diag:
                    @ch.step("vector")
                    def _(v, sem=None):
                        v.tensor_tensor(msc[:, :], scp[:, :],
                                        maskfull[d0 // 128][:, :], ALU.add)

                for ce in range(nex):
                    c = ce * ebw

                    @ch.step("vector")
                    def _(v, sem=None):
                        v.tensor_scalar(bias_v[:, :], iota[:, :],
                                        float(kb - qglob0 - c),
                                        slrep[:, h:h + 1], ALU.add, ALU.mult)
                        v.tensor_tensor(bias_v[:, :], bias_v[:, :],
                                        pmrep[:, kt:kt + 1], ALU.add)

                    @ch.step("scalar")
                    def _(s, sem=None):
                        src = msc if diag else scp
                        s.activation(probs[:, c:c + ebw], src[:, c:c + ebw],
                                     AF.Exp, bias=bias_v[:, 0:1],
                                     scale=ATT_SCALE)

                @ch.step("tensor")
                def _(te, sem=None):
                    te.matmul_q(av_ps[:, :], vtk[kt][:, :], probs[:, :],
                                start=first, stop=False,
                                skip_group_check=True)
                    te.matmul(den_ps[0:1, :], onescol[:, :], probs[:, :],
                              start=first, stop=False, skip_group_check=True)

            @ch.step("vector")
            def _(v, sem=None):
                v.tensor_copy(den_sb[0:1, :], den_ps[0:1, :])
                v.tensor_copy(onum_bf[:, :], av_ps[:, :])

            @ch.step("sync")
            def _(sync, sem=None):
                sync.dma_start(dr["onum_scr"][128 * h:128 * (h + 1),
                                              q0:q0 + 512], onum_bf[:, :])
                sync.dma_start(dr["oden_scr"][h:h + 1, q0:q0 + 512],
                               den_sb[0:1, :])


def emit_wo(nc, ch, st, dr):
    sb = lambda n, sh, dt: st.enter_context(nc.sbuf_tensor(_u(n), sh, dt))
    atsb = [sb(f"wo_a{h}", [128, TH], BF16) for h in range(32)]
    drow = sb("wo_drow", [1, TH], F32)
    rrow = sb("wo_rrow", [1, TH], F32)
    onesrow = sb("wo_ones", [1, 128], F32)
    drep = sb("wo_drep", [128, TH], F32)
    numf = sb("wo_numf", [128, TH], F32)
    wosb = [sb(f"wo_w{cb}", [128, 512], BF16) for cb in range(32)]
    sout = sb("wo_sout", [128, 512], BF16)
    dr_ps = st.enter_context(nc.psum_tensor(_u("wo_dps"), [128, TH], F32))
    s_ps = [st.enter_context(nc.psum_tensor(_u("wo_sps"), [128, 512], F32))
            for _i in range(2)]
    onesr_d = nc.inline_tensor(np.ones((1, 128), dtype=np.float32), "wo_or_d")

    @ch.step("sync")
    def _(sync, sem=None):
        sync.dma_start(onesrow[:, :], onesr_d[:, :])

    for h in range(32):
        @ch.step("gpsimd")
        def _(gp, sem=None):
            gp.dma_start(numf[:, :],
                         dr["onum_scr"][128 * h:128 * (h + 1), :])

        @ch.step("sync")
        def _(sync, sem=None):
            sync.dma_start(drow[0:1, :], dr["oden_scr"][h:h + 1, :])

        @ch.step("vector")
        def _(v, sem=None):
            v.reciprocal(rrow[0:1, :], drow[0:1, :])

        @ch.step("tensor")
        def _(te, sem=None):
            for c in range(0, TH, 512):
                mi = te.matmul_q(dr_ps[:, c:c + 512], onesrow[:, :],
                                 rrow[0:1, c:c + 512], start=True, stop=True,
                                 skip_group_check=True)
            te.mark(mi)

        @ch.step("vector")
        def _(v, sem=None):
            v.tensor_copy(drep[:, :], dr_ps[:, :])
            v.tensor_tensor(atsb[h][:, :], numf[:, :], drep[:, :], ALU.mult)

    for oc in range(8):
        @ch.step("gpsimd")
        def _(gp, sem=None):
            for cb in range(32):
                gp.dma_start(wosb[cb][:, :],
                             dr["Wo"][128 * cb:128 * (cb + 1),
                                      512 * oc:512 * (oc + 1)])

        for t in range(NTH):
            @ch.step("tensor")
            def _(te, sem=None):
                for cb in range(32):
                    mi = te.matmul_q(s_ps[t % 2][:, :],
                                     atsb[cb][:, 128 * t:128 * (t + 1)],
                                     wosb[cb][:, :],
                                     start=(cb == 0), stop=(cb == 31))
                te.mark(mi)

            @ch.step("vector")
            def _(v, sem=None):
                v.tensor_copy(sout[:, :], s_ps[t % 2][:, :])

            @ch.step("sync")
            def _(sync, sem=None):
                sync.dma_start(dr["s1_scr"][128 * t:128 * (t + 1),
                                            512 * oc:512 * (oc + 1)],
                               sout[:, :])


def emit_ffn1(nc, ch, st, dr):
    sb = lambda n, sh, dt: st.enter_context(nc.sbuf_tensor(_u(n), sh, dt))
    NV = 32
    xsb = [sb(f"f1_x{v}", [128, TH], BF16) for v in range(NV)]
    wc = sb("f1_wc", [128, D], BF16)
    atsb = sb("f1_at", [128, 48 * 32], BF16)
    bssb = sb("f1_bs", [48, D], BF16)
    bsf = sb("f1_bsf", [128, 32], F32)
    lsirep = sb("f1_lsi", [48, 4], I32)
    lsif = sb("f1_lsif", [48, 4], F32)
    brep = sb("f1_bv", [48, 1], F32)
    arsb = sb("f1_ar", [48, 1], F32)
    cnt = sb("f1_cnt", [48, 1], F32)
    seg = sb("f1_seg", [48, 1], F32)
    use = sb("f1_use", [48, 1], F32)
    aidx = sb("f1_aidx", [48, 1], F32)
    msk = sb("f1_msk", [48, 1], F32)
    rT = sb("f1_rT", [48, TH], BF16)
    gout = sb("f1_gout", [128, TH], BF16)
    r_ps = st.enter_context(nc.psum_tensor(_u("f1_rps"), [48, TH], F32))
    g_ps = st.enter_context(nc.psum_tensor(_u("f1_gps"), [128, TH], F32))

    @ch.step("sync")
    def _(sync, sem=None):
        for v in range(NV):
            sync.dma_start(xsb[v][:, :],
                           dr["xn1_scr"][TH:T, 128 * v:128 * (v + 1)],
                           transpose=True)
        sync.dma_start(lsirep[:, :], bass.AP(dr["lsi"], 0, [[0, 48], [1, 4]]))
        sync.dma_start(brep[:, :], bass.AP(dr["bval"], 0, [[0, 48], [1, 1]]))
        sync.dma_start(arsb[:, :], dr["arow"][:, :])
        sync.dma_start(bsf[:, :], dr["biasc1"][:, :])

    @ch.step("gpsimd")
    def _(gp, sem=None):
        for v in range(NV):
            gp.dma_start(ap(atsb, v * 48, (48 * 32, 128), (1, 48)),
                         dr["AT"][128 * v:128 * (v + 1), :])
        gp.dma_start(bssb[0:48, :], dr["BS"][:, :])

    @ch.step("vector")
    def _(v, sem=None):
        v.tensor_copy(lsif[0:48, :], lsirep[0:48, :])
        v.tensor_scalar(lsif[0:48, :], lsif[0:48, :], brep[0:48, 0:1], None,
                        ALU.is_le)
        v.tensor_reduce(cnt[0:48, :], lsif[0:48, :], mybir.AxisListType.X,
                        ALU.add)
        v.tensor_scalar(seg[0:48, :], cnt[0:48, :], -1.0, None, ALU.add)
        v.tensor_scalar(use[0:48, :], seg[0:48, :], 0.0, None, ALU.is_gt)
        v.tensor_scalar(aidx[0:48, :], seg[0:48, :], -1.0, 0.0, ALU.add,
                        ALU.max)
        v.tensor_scalar(aidx[0:48, :], aidx[0:48, :], 2.0, None, ALU.min)
        v.tensor_tensor(msk[0:48, :], aidx[0:48, :], arsb[0:48, :],
                        ALU.is_equal)
        v.tensor_tensor(msk[0:48, :], msk[0:48, :], use[0:48, :], ALU.mult)

    @ch.step("tensor")
    def _(te, sem=None):
        for tq in range(4):
            for v in range(NV):
                mi = te.matmul_q(r_ps[0:48, 512 * tq:512 * (tq + 1)],
                                 ap(atsb, v * 48, (48 * 32, 128), (1, 48)),
                                 xsb[v][:, 512 * tq:512 * (tq + 1)],
                                 start=(v == 0), stop=(v == NV - 1),
                                 skip_group_check=True)
        te.mark(mi)

    @ch.step("vector")
    def _(v, sem=None):
        v.tensor_scalar(rT[0:48, :], r_ps[0:48, :], msk[0:48, 0:1], None,
                        ALU.mult)

    for oc in range(32):
        @ch.step("gpsimd")
        def _(gp, sem=None):
            gp.dma_start(ap(wc, 0, (D, 128), (128, 32), (1, 128)),
                         ap(dr["WT"], oc * 128, (D, 128), (128 * D, 32),
                            (1, 128)))

        @ch.step("tensor")
        def _(te, sem=None):
            for tq in range(4):
                for g in range(NV):
                    te.matmul_q(g_ps[:, 512 * tq:512 * (tq + 1)],
                                wc[:, 128 * g:128 * (g + 1)],
                                xsb[g][:, 512 * tq:512 * (tq + 1)],
                                start=(g == 0), stop=False,
                                skip_group_check=True)
                mi = te.matmul_q(g_ps[:, 512 * tq:512 * (tq + 1)],
                                 bssb[0:48, 128 * oc:128 * (oc + 1)],
                                 rT[0:48, 512 * tq:512 * (tq + 1)],
                                 start=False, stop=True,
                                 skip_group_check=True)
            te.mark(mi)

        @ch.step("scalar")
        def _(s, sem=None):
            s.activation(gout[:, :], g_ps[:, :], AF.Silu,
                         bias=bsf[:, oc:oc + 1])

        @ch.step("sync")
        def _(sync, sem=None):
            sync.dma_start(dr["g1T_scr"][128 * oc:128 * (oc + 1), :],
                           gout[:, :])


def build_mega(slopes_np):
    nc = bass.Bass("TRN2", target_bir_lowering=False)
    dt_ = lambda n, sh, dty, kind="Internal": nc.dram_tensor(n, sh, dty,
                                                             kind=kind)
    dr = {}
    EI = "ExternalInput"
    dr["xb"] = dt_("xb", [T, D], F32, EI)
    dr["WinT"] = dt_("WinT", [D, MD], F32, EI)
    dr["binm"] = dt_("binm", [1, MD], F32, EI)
    dr["qw"] = dt_("qw", [512, D], I32, EI)
    dr["qz"] = dt_("qz", [32, 512], I32, EI)
    dr["scales"] = dt_("scales", [32, D], F16, EI)
    dr["biasc0"] = dt_("biasc0", [128, 32], F16, EI)
    dr["shamt"] = dt_("shamt", [128, 1], I32, EI)
    dr["cmask"] = dt_("cmask", [128, 8], F32, EI)
    for dd in range(4):
        for nm, sh in [(f"lnw{dd}", [1, MD]), (f"lnb{dd}", [1, MD]),
                       (f"scl{dd}", [1, MD]), (f"phzs{dd}", [128, 8]),
                       (f"phzrow{dd}", [2, MD]), (f"cinit{dd}", [128, 8]),
                       (f"cinj{dd}", [128, 8]), (f"ln2w{dd}", [1, MD]),
                       (f"ln2b{dd}", [1, MD]), (f"W1T{dd}", [MD, MF]),
                       (f"b1c{dd}", [128, 16]), (f"W2T{dd}", [MF, MD]),
                       (f"b2_{dd}", [1, MD])]:
            dr[nm] = dt_(nm, sh, F32, EI)
    dr["lnlw"] = dt_("lnlw", [1, MD], F32, EI)
    dr["lnlb"] = dt_("lnlb", [1, MD], F32, EI)
    dr["WoutT"] = dt_("WoutT", [MD, D], F32, EI)
    dr["bout"] = dt_("bout", [1, D], F32, EI)
    dr["Wq"] = dt_("Wq", [D, D], F32, EI)
    dr["Wkv"] = dt_("Wkv", [D, 256], F32, EI)
    dr["slopes"] = dt_("slopes", [1, 32], F32, EI)
    dr["pmask"] = dt_("pmask", [128, NT], F32, EI)
    dr["Wo"] = dt_("Wo", [D, D], F32, EI)
    dr["WT"] = dt_("WT", [D, D], F32, EI)
    dr["biasc1"] = dt_("biasc1", [128, 32], F32, EI)
    dr["AT"] = dt_("AT", [D, 48], F32, EI)
    dr["BS"] = dt_("BS", [48, D], F32, EI)
    dr["lsi"] = dt_("lsi", [1, 4], I32, EI)
    dr["bval"] = dt_("bval", [1, 1], F32, EI)
    dr["arow"] = dt_("arow", [48, 1], F32, EI)
    dr["out"] = dt_("out", [TH, D], F32, "ExternalOutput")
    dr["xn_scr"] = dt_("xn_scr", [T, D], BF16)
    dr["h_scr"] = dt_("h_scr", [T, MD], F32)
    dr["g0T_scr"] = dt_("g0T_scr", [D, T], BF16)
    dr["s0_scr"] = dt_("s0_scr", [T, D], BF16)
    dr["x1_scr"] = dt_("x1_scr", [T, D], F32)
    dr["xn1_scr"] = dt_("xn1_scr", [T, D], BF16)
    dr["qt_scr"] = dt_("qt_scr", [D, TH], BF16)
    dr["kvt_scr"] = dt_("kvt_scr", [256, T], BF16)
    dr["onum_scr"] = dt_("onum_scr", [D, TH], BF16)
    dr["oden_scr"] = dt_("oden_scr", [32, TH], F32)
    dr["s1_scr"] = dt_("s1_scr", [TH, D], BF16)
    dr["g1T_scr"] = dt_("g1T_scr", [D, TH], BF16)
    dr["lny_scr"] = dt_("lny_scr", [T, MD], BF16)
    dr["w4_scr"] = dt_("w4_scr", [D, D], BF16)
    dr["crow_scr"] = dt_("crow_scr", [2 * 16 * MD], F32)
    dr["prow_scr"] = dt_("prow_scr", [2, MD], F32)

    ch = Chain(nc)
    if True:
        with ExitStack() as st:
            emit_ln(nc, ch, st, dr["xb"], dr["xn_scr"], NT)
        with ExitStack() as st:
            emit_h0_gate0(nc, ch, st, dr)
        with ExitStack() as st:
            emit_mini_stack(nc, ch, st, dr)
        with ExitStack() as st:
            emit_combine(nc, ch, st, dr["xb"], dr["s0_scr"], dr["g0T_scr"],
                         dr["x1_scr"], NT, 0, 0, 0)
        with ExitStack() as st:
            emit_ln(nc, ch, st, dr["x1_scr"], dr["xn1_scr"], NT)
        with ExitStack() as st:
            emit_qkv(nc, ch, st, dr)
        with ExitStack() as st:
            emit_attn(nc, ch, st, dr, slopes_np)
        with ExitStack() as st:
            emit_wo(nc, ch, st, dr)
        with ExitStack() as st:
            emit_ffn1(nc, ch, st, dr)
        with ExitStack() as st:
            emit_combine(nc, ch, st, dr["x1_scr"], dr["s1_scr"],
                         dr["g1T_scr"], dr["out"], NTH, TH, 0, 0)
    ch.finish()
    return nc


_PROG = {}


def kernel(**inp):
    C = np.ascontiguousarray
    f32 = np.float32
    sl_np = alibi_slopes(32)
    if "mega" not in _PROG:
        _PROG["mega"] = build_mega(tuple(float(v) for v in sl_np))
    nc = _PROG["mega"]

    x = np.asarray(inp["L0_attn_x"], f32)
    scan_layout = lambda a: C(np.asarray(a, f32).reshape(4, 128).T)
    zeros8 = np.zeros((128, 8), f32)
    ones8 = np.ones((128, 8), f32)

    shamt = ((np.arange(128) % 8) * 4).astype(np.int32)[:, None]
    biasc0 = C(np.asarray(inp["L0_ffn_bias"]).reshape(32, 128).T)
    WinT = C(np.asarray(inp["L0_attn_Win"], f32).T)
    WoutT = C(np.asarray(inp["L0_attn_Wout"], f32).T)
    WT = C(np.asarray(inp["L1_ffn_weight"], f32).T)
    biasc1 = C(np.asarray(inp["L1_ffn_bias"], f32).reshape(32, 128).T)
    AT = C(np.asarray(inp["L1_ffn_lora_A"],
                      f32).transpose(2, 0, 1).reshape(D, 48))
    BS = C(np.asarray(inp["L1_ffn_lora_B"],
                      f32).transpose(0, 2, 1).reshape(48, D))
    arow = np.repeat(np.arange(3, dtype=f32), 16)[:, None]
    lsi = np.asarray(inp["L1_ffn_lora_start_indices"], np.int32)[None]

    in_maps = []
    for c in range(NCORE):
        b, hf = c // 2, c % 2
        if hf == 0:
            xb = np.concatenate([np.zeros((TH, D), f32),
                                 np.asarray(x[b, 0:TH], f32)], 0)
        else:
            xb = np.asarray(x[b], f32)
        m = {
            "xb": C(xb), "WinT": WinT,
            "binm": np.asarray(inp["L0_attn_bin"], f32)[None],
            "qw": np.asarray(inp["L0_ffn_qweight"]),
            "qz": np.asarray(inp["L0_ffn_qzeros"]),
            "scales": np.asarray(inp["L0_ffn_scales"]),
            "biasc0": biasc0, "shamt": shamt,
            "cmask": ones8 if hf == 1 else zeros8,
            "lnlw": np.asarray(inp["L0_attn_lnl_w"], f32)[None],
            "lnlb": np.asarray(inp["L0_attn_lnl_b"], f32)[None],
            "WoutT": WoutT,
            "bout": np.asarray(inp["L0_attn_bout"], f32)[None],
            "Wq": np.asarray(inp["L1_attn_Wq"], f32),
            "Wkv": np.asarray(inp["L1_attn_Wkv"], f32),
            "slopes": sl_np[None],
            "pmask": np.concatenate(
                [np.full((128, 16), NEG if hf == 0 else 0.0, f32),
                 np.zeros((128, 16), f32)], 1),
            "Wo": np.asarray(inp["L1_attn_Wo"], f32),
            "WT": WT, "biasc1": biasc1, "AT": AT, "BS": BS,
            "lsi": lsi, "bval": np.full((1, 1), float(b), f32),
            "arow": arow,
        }
        for dd in range(4):
            lc8 = np.concatenate([scan_layout(inp["L0_attn_lc_re"][dd]),
                                  scan_layout(inp["L0_attn_lc_im"][dd])], 1)
            m[f"lnw{dd}"] = np.asarray(inp["L0_attn_ln1_w"][dd], f32)[None]
            m[f"lnb{dd}"] = np.asarray(inp["L0_attn_ln1_b"][dd], f32)[None]
            m[f"scl{dd}"] = np.asarray(inp["L0_attn_sc_lin"][dd], f32)[None]
            m[f"phzs{dd}"] = np.concatenate(
                [scan_layout(inp["L0_attn_phz_re"][dd]),
                 scan_layout(inp["L0_attn_phz_im"][dd])], 1)
            m[f"phzrow{dd}"] = C(np.stack(
                [np.asarray(inp["L0_attn_phz_re"][dd], f32),
                 np.asarray(inp["L0_attn_phz_im"][dd], f32)]))
            m[f"cinit{dd}"] = lc8 if hf == 1 else zeros8.copy()
            m[f"cinj{dd}"] = lc8 if hf == 0 else zeros8.copy()
            m[f"ln2w{dd}"] = np.asarray(inp["L0_attn_ln2_w"][dd], f32)[None]
            m[f"ln2b{dd}"] = np.asarray(inp["L0_attn_ln2_b"][dd], f32)[None]
            m[f"W1T{dd}"] = C(np.asarray(inp["L0_attn_W1"][dd], f32).T)
            m[f"b1c{dd}"] = C(np.asarray(inp["L0_attn_b1"][dd],
                                         f32).reshape(16, 128).T)
            m[f"W2T{dd}"] = C(np.asarray(inp["L0_attn_W2"][dd], f32).T)
            m[f"b2_{dd}"] = np.asarray(inp["L0_attn_b2"][dd], f32)[None]
        in_maps.append(m)

    res = run_bass_kernel_spmd(nc, in_maps, core_ids=list(range(NCORE)))
    out = np.empty((4, 4096, D), f32)
    for c in range(NCORE):
        b, hf = c // 2, c % 2
        out[b, hf * TH:(hf + 1) * TH] = np.asarray(res.results[c]["out"], f32)
    return out


# version 09
# speedup vs baseline: 1.3350x; 1.3001x over previous
import math
from contextlib import ExitStack
import numpy as np
import ml_dtypes
import concourse.bass as bass
import concourse.mybir as mybir
from concourse.bass_utils import run_bass_kernel_spmd

F32 = mybir.dt.float32
BF16 = mybir.dt.bfloat16
F16 = mybir.dt.float16
I32 = mybir.dt.int32
AF = mybir.ActivationFunctionType
ALU = mybir.AluOpType
BF = ml_dtypes.bfloat16

NCORE = 8
T = 4096
TH = 2048
D = 4096
MD = 512
MF = 2048
NT = T // 128
NTH = TH // 128
EPS = 1e-5
ATT_SCALE = float(128 ** -0.5)
WTHRESH = 60.0
NEG = -1e30

_UN = [0]


def _u(n):
    _UN[0] += 1
    return f"{n}_{_UN[0]}"


def alibi_slopes(n_heads=32):
    cp2 = 2 ** math.floor(math.log2(n_heads))
    base = 2.0 ** (-(2.0 ** (-(math.log2(cp2) - 3))))
    slopes = [base ** k for k in range(1, cp2 + 1)]
    if cp2 != n_heads:
        eb = 2.0 ** (-(2.0 ** (-(math.log2(2 * cp2) - 3))))
        ne = min(cp2, n_heads - cp2)
        slopes += [eb ** k for k in range(1, 2 * ne + 1, 2)]
    return np.asarray(slopes, dtype=np.float32)


def ap(tsr, off, *dims):
    return bass.AP(tsr, off, [list(d) for d in dims])


def smax(*snaps):
    snaps = [s for s in snaps if s is not None]
    if not snaps:
        return None
    return tuple(max(v) for v in zip(*snaps))


_NOSYNC = {"wait_ge", "nop", "engine_nop", "register", "reg_mov", "print"}


class _Auto:

    def __init__(self, ch, eng):
        self._ch = ch
        self._eng = eng

    def matmul_q(self, *a, **k):
        return self._eng.matmul(*a, **k)

    def mark(self, inst):
        inst.then_inc(self._ch.csem, 1)
        self._ch.ct += 1
        return inst

    def __getattr__(self, name):
        target = getattr(self._eng, name)
        if name.startswith("_") or not callable(target) or name in _NOSYNC:
            return target

        def wrap(*a, **k):
            inst = target(*a, **k)
            if inst is None or not hasattr(inst, "then_inc"):
                return inst
            if name in ("dma_start", "indirect_dma_start"):
                from concourse.bass import BassGpSimd
                if isinstance(self._eng, BassGpSimd):
                    inst.then_inc(self._ch.gsem, 16)
                    self._ch.gt += 16
                else:
                    inst.then_inc(self._ch.dsem, 16)
                    self._ch.dt += 16
            else:
                inst.then_inc(self._ch.csem, 1)
                self._ch.ct += 1
                self._eng.wait_ge(self._ch.csem, self._ch.ct)
            return inst
        return wrap


class Chain:

    def __init__(self, nc):
        self.nc = nc
        self.block = nc.Block()
        self.block.__enter__()
        self.csem = nc.alloc_semaphore("ch_c")
        self.dsem = nc.alloc_semaphore("ch_d")
        self.gsem = nc.alloc_semaphore("ch_g")
        self.ct = 0
        self.dt = 0
        self.gt = 0
        self.last = {}

    def snap(self):
        return (self.ct, self.dt, self.gt)

    def step(self, engine, at=None):
        def deco(fn):
            if at is None:
                ct0, dt0, gt0 = self.ct, self.dt, self.gt
            else:
                ct0, dt0, gt0 = at

            def body(eng):
                if ct0 > 0 and self.last.get("c") != (engine, ct0):
                    eng.wait_ge(self.csem, ct0)
                if dt0 > 0 and self.last.get("d") != (engine, dt0):
                    eng.wait_ge(self.dsem, dt0)
                if gt0 > 0 and self.last.get("g") != (engine, gt0):
                    eng.wait_ge(self.gsem, gt0)
                fn(_Auto(self, eng), None)
            getattr(self.block, engine)(body)
            self.last["c"] = (engine, ct0)
            self.last["d"] = (engine, dt0)
            self.last["g"] = (engine, gt0)
            return fn
        return deco

    def finish(self):
        ct, dt, gt = self.ct, self.dt, self.gt

        def fin(g):
            g.wait_ge(self.csem, ct)
            g.wait_ge(self.dsem, dt)
            g.wait_ge(self.gsem, gt)
        self.block.gpsimd(fin)
        self.block.__exit__(None, None, None)


def emit_ln(nc, ch, st, x_dram, xn_dram, ntiles):
    sb = lambda n, sh, dt: st.enter_context(nc.sbuf_tensor(_u(n), sh, dt))
    NB = 3
    xt = [sb("ln_xt", [128, D], F32) for _i in range(NB)]
    ot = [sb("ln_ot", [128, D], BF16) for _i in range(NB)]
    scr = [sb("ln_scr", [128, D], F32) for _i in range(2)]
    sums = [sb("ln_sums", [128, 1], F32) for _i in range(2)]
    sumsq = [sb("ln_sumsq", [128, 1], F32) for _i in range(2)]
    nmu = [sb("ln_nmu", [128, 1], F32) for _i in range(2)]
    ve = [sb("ln_ve", [128, 1], F32) for _i in range(2)]
    rstd = [sb("ln_rstd", [128, 1], F32) for _i in range(2)]
    scr0 = [sb("ln_scr0", [128, 1], F32) for _i in range(2)]
    ld = {}
    stats = {}
    sq = {}
    aff = {}
    stq = {}
    for j in range(ntiles):
        b = j % NB
        p = j % 2

        @ch.step("sync", at=smax(aff.get(j - NB), stq.get(j - NB)))
        def _(sync, sem=None):
            sync.dma_start(xt[b][:, :], x_dram[128 * j:128 * (j + 1), :])
        ld[j] = ch.snap()

        @ch.step("vector", at=smax(ld[j], sq.get(j - 2)))
        def _(v, sem=None):
            v.tensor_scalar(scr[p][:, :], xt[b][:, :], 1.0, 0.0, ALU.mult,
                            ALU.add, accum_out=sums[p][:, :])
            v.scalar_tensor_tensor(scr[p][:, :], xt[b][:, :], 1.0, xt[b][:, :],
                                   ALU.mult, ALU.mult,
                                   accum_out=sumsq[p][:, :])
            v.tensor_scalar(nmu[p][:, :], sums[p][:, :], -1.0 / D, None,
                            ALU.mult)
            v.tensor_scalar(ve[p][:, :], sumsq[p][:, :], 1.0 / D, EPS,
                            ALU.mult, ALU.add)
            v.tensor_tensor(scr0[p][:, :], nmu[p][:, :], nmu[p][:, :],
                            ALU.mult)
            v.tensor_tensor(ve[p][:, :], ve[p][:, :], scr0[p][:, :],
                            ALU.subtract)
            v.reciprocal(ve[p][:, :], ve[p][:, :])
        stats[j] = ch.snap()

        @ch.step("scalar", at=stats[j])
        def _(s, sem=None):
            s.activation(rstd[p][:, :], ve[p][:, :], AF.Sqrt)
        sq[j] = ch.snap()

        @ch.step("vector", at=smax(sq[j], stq.get(j - NB)))
        def _(v, sem=None):
            v.tensor_scalar(ot[b][:, :], xt[b][:, :], nmu[p][:, 0:1],
                            rstd[p][:, 0:1], ALU.add, ALU.mult)
        aff[j] = ch.snap()

        @ch.step("gpsimd", at=aff[j])
        def _(g, sem=None):
            g.dma_start(xn_dram[128 * j:128 * (j + 1), :], ot[b][:, :])
        stq[j] = ch.snap()


def emit_wln_chunk(ch, hsb_j, lwrep, lbrep, out_bf, tmp, width=MD):
    scr, sums, sumsq, nmu, ve, rstd, scr0, lnt = tmp

    @ch.step("vector")
    def _(v, sem=None):
        v.tensor_scalar(scr[:, :], hsb_j[:, :], 1.0, 0.0, ALU.mult, ALU.add,
                        accum_out=sums[:, :])
        v.scalar_tensor_tensor(scr[:, :], hsb_j[:, :], 1.0, hsb_j[:, :],
                               ALU.mult, ALU.mult, accum_out=sumsq[:, :])
        v.tensor_scalar(nmu[:, :], sums[:, :], -1.0 / width, None, ALU.mult)
        v.tensor_scalar(ve[:, :], sumsq[:, :], 1.0 / width, EPS,
                        ALU.mult, ALU.add)
        v.tensor_tensor(scr0[:, :], nmu[:, :], nmu[:, :], ALU.mult)
        v.tensor_tensor(ve[:, :], ve[:, :], scr0[:, :], ALU.subtract)
        v.reciprocal(ve[:, :], ve[:, :])

    @ch.step("scalar")
    def _(s, sem=None):
        s.activation(rstd[:, :], ve[:, :], AF.Sqrt)

    @ch.step("vector")
    def _(v, sem=None):
        v.tensor_scalar(lnt[:, :], hsb_j[:, :], nmu[:, 0:1], rstd[:, 0:1],
                        ALU.add, ALU.mult)
        v.tensor_tensor(lnt[:, :], lnt[:, :], lwrep[:, :], ALU.mult)
        v.tensor_tensor(out_bf[:, :], lnt[:, :], lbrep[:, :], ALU.add)


def emit_h0_gate0(nc, ch, st, dr):
    sb = lambda n, sh, dt: st.enter_context(nc.sbuf_tensor(_u(n), sh, dt))
    NV = D // 128
    zsn = sb("g0_zsn", [32, D], BF16)
    with ExitStack() as dst:
        dsb = lambda n, sh, dt: dst.enter_context(
            nc.sbuf_tensor(_u(n), sh, dt))
        ssb = dsb("g0_ssb", [32, D], BF16)
        qwrep = [dsb("g0_qwrep", [128, D], I32) for _i in range(2)]
        srep = [dsb("g0_srep", [128, D], BF16) for _i in range(2)]
        w4g = [dsb("g0_w4g", [128, D], BF16) for _i in range(2)]
        sh_sb = dsb("g0_sh", [128, 1], I32)
        ztmp = dsb("g0_ztmp", [32, 512], I32)
        qzsb = dsb("g0_qz", [32, 512], I32)

        @ch.step("sync")
        def _(sync, sem=None):
            sync.dma_start(qzsb[:, :], dr["qz"][:, :])
            sync.dma_start(sh_sb[:, :], dr["shamt"][:, :])

        @ch.step("gpsimd")
        def _(gp, sem=None):
            gp.dma_start(ssb[0:32, :], dr["scales"][:, :])

        @ch.step("vector")
        def _(v, sem=None):
            for nib in range(8):
                v.tensor_scalar(ztmp[0:32, :], qzsb[0:32, :], 4 * nib, 15,
                                ALU.logical_shift_right, ALU.bitwise_and)
                v.tensor_copy(ap(zsn, nib, (D, 32), (8, 512)), ztmp[0:32, :])
            v.tensor_tensor(zsn[0:32, :], zsn[0:32, :], ssb[0:32, :], ALU.mult)
            v.tensor_scalar(zsn[0:32, :], zsn[0:32, :], -1.0, None, ALU.mult)

        dld = {}
        dcp = {}
        dst_ = {}
        for g in range(32):
            b = g % 2

            @ch.step("sync", at=dcp.get(g - 2))
            def _(sync, sem=None):
                sync.dma_start(qwrep[b][:, :],
                               bass.AP(dr["qw"], 16 * g * D,
                                       [[D, 16], [0, 8], [1, D]]))

            @ch.step("gpsimd", at=dcp.get(g - 2))
            def _(gp, sem=None):
                gp.dma_start(srep[b][:, :],
                             bass.AP(dr["scales"], g * D, [[0, 128], [1, D]]))
            dld[g] = ch.snap()

            @ch.step("vector", at=smax(dld[g], dst_.get(g - 2)))
            def _(v, sem=None):
                v.tensor_scalar(qwrep[b][:, :], qwrep[b][:, :], sh_sb[:, 0:1],
                                15, ALU.logical_shift_right, ALU.bitwise_and)
                v.tensor_copy(w4g[b][:, :], qwrep[b][:, :])
                v.tensor_tensor(w4g[b][:, :], w4g[b][:, :], srep[b][:, :],
                                ALU.mult)
            dcp[g] = ch.snap()

            @ch.step("gpsimd", at=dcp[g])
            def _(gp, sem=None):
                gp.dma_start(dr["w4_scr"][128 * g:128 * (g + 1), :],
                             w4g[b][:, :])
            dst_[g] = ch.snap()

    xsb = [sb(f"hg_x{v}", [128, TH], BF16) for v in range(NV)]
    winsb = sb("hg_win", [128, NV * MD], BF16)
    brep = sb("hg_bin", [128, MD], F32)
    b0rep = sb("hg_b0", [128, 32], F32)
    b0h = sb("hg_b0h", [128, 32], F16)
    hout = [sb("hg_hout", [128, MD], F32) for _i in range(2)]
    w4c = [sb("hg_w4c", [128, D], BF16) for _i in range(2)]
    gout = [sb("hg_gout", [128, TH], BF16) for _i in range(2)]
    xs_sb = sb("hg_xs", [32, TH], BF16)
    OHsb = sb("hg_OH", [128, 32 * 32], BF16)
    OH32_np = np.zeros((128, 32 * 32), dtype=BF)
    for g in range(32):
        OH32_np[:, g * 32 + g] = 1.0
    OH32_d = nc.inline_tensor(np.asarray(OH32_np), "hg_OH_d")

    @ch.step("sync")
    def _(sync, sem=None):
        sync.dma_start(OHsb[:, :], OH32_d[:, :])
        sync.dma_start(brep[:, :], bass.AP(dr["binm"], 0, [[0, 128], [1, MD]]))
        sync.dma_start(b0h[:, :], dr["biasc0"][:, :])

    @ch.step("gpsimd")
    def _(gp, sem=None):
        for v in range(NV):
            gp.dma_start(winsb[:, v * MD:(v + 1) * MD],
                         dr["WinT"][128 * v:128 * (v + 1), :])

    @ch.step("vector")
    def _(v, sem=None):
        v.tensor_copy(b0rep[:, :], b0h[:, :])

    for th in range(2):
        t0 = th * TH

        @ch.step("sync")
        def _(sync, sem=None):
            for v in range(NV):
                sync.dma_start(xsb[v][:, :],
                               dr["xn_scr"][t0:t0 + TH,
                                            128 * v:128 * (v + 1)],
                               transpose=True)

        with ExitStack() as hps_st:
            h_ps = [hps_st.enter_context(
                nc.psum_tensor(_u("hg_hps"), [128, MD], F32))
                for _i in range(2)]
            xmm = {}
            xcp = {}
            for tq in range(4):
                @ch.step("tensor", at=(None if tq < 2 else xcp.get(tq - 2))
                         if tq else None)
                def _(te, sem=None):
                    for g in range(NV):
                        mi = te.matmul_q(h_ps[tq % 2][0:32, :],
                                         OHsb[:, 32 * g:32 * (g + 1)],
                                         xsb[g][:, 512 * tq:512 * (tq + 1)],
                                         start=(g == 0), stop=(g == NV - 1),
                                         skip_group_check=True)
                    te.mark(mi)
                xmm[tq] = ch.snap()

                @ch.step("vector", at=xmm[tq])
                def _(v, sem=None):
                    v.tensor_copy(xs_sb[0:32, 512 * tq:512 * (tq + 1)],
                                  h_ps[tq % 2][0:32, :])
                xcp[tq] = ch.snap()

            hmm = {}
            hcp = {}
            hst = {}
            for t in range(NTH):
                @ch.step("tensor", at=smax(hcp.get(t - 2),
                                           xcp[3] if t == 0 else None))
                def _(te, sem=None):
                    for v in range(NV):
                        mi = te.matmul_q(h_ps[t % 2][:, :],
                                         xsb[v][:, 128 * t:128 * (t + 1)],
                                         winsb[:, v * MD:(v + 1) * MD],
                                         start=(v == 0), stop=(v == NV - 1))
                    te.mark(mi)
                hmm[t] = ch.snap()

                @ch.step("vector", at=smax(hmm[t], hst.get(t - 2)))
                def _(v, sem=None):
                    v.tensor_tensor(hout[t % 2][:, :], h_ps[t % 2][:, :],
                                    brep[:, :], ALU.add)
                hcp[t] = ch.snap()

                @ch.step("gpsimd", at=hcp[t])
                def _(g, sem=None):
                    g.dma_start(
                        dr["h_scr"][t0 + 128 * t:t0 + 128 * (t + 1), :],
                        hout[t % 2][:, :])
                hst[t] = ch.snap()

        with ExitStack() as gps_st:
            g_ps = [gps_st.enter_context(
                nc.psum_tensor(_u("hg_gps"), [128, TH], F32))
                for _i in range(2)]
            gld = {}
            gmm = {}
            gsl = {}
            gst_ = {}
            for oc in range(32):
                b = oc % 2

                @ch.step("sync", at=gmm.get(oc - 2))
                def _(sync, sem=None):
                    sync.dma_start(
                        ap(w4c[b], 0, (D, 128), (128, 32), (1, 128)),
                        ap(dr["w4_scr"], oc * 128, (D, 128), (128 * D, 32),
                           (1, 128)))
                gld[oc] = ch.snap()

                @ch.step("tensor", at=smax(gld[oc], gsl.get(oc - 2)))
                def _(te, sem=None):
                    for tq in range(4):
                        for g in range(NV):
                            te.matmul_q(g_ps[b][:, 512 * tq:512 * (tq + 1)],
                                        w4c[b][:, 128 * g:128 * (g + 1)],
                                        xsb[g][:, 512 * tq:512 * (tq + 1)],
                                        start=(g == 0), stop=False,
                                        skip_group_check=True)
                        mi = te.matmul_q(g_ps[b][:, 512 * tq:512 * (tq + 1)],
                                         zsn[0:32, 128 * oc:128 * (oc + 1)],
                                         xs_sb[0:32,
                                               512 * tq:512 * (tq + 1)],
                                         start=False, stop=True,
                                         skip_group_check=True)
                    te.mark(mi)
                gmm[oc] = ch.snap()

                @ch.step("scalar", at=smax(gmm[oc], gst_.get(oc - 2)))
                def _(s, sem=None):
                    s.activation(gout[b][:, :], g_ps[b][:, :], AF.Silu,
                                 bias=b0rep[:, oc:oc + 1])
                gsl[oc] = ch.snap()

                @ch.step("gpsimd", at=gsl[oc])
                def _(g, sem=None):
                    g.dma_start(dr["g0T_scr"][128 * oc:128 * (oc + 1),
                                              t0:t0 + TH], gout[b][:, :])
                gst_[oc] = ch.snap()


def _dft_consts_np():
    i = np.arange(128)
    f = np.arange(128)
    ang = 2 * np.pi * np.outer(i, f) / 256.0
    WfrT = np.cos(ang).astype(BF)
    WfiT = (-np.sin(ang)).astype(BF)
    alt = ((-1.0) ** i).astype(BF)[:, None]
    w = np.full(128, 2.0)
    w[0] = 1.0
    angl = 2 * np.pi * np.outer(f, np.arange(128)) / 256.0
    VcT = (w[:, None] * np.cos(angl) / 256.0).astype(BF)
    VsT = (-w[:, None] * np.sin(angl) / 256.0).astype(BF)
    altl = ((((-1.0) ** np.arange(128)) / 256.0).astype(BF))[None, :]
    ident = np.eye(128, dtype=np.float32)
    OH = np.zeros((128, 256), dtype=BF)
    for j in range(16):
        OH[:, j * 16 + j] = 1.0
    return WfrT, WfiT, alt, VcT, VsT, altl, ident, OH


def emit_mini_stack(nc, ch, st, dr):
    sb = lambda n, sh, dt: st.enter_context(nc.sbuf_tensor(_u(n), sh, dt))
    consts = _dft_consts_np()
    names = ["WfrT", "WfiT", "alt", "VcT", "VsT", "altl", "ident", "OH"]
    shapes = [[128, 128], [128, 128], [128, 1], [128, 128], [128, 128],
              [1, 128], [128, 128], [128, 256]]
    dts = [BF16, BF16, BF16, BF16, BF16, BF16, F32, BF16]
    csb = {}
    cds = {}
    for nm, shp, dty, arr in zip(names, shapes, dts, consts):
        csb[nm] = sb("m_" + nm, shp, dty)
        cds[nm] = nc.inline_tensor(np.asarray(arr), "m_d_" + nm)
    hsb = [sb(f"m_h{t}", [128, MD], F32) for t in range(NT)]
    cmask = sb("m_cmask", [128, 8], F32)

    @ch.step("sync")
    def _(sync, sem=None):
        for nm in names:
            sync.dma_start(csb[nm][:, :], cds[nm][:, :])
        sync.dma_start(cmask[:, :], dr["cmask"][:, :])
        for t in range(NT):
            sync.dma_start(hsb[t][:, :], dr["h_scr"][128 * t:128 * (t + 1), :])

    for dd in range(4):
        _emit_mini_layer(nc, ch, st, dr, dd, hsb, csb, cmask)

    with ExitStack() as fst:
        fsb = lambda n, sh, dt: fst.enter_context(
            nc.sbuf_tensor(_u(n), sh, dt))
        lwrep = fsb("sf_lw", [128, MD], F32)
        lbrep = fsb("sf_lb", [128, MD], F32)
        borep = fsb("sf_bo", [128, D], F32)
        lny = fsb("sf_lny", [128, MD], BF16)
        hnT = [fsb(f"sf_hnT{cb}", [128, T], BF16) for cb in range(4)]
        wsb = [fsb(f"sf_w{cb}", [128, D], BF16) for cb in range(4)]
        sout = fsb("sf_sout", [128, D], BF16)
        tmp = [fsb("sf_scr", [128, MD], F32), fsb("sf_su", [128, 1], F32),
               fsb("sf_sq", [128, 1], F32), fsb("sf_nm", [128, 1], F32),
               fsb("sf_ve", [128, 1], F32), fsb("sf_rs", [128, 1], F32),
               fsb("sf_s0", [128, 1], F32), fsb("sf_lnt", [128, MD], F32)]
        psum = fst.enter_context(nc.psum_tensor(_u("sf_ps"), [128, D], F32))

        @ch.step("sync")
        def _(sync, sem=None):
            sync.dma_start(lwrep[:, :],
                           bass.AP(dr["lnlw"], 0, [[0, 128], [1, MD]]))
            sync.dma_start(lbrep[:, :],
                           bass.AP(dr["lnlb"], 0, [[0, 128], [1, MD]]))
            sync.dma_start(borep[:, :],
                           bass.AP(dr["bout"], 0, [[0, 128], [1, D]]))

        @ch.step("gpsimd")
        def _(gp, sem=None):
            for cb in range(4):
                gp.dma_start(wsb[cb][:, :],
                             dr["WoutT"][128 * cb:128 * (cb + 1), :])

        for j in range(NT):
            emit_wln_chunk(ch, hsb[j], lwrep, lbrep, lny, tmp)

            @ch.step("sync")
            def _(sync, sem=None):
                sync.dma_start(dr["lny_scr"][128 * j:128 * (j + 1), :],
                               lny[:, :])

        @ch.step("sync")
        def _(sync, sem=None):
            for cb in range(4):
                sync.dma_start(hnT[cb][:, :],
                               dr["lny_scr"][:, 128 * cb:128 * (cb + 1)],
                               transpose=True)

        for t in range(NT):
            @ch.step("tensor")
            def _(te, sem=None):
                for oq in range(8):
                    for cb in range(4):
                        mi = te.matmul_q(psum[:, 512 * oq:512 * (oq + 1)],
                                         hnT[cb][:, 128 * t:128 * (t + 1)],
                                         wsb[cb][:, 512 * oq:512 * (oq + 1)],
                                         start=(cb == 0), stop=(cb == 3))
                te.mark(mi)

            @ch.step("vector")
            def _(v, sem=None):
                v.tensor_tensor(sout[:, :], psum[:, :], borep[:, :], ALU.add)

            @ch.step("sync")
            def _(sync, sem=None):
                sync.dma_start(dr["s0_scr"][128 * t:128 * (t + 1), :],
                               sout[:, :])


def _emit_mini_layer(nc, ch, st, dr, dd, hsb, cc, cmask):
    with ExitStack() as lst:
        sb = lambda n, sh, dt: lst.enter_context(nc.sbuf_tensor(_u(n), sh, dt))
        WfrT, WfiT, alt = cc["WfrT"], cc["WfiT"], cc["alt"]
        VcT, VsT, altl = cc["VcT"], cc["VsT"], cc["altl"]
        ident, OH = cc["ident"], cc["OH"]
        PIT = lambda t_: t_.shape[1]

        lnt = sb("ml_lnt", [128, MD], F32)
        sums = sb("ml_sums", [128, 1], F32)
        sumsq = sb("ml_sumsq", [128, 1], F32)
        nmu = sb("ml_nmu", [128, 1], F32)
        ve = sb("ml_ve", [128, 1], F32)
        rstd = sb("ml_rstd", [128, 1], F32)
        scr = sb("ml_scr", [128, MD], F32)
        scr0 = sb("ml_scr0", [128, 1], F32)
        pst = ExitStack()
        sv = lambda n, sh, dt: pst.enter_context(
            nc.sbuf_tensor(_u(n), sh, dt))
        lwrep = sv("ml_lw", [128, MD], F32)
        lbrep = sv("ml_lb", [128, MD], F32)
        sclrep = sv("ml_scl", [128, MD], F32)
        prep_r = sv("ml_pr", [128, MD], F32)
        prep_i = sv("ml_pi", [128, MD], F32)
        pz = sv("ml_pz", [128, 8], F32)
        psc = sv("ml_psc", [128, 8], F32)
        przr = sv("ml_przr", [1, MD], F32)
        przi = sv("ml_przi", [1, MD], F32)
        rowsq = sv("ml_rowsq", [1, MD], F32)
        rowtmp = sv("ml_rowtmp", [1, MD], F32)
        prowpr = sv("ml_prowpr", [1, MD], F32)
        prowpi = sv("ml_prowpi", [1, MD], F32)
        tr8 = sv("ml_tr8", [128, 8], F32)
        Pcm_r = sv("ml_Pcmr", [128, 4 * 132], F32)
        Pcm_i = sv("ml_Pcmi", [128, 4 * 132], F32)
        tmpA = sv("ml_tmpA", [128, 4 * 64], F32)
        tmpB = sv("ml_tmpB", [128, 4 * 64], F32)
        Pmc_r = sv("ml_Pmcr", [128, MD], F32)
        Pmc_i = sv("ml_Pmci", [128, MD], F32)
        Pmc_rbf = sv("ml_Pmcrb", [128, MD], BF16)
        Prev_r = sv("ml_Prevr", [128, MD], F32)
        Prev_i = sv("ml_Previ", [128, MD], F32)
        frrev = sv("ml_frrev", [128, MD], BF16)
        firev = sv("ml_firev", [128, MD], BF16)
        Frr = sv("ml_Frr", [128, MD], BF16)
        Fri = sv("ml_Fri", [128, MD], BF16)
        Frr128 = sv("ml_Frr128", [1, MD], BF16)
        Pp1r = sv("ml_Pp1r", [128, MD], BF16)
        Pp1i = sv("ml_Pp1i", [128, MD], BF16)
        tbig1 = sv("ml_tb1", [128, MD], F32)
        tbig2 = sv("ml_tb2", [128, MD], F32)
        P128 = sv("ml_P128", [128, 8], F32)
        P128p = [sv(f"ml_P128p{k}", [128, 8], F32) for k in range(4)]
        conv_sb = sv("ml_conv", [128, MD], F32)
        ysb = sv("ml_ysb", [128, MD], BF16)
        Yr_sb = sv("ml_Yr", [128, MD], BF16)
        Yi_sb = sv("ml_Yi", [128, MD], BF16)
        yr128 = sv("ml_yr128", [1, MD], BF16)
        Hr_sb = sv("ml_Hr", [128, MD], BF16)
        Hi_sb = sv("ml_Hi", [128, MD], BF16)
        Hr128 = sv("ml_Hr128", [1, MD], BF16)
        tre = sv("ml_tre", [128, MD], BF16)
        tim = sv("ml_tim", [128, MD], BF16)
        tA = sv("ml_tA", [128, MD], BF16)
        tB = sv("ml_tB", [128, MD], BF16)
        U_r = sv("ml_Ur", [128, 64], F32)
        U_i = sv("ml_Ui", [128, 64], F32)
        Urow_r = sv("ml_Urowr", [16, MD], F32)
        Urow_i = sv("ml_Urowi", [16, MD], F32)
        PPc_r = sv("ml_PPcr", [128, 64], F32)
        PPc_i = sv("ml_PPci", [128, 64], F32)
        ts1 = sv("ml_ts1", [128, 64], F32)
        ts2 = sv("ml_ts2", [128, 64], F32)
        ts3 = sv("ml_ts3", [128, 64], F32)
        ts4 = sv("ml_ts4", [128, 64], F32)
        csh_r = sv("ml_cshr", [128, 64], F32)
        csh_i = sv("ml_cshi", [128, 64], F32)
        crow_sb = sv("ml_crow", [16, 2 * MD], F32)
        ci_sb = sv("ml_ci", [128, 8], F32)
        ci2_sb = sv("ml_ci2", [128, 8], F32)
        cfin_sb = sv("ml_cfin", [128, 8], F32)
        cinj_sb = sv("ml_cinj", [128, 8], F32)
        crj_r = sv("ml_crjr", [128, MD], BF16)
        crj_i = sv("ml_crji", [128, MD], BF16)

        @ch.step("sync")
        def _(sync, sem=None):
            for dst_, src_ in [(lwrep, dr[f"lnw{dd}"]),
                               (lbrep, dr[f"lnb{dd}"]),
                               (sclrep, dr[f"scl{dd}"])]:
                sync.dma_start(dst_[:, :],
                               bass.AP(src_, 0, [[0, 128], [1, MD]]))
            sync.dma_start(pz[:, :], dr[f"phzs{dd}"][:, :])
            sync.dma_start(przr[:, :], dr[f"phzrow{dd}"][0:1, :])
            sync.dma_start(przi[:, :], dr[f"phzrow{dd}"][1:2, :])
            sync.dma_start(ci_sb[:, :], dr[f"cinit{dd}"][:, :])
            sync.dma_start(cinj_sb[:, :], dr[f"cinj{dd}"][:, :])

        psm = lambda n, shp: pst.enter_context(
            nc.psum_tensor(_u(n), shp, F32))
        Yr_ps = psm("ml_Yrps", [128, MD])
        Yi_ps = psm("ml_Yips", [128, MD])
        yrow_ps = psm("ml_yrps", [1, MD])
        conv_ps = psm("ml_cps", [128, MD])
        Ur_ps = psm("ml_Urps", [16, MD])
        Ui_ps = psm("ml_Uips", [16, MD])
        tr_ps = psm("ml_trps", [128, 128])

        @ch.step("vector")
        def _(v, sem=None):
            v.tensor_tensor(tr8[:, 0:4], pz[:, 0:4], pz[:, 0:4], ALU.mult)
            v.tensor_tensor(tr8[:, 4:8], pz[:, 4:8], pz[:, 4:8], ALU.mult)
            v.tensor_tensor(tr8[:, 0:4], tr8[:, 0:4], tr8[:, 4:8],
                            ALU.add)
            v.tensor_tensor(rowsq[:, :], przr[:, :], przr[:, :], ALU.mult)
            v.tensor_tensor(rowtmp[:, :], przi[:, :], przi[:, :],
                            ALU.mult)
            v.tensor_tensor(rowsq[:, :], rowsq[:, :], rowtmp[:, :],
                            ALU.add)

        @ch.step("scalar")
        def _(s, sem=None):
            s.activation(tr8[:, 4:8], tr8[:, 0:4], AF.Sqrt)
            s.activation(tr8[:, 0:4], tr8[:, 4:8], AF.Tanh)
            s.activation(rowsq[:, :], rowsq[:, :], AF.Sqrt)
            s.activation(rowtmp[:, :], rowsq[:, :], AF.Tanh)

        @ch.step("vector")
        def _(v, sem=None):
            v.reciprocal(tr8[:, 4:8], tr8[:, 4:8])
            v.tensor_tensor(tr8[:, 4:8], tr8[:, 4:8], tr8[:, 0:4],
                            ALU.mult)
            v.tensor_tensor(psc[:, 0:4], pz[:, 0:4], tr8[:, 4:8],
                            ALU.mult)
            v.tensor_tensor(psc[:, 4:8], pz[:, 4:8], tr8[:, 4:8],
                            ALU.mult)
            v.reciprocal(rowsq[:, :], rowsq[:, :])
            v.tensor_tensor(rowsq[:, :], rowsq[:, :], rowtmp[:, :],
                            ALU.mult)
            v.tensor_tensor(prowpr[:, :], przr[:, :], rowsq[:, :],
                            ALU.mult)
            v.tensor_tensor(prowpi[:, :], przi[:, :], rowsq[:, :],
                            ALU.mult)

        @ch.step("gpsimd")
        def _(g, sem=None):
            g.dma_start(dr["prow_scr"][0:1, :], prowpr[:, :])
            g.dma_start(dr["prow_scr"][1:2, :], prowpi[:, :])

        @ch.step("gpsimd")
        def _(g, sem=None):
            g.dma_start(prep_r[:, :],
                        bass.AP(dr["prow_scr"], 0, [[0, 128], [1, MD]]))
            g.dma_start(prep_i[:, :],
                        bass.AP(dr["prow_scr"], MD, [[0, 128], [1, MD]]))

        @ch.step("vector")
        def _(v, sem=None):
            for cb in range(4):
                o = cb * 132
                v.memset(Pcm_r[:, o:o + 1], 1.0)
                v.memset(Pcm_i[:, o:o + 1], 0.0)
                v.tensor_copy(Pcm_r[:, o + 1:o + 2], psc[:, cb:cb + 1])
                v.tensor_copy(Pcm_i[:, o + 1:o + 2],
                              psc[:, 4 + cb:5 + cb])
            w = 1
            for k in range(7):
                sl = lambda tsr, off, cnt: ap(tsr, off, (PIT(tsr), 128),
                                              (132, 4), (1, cnt))
                Ar, Ai = sl(Pcm_r, 1, w), sl(Pcm_i, 1, w)
                Br = ap(Pcm_r, w, (PIT(Pcm_r), 128), (132, 4), (0, w))
                Bi = ap(Pcm_i, w, (PIT(Pcm_i), 128), (132, 4), (0, w))
                Or, Oi = sl(Pcm_r, w + 1, w), sl(Pcm_i, w + 1, w)
                t1 = ap(tmpA, 0, (PIT(tmpA), 128), (64, 4), (1, w))
                t2 = ap(tmpB, 0, (PIT(tmpB), 128), (64, 4), (1, w))
                v.tensor_tensor(t1, Ar, Br, ALU.mult)
                v.tensor_tensor(t2, Ai, Bi, ALU.mult)
                v.tensor_tensor(Or, t1, t2, ALU.subtract)
                v.tensor_tensor(t1, Ar, Bi, ALU.mult)
                v.tensor_tensor(t2, Ai, Br, ALU.mult)
                v.tensor_tensor(Oi, t1, t2, ALU.add)
                w *= 2

        for cb in range(4):
            for ri in range(2):
                @ch.step("tensor")
                def _(te, sem=None):
                    src = [Pcm_r, Pcm_i][ri]
                    te.transpose(tr_ps[:, :],
                                 src[:, cb * 132:cb * 132 + 128],
                                 ident[:, :])

                @ch.step("vector")
                def _(v, sem=None):
                    dst = [Pmc_r, Pmc_i][ri]
                    v.tensor_copy(dst[:, cb * 128:(cb + 1) * 128],
                                  tr_ps[:, :])

        @ch.step("vector")
        def _(v, sem=None):
            v.tensor_copy(ap(Prev_r, 0, (PIT(Prev_r), 128), (128, 4),
                             (1, 128)),
                          ap(Pcm_r, 127, (PIT(Pcm_r), 128), (132, 4),
                             (-1, 128)))
            v.tensor_copy(ap(Prev_i, 0, (PIT(Prev_i), 128), (128, 4),
                             (1, 128)),
                          ap(Pcm_i, 127, (PIT(Pcm_i), 128), (132, 4),
                             (-1, 128)))

        for cb in range(4):
            for ri in range(2):
                @ch.step("tensor")
                def _(te, sem=None):
                    src = [Prev_r, Prev_i][ri]
                    te.transpose(tr_ps[:, :],
                                 src[:, cb * 128:(cb + 1) * 128],
                                 ident[:, :])

                @ch.step("vector")
                def _(v, sem=None):
                    dst = [frrev, firev][ri]
                    v.tensor_copy(dst[:, cb * 128:(cb + 1) * 128],
                                  tr_ps[:, :])

        @ch.step("vector")
        def _(v, sem=None):
            v.tensor_copy(Pmc_rbf[:, :], Pmc_r[:, :])

        @ch.step("tensor")
        def _(te, sem=None):
            te.matmul_q(Yr_ps[:, :], WfrT[:, :], Pmc_rbf[:, :],
                        start=True, stop=True)
            te.matmul_q(Yi_ps[:, :], WfiT[:, :], Pmc_rbf[:, :],
                        start=True, stop=True)
            te.matmul(yrow_ps[0:1, :], alt[:, :], Pmc_rbf[:, :],
                      start=True, stop=True)

        @ch.step("vector")
        def _(v, sem=None):
            v.tensor_tensor(Frr[:, :], Yr_ps[:, :], sclrep[:, :],
                            ALU.mult)
            v.tensor_tensor(Fri[:, :], Yi_ps[:, :], sclrep[:, :],
                            ALU.mult)
            v.tensor_tensor(Frr128[0:1, :], yrow_ps[0:1, :],
                            sclrep[0:1, :], ALU.mult)

        @ch.step("vector")
        def _(v, sem=None):
            v.tensor_tensor(tbig1[:, :], Pmc_r[:, :], prep_r[:, :],
                            ALU.mult)
            v.tensor_tensor(tbig2[:, :], Pmc_i[:, :], prep_i[:, :],
                            ALU.mult)
            v.tensor_tensor(tbig1[:, :], tbig1[:, :], tbig2[:, :],
                            ALU.subtract)
            v.tensor_tensor(scr[:, :], Pmc_r[:, :], prep_i[:, :],
                            ALU.mult)
            v.tensor_tensor(tbig2[:, :], Pmc_i[:, :], prep_r[:, :],
                            ALU.mult)
            v.tensor_tensor(tbig2[:, :], scr[:, :], tbig2[:, :], ALU.add)
            v.tensor_tensor(Pp1r[:, :], tbig1[:, :], sclrep[:, :],
                            ALU.mult)
            v.tensor_tensor(Pp1i[:, :], tbig2[:, :], sclrep[:, :],
                            ALU.mult)
            v.tensor_copy(P128[:, 0:4],
                          ap(Pcm_r, 128, (PIT(Pcm_r), 128), (132, 4)))
            v.tensor_copy(P128[:, 4:8],
                          ap(Pcm_i, 128, (PIT(Pcm_i), 128), (132, 4)))
            v.tensor_copy(P128p[0][:, :], P128[:, :])
            for k in range(1, 4):
                a = P128p[k - 1]
                o = P128p[k]
                v.tensor_tensor(tr8[:, 0:4], a[:, 0:4], a[:, 0:4],
                                ALU.mult)
                v.tensor_tensor(tr8[:, 4:8], a[:, 4:8], a[:, 4:8],
                                ALU.mult)
                v.tensor_tensor(o[:, 0:4], tr8[:, 0:4], tr8[:, 4:8],
                                ALU.subtract)
                v.tensor_tensor(tr8[:, 0:4], a[:, 0:4], a[:, 4:8],
                                ALU.mult)
                v.tensor_tensor(o[:, 4:8], tr8[:, 0:4], tr8[:, 0:4],
                                ALU.add)

        for seg in range(2):
            j0 = seg * 16
            cinit = ci_sb if seg == 0 else ci2_sb
            for jl in range(16):
                j = j0 + jl

                @ch.step("vector")
                def _(v, sem=None):
                    v.tensor_scalar(scr[:, :], hsb[j][:, :], 1.0, 0.0,
                                    ALU.mult, ALU.add,
                                    accum_out=sums[:, :])
                    v.scalar_tensor_tensor(scr[:, :], hsb[j][:, :], 1.0,
                                           hsb[j][:, :], ALU.mult,
                                           ALU.mult,
                                           accum_out=sumsq[:, :])
                    v.tensor_scalar(nmu[:, :], sums[:, :], -1.0 / MD,
                                    None, ALU.mult)
                    v.tensor_scalar(ve[:, :], sumsq[:, :], 1.0 / MD, EPS,
                                    ALU.mult, ALU.add)
                    v.tensor_tensor(scr0[:, :], nmu[:, :], nmu[:, :],
                                    ALU.mult)
                    v.tensor_tensor(ve[:, :], ve[:, :], scr0[:, :],
                                    ALU.subtract)
                    v.reciprocal(ve[:, :], ve[:, :])

                @ch.step("scalar")
                def _(s, sem=None):
                    s.activation(rstd[:, :], ve[:, :], AF.Sqrt)

                @ch.step("vector")
                def _(v, sem=None):
                    v.tensor_scalar(lnt[:, :], hsb[j][:, :], nmu[:, 0:1],
                                    rstd[:, 0:1], ALU.add, ALU.mult)
                    v.tensor_tensor(lnt[:, :], lnt[:, :], lwrep[:, :],
                                    ALU.mult)
                    v.tensor_tensor(lnt[:, :], lnt[:, :], lbrep[:, :],
                                    ALU.add)

                @ch.step("scalar")
                def _(s, sem=None):
                    s.activation(ysb[:, :], lnt[:, :], AF.Silu)

                @ch.step("tensor")
                def _(te, sem=None):
                    te.matmul_q(Yr_ps[:, :], WfrT[:, :], ysb[:, :],
                                start=True, stop=True)
                    te.matmul_q(Yi_ps[:, :], WfiT[:, :], ysb[:, :],
                                start=True, stop=True)
                    te.matmul(yrow_ps[0:1, :], alt[:, :], ysb[:, :],
                              start=True, stop=True)

                @ch.step("scalar")
                def _(s, sem=None):
                    s.copy(Yr_sb[:, :], Yr_ps[:, :])
                    s.copy(Yi_sb[:, :], Yi_ps[:, :])
                    s.copy(yr128[0:1, :], yrow_ps[0:1, :])

                @ch.step("vector")
                def _(v, sem=None):
                    v.tensor_tensor(tA[:, :], Yr_sb[:, :], Frr[:, :],
                                    ALU.mult)
                    v.tensor_tensor(tB[:, :], Yi_sb[:, :], Fri[:, :],
                                    ALU.mult)
                    v.tensor_tensor(Hr_sb[:, :], tA[:, :], tB[:, :],
                                    ALU.subtract)
                    v.tensor_tensor(tA[:, :], Yr_sb[:, :], Fri[:, :],
                                    ALU.mult)
                    v.tensor_tensor(tB[:, :], Yi_sb[:, :], Frr[:, :],
                                    ALU.mult)
                    v.tensor_tensor(Hi_sb[:, :], tA[:, :], tB[:, :],
                                    ALU.add)
                    v.tensor_tensor(Hr128[0:1, :], yr128[0:1, :],
                                    Frr128[0:1, :], ALU.mult)
                    v.tensor_tensor(tre[:, :], ysb[:, :], frrev[:, :],
                                    ALU.mult)
                    v.tensor_tensor(tim[:, :], ysb[:, :], firev[:, :],
                                    ALU.mult)

                @ch.step("tensor")
                def _(te, sem=None):
                    te.matmul_q(Ur_ps[0:16, :],
                                OH[:, jl * 16:(jl + 1) * 16],
                                tre[:, :], start=(jl == 0),
                                stop=(jl == 15), skip_group_check=True)
                    te.matmul_q(Ui_ps[0:16, :],
                                OH[:, jl * 16:(jl + 1) * 16],
                                tim[:, :], start=(jl == 0),
                                stop=(jl == 15), skip_group_check=True)
                    te.matmul_q(conv_ps[:, :], VcT[:, :], Hr_sb[:, :],
                                start=True, stop=False,
                                skip_group_check=True)
                    te.matmul_q(conv_ps[:, :], VsT[:, :], Hi_sb[:, :],
                                start=False, stop=False,
                                skip_group_check=True)
                    te.matmul(conv_ps[:, :], altl[0:1, :], Hr128[0:1, :],
                              start=False, stop=True,
                              skip_group_check=True)

                @ch.step("scalar")
                def _(s, sem=None):
                    s.copy(conv_sb[:, :], conv_ps[:, :])

                @ch.step("vector")
                def _(v, sem=None):
                    v.tensor_tensor(hsb[j][:, :], hsb[j][:, :],
                                    conv_sb[:, :], ALU.add)

            @ch.step("vector")
            def _(v, sem=None):
                v.tensor_copy(Urow_r[:, :], Ur_ps[:, :])
                v.tensor_copy(Urow_i[:, :], Ui_ps[:, :])

            for cb in range(4):
                for ri in range(2):
                    @ch.step("tensor")
                    def _(te, sem=None):
                        src = [Urow_r, Urow_i][ri]
                        te.transpose(tr_ps[0:128, 0:16],
                                     src[0:16, cb * 128:(cb + 1) * 128],
                                     ident[0:16, 0:16])

                    @ch.step("vector")
                    def _(v, sem=None):
                        dst = [U_r, U_i][ri]
                        v.tensor_copy(dst[:, cb * 16:(cb + 1) * 16],
                                      tr_ps[0:128, 0:16])

            @ch.step("vector")
            def _(v, sem=None):
                w = 1
                for k in range(4):
                    n = 16 - w
                    sl = lambda tsr, off: ap(tsr, off, (PIT(tsr), 128),
                                             (16, 4), (1, n))
                    zsr, zsi = sl(U_r, 0), sl(U_i, 0)
                    tgr, tgi = sl(U_r, w), sl(U_i, w)
                    Pr = ap(P128p[k], 0, (8, 128), (1, 4), (0, n))
                    Pi = ap(P128p[k], 4, (8, 128), (1, 4), (0, n))
                    t1 = ap(ts1, 0, (PIT(ts1), 128), (16, 4), (1, n))
                    t2 = ap(ts2, 0, (PIT(ts2), 128), (16, 4), (1, n))
                    t3 = ap(ts3, 0, (PIT(ts3), 128), (16, 4), (1, n))
                    t4 = ap(ts4, 0, (PIT(ts4), 128), (16, 4), (1, n))
                    v.tensor_tensor(t1, zsr, Pr, ALU.mult)
                    v.tensor_tensor(t2, zsi, Pi, ALU.mult)
                    v.tensor_tensor(t3, zsr, Pi, ALU.mult)
                    v.tensor_tensor(t4, zsi, Pr, ALU.mult)
                    v.tensor_tensor(t1, t1, t2, ALU.subtract)
                    v.tensor_tensor(t3, t3, t4, ALU.add)
                    v.tensor_tensor(tgr, tgr, t1, ALU.add)
                    v.tensor_tensor(tgi, tgi, t3, ALU.add)
                    w *= 2
                v.tensor_copy(ap(PPc_r, 0, (PIT(PPc_r), 128), (16, 4)),
                              P128[:, 0:4])
                v.tensor_copy(ap(PPc_i, 0, (PIT(PPc_i), 128), (16, 4)),
                              P128[:, 4:8])
                w = 1
                for k in range(4):
                    n = min(2 * w, 16) - w
                    sl2 = lambda tsr, off: ap(tsr, off, (PIT(tsr), 128),
                                              (16, 4), (1, n))
                    ar, ai = sl2(PPc_r, 0), sl2(PPc_i, 0)
                    tgr, tgi = sl2(PPc_r, w), sl2(PPc_i, w)
                    Pr = ap(P128p[k], 0, (8, 128), (1, 4), (0, n))
                    Pi = ap(P128p[k], 4, (8, 128), (1, 4), (0, n))
                    t1 = ap(ts1, 0, (PIT(ts1), 128), (16, 4), (1, n))
                    t2 = ap(ts2, 0, (PIT(ts2), 128), (16, 4), (1, n))
                    v.tensor_tensor(t1, ar, Pr, ALU.mult)
                    v.tensor_tensor(t2, ai, Pi, ALU.mult)
                    v.tensor_tensor(tgr, t1, t2, ALU.subtract)
                    v.tensor_tensor(t1, ar, Pi, ALU.mult)
                    v.tensor_tensor(t2, ai, Pr, ALU.mult)
                    v.tensor_tensor(tgi, t1, t2, ALU.add)
                    w *= 2
                cir = ap(cinit, 0, (8, 128), (1, 4), (0, 16))
                cii = ap(cinit, 4, (8, 128), (1, 4), (0, 16))
                sl3 = lambda tsr: ap(tsr, 0, (PIT(tsr), 128), (16, 4),
                                     (1, 16))
                v.tensor_tensor(sl3(ts1), sl3(PPc_r), cir, ALU.mult)
                v.tensor_tensor(sl3(ts2), sl3(PPc_i), cii, ALU.mult)
                v.tensor_tensor(sl3(ts1), sl3(ts1), sl3(ts2),
                                ALU.subtract)
                v.tensor_tensor(sl3(U_r), sl3(U_r), sl3(ts1), ALU.add)
                v.tensor_tensor(sl3(ts1), sl3(PPc_r), cii, ALU.mult)
                v.tensor_tensor(sl3(ts2), sl3(PPc_i), cir, ALU.mult)
                v.tensor_tensor(sl3(ts1), sl3(ts1), sl3(ts2), ALU.add)
                v.tensor_tensor(sl3(U_i), sl3(U_i), sl3(ts1), ALU.add)
                v.tensor_copy(cfin_sb[:, 0:4],
                              ap(U_r, 15, (PIT(U_r), 128), (16, 4)))
                v.tensor_copy(cfin_sb[:, 4:8],
                              ap(U_i, 15, (PIT(U_i), 128), (16, 4)))
                v.tensor_copy(ap(csh_r, 0, (PIT(csh_r), 128), (16, 4)),
                              cinit[:, 0:4])
                v.tensor_copy(ap(csh_i, 0, (PIT(csh_i), 128), (16, 4)),
                              cinit[:, 4:8])
                v.tensor_copy(ap(csh_r, 1, (PIT(csh_r), 128), (16, 4),
                                 (1, 15)),
                              ap(U_r, 0, (PIT(U_r), 128), (16, 4),
                                 (1, 15)))
                v.tensor_copy(ap(csh_i, 1, (PIT(csh_i), 128), (16, 4),
                                 (1, 15)),
                              ap(U_i, 0, (PIT(U_i), 128), (16, 4),
                                 (1, 15)))
                if seg == 0:
                    v.tensor_tensor(ci2_sb[:, :], cfin_sb[:, :],
                                    cmask[:, :], ALU.mult)
                    v.tensor_tensor(ci2_sb[:, :], ci2_sb[:, :],
                                    cinj_sb[:, :], ALU.add)

            for cb in range(4):
                for ri in range(2):
                    @ch.step("tensor")
                    def _(te, sem=None):
                        src = [csh_r, csh_i][ri]
                        te.transpose(tr_ps[0:16, 0:128],
                                     src[:, cb * 16:(cb + 1) * 16],
                                     ident[:, :])

                    @ch.step("vector")
                    def _(v, sem=None):
                        v.tensor_copy(
                            crow_sb[0:16, ri * MD + cb * 128:
                                    ri * MD + (cb + 1) * 128],
                            tr_ps[0:16, 0:128])

            @ch.step("sync")
            def _(sync, sem=None):
                sync.dma_start(
                    ap(dr["crow_scr"], 0, (512, 16), (8192, 2), (1, 512)),
                    ap(crow_sb, 0, (PIT(crow_sb), 16), (MD, 2), (1, MD)))

            for jl in range(16):
                j = j0 + jl

                @ch.step("gpsimd")
                def _(g, sem=None):
                    g.dma_start(crj_r[:, :],
                                bass.AP(dr["crow_scr"], jl * MD,
                                        [[0, 128], [1, MD]]))
                    g.dma_start(crj_i[:, :],
                                bass.AP(dr["crow_scr"], 8192 + jl * MD,
                                        [[0, 128], [1, MD]]))

                @ch.step("vector")
                def _(v, sem=None):
                    v.tensor_tensor(tbig1[:, :], crj_r[:, :], Pp1r[:, :],
                                    ALU.mult)
                    v.tensor_tensor(tbig2[:, :], crj_i[:, :], Pp1i[:, :],
                                    ALU.mult)
                    v.tensor_tensor(tbig1[:, :], tbig1[:, :], tbig2[:, :],
                                    ALU.subtract)
                    v.tensor_tensor(hsb[j][:, :], hsb[j][:, :],
                                    tbig1[:, :], ALU.add)

        pst.close()

        lw2 = sb("ml_lw2", [128, MD], F32)
        lb2 = sb("ml_lb2", [128, MD], F32)
        b2rep = sb("ml_b2", [128, MD], F32)
        b1sb = sb("ml_b1", [128, 16], F32)
        lny = sb("ml_lny", [128, MD], BF16)
        lnyT = [sb(f"ml_lnyT{cb}", [128, TH], BF16) for cb in range(4)]
        w1sb = [sb(f"ml_w1{cb}", [128, MF], BF16) for cb in range(4)]
        w2sb = [sb(f"ml_w2{f}", [128, MD], BF16) for f in range(16)]
        y2sb = [sb(f"ml_y2{f}", [128, 512], BF16) for f in range(16)]
        ffn_ps = [lst.enter_context(nc.psum_tensor(_u("ml_fps"), [128, 512],
                                                   F32)) for _i in range(2)]
        ffh_ps = [lst.enter_context(nc.psum_tensor(_u("ml_fhps"), [128, MD],
                                                   F32)) for _i in range(2)]
        tmp = (scr, sums, sumsq, nmu, ve, rstd, scr0, lnt)

        @ch.step("sync")
        def _(sync, sem=None):
            for dst_, src_ in [(lw2, dr[f"ln2w{dd}"]), (lb2, dr[f"ln2b{dd}"]),
                               (b2rep, dr[f"b2_{dd}"])]:
                sync.dma_start(dst_[:, :],
                               bass.AP(src_, 0, [[0, 128], [1, MD]]))
            sync.dma_start(b1sb[:, :], dr[f"b1c{dd}"][:, :])

        @ch.step("gpsimd")
        def _(g, sem=None):
            for cb in range(4):
                g.dma_start(w1sb[cb][:, :],
                            dr[f"W1T{dd}"][128 * cb:128 * (cb + 1), :])
            for f in range(16):
                g.dma_start(w2sb[f][:, :],
                            dr[f"W2T{dd}"][128 * f:128 * (f + 1), :])

        for j in range(NT):
            emit_wln_chunk(ch, hsb[j], lw2, lb2, lny, tmp)

            @ch.step("sync")
            def _(sync, sem=None):
                sync.dma_start(dr["lny_scr"][128 * j:128 * (j + 1), :],
                               lny[:, :])

        for th2 in range(2):
            t0 = th2 * TH

            @ch.step("sync")
            def _(sync, sem=None):
                for cb in range(4):
                    sync.dma_start(lnyT[cb][:, :],
                                   dr["lny_scr"][t0:t0 + TH,
                                                 128 * cb:128 * (cb + 1)],
                                   transpose=True)

            for tb in range(4):
                c0 = tb * 512
                for f in range(16):
                    @ch.step("tensor")
                    def _(te, sem=None):
                        for cb in range(4):
                            mi = te.matmul_q(
                                ffn_ps[f % 2][:, :],
                                w1sb[cb][:, 128 * f:128 * (f + 1)],
                                lnyT[cb][:, c0:c0 + 512],
                                start=(cb == 0), stop=(cb == 3))
                        te.mark(mi)

                    @ch.step("scalar")
                    def _(s, sem=None):
                        s.activation(y2sb[f][:, :], ffn_ps[f % 2][:, :],
                                     AF.Silu, bias=b1sb[:, f:f + 1])

                for tt in range(4):
                    j = th2 * 16 + tb * 4 + tt

                    @ch.step("tensor")
                    def _(te, sem=None):
                        for f in range(16):
                            mi = te.matmul_q(
                                ffh_ps[tt % 2][:, :],
                                y2sb[f][:, 128 * tt:128 * (tt + 1)],
                                w2sb[f][:, :],
                                start=(f == 0), stop=(f == 15))
                        te.mark(mi)

                    @ch.step("vector")
                    def _(v, sem=None):
                        v.tensor_tensor(scr[:, :], ffh_ps[tt % 2][:, :],
                                        b2rep[:, :], ALU.add)
                        v.tensor_tensor(hsb[j][:, :], hsb[j][:, :], scr[:, :],
                                        ALU.add)


def emit_combine(nc, ch, st, xsrc, s_scr, gT_scr, xdst, ntiles,
                 xrow0, srow0, gcol0):
    sb = lambda n, sh, dt: st.enter_context(nc.sbuf_tensor(_u(n), sh, dt))
    NB = 3
    xt = [sb("cb_x", [128, D], F32) for _i in range(NB)]
    st_ = [sb("cb_s", [128, D], BF16) for _i in range(NB)]
    gt = [sb("cb_g", [128, D], BF16) for _i in range(NB)]
    pt = sb("cb_p", [128, D], F32)
    ld = {}
    cmp_ = {}
    stq = {}
    for j in range(ntiles):
        b = j % NB

        @ch.step("sync", at=smax(cmp_.get(j - NB), stq.get(j - NB)))
        def _(sync, sem=None):
            sync.dma_start(gt[b][:, :],
                           gT_scr[:, gcol0 + 128 * j:gcol0 + 128 * (j + 1)],
                           transpose=True)
            sync.dma_start(st_[b][:, :],
                           s_scr[srow0 + 128 * j:srow0 + 128 * (j + 1), :])

        @ch.step("gpsimd", at=smax(cmp_.get(j - NB), stq.get(j - NB)))
        def _(g, sem=None):
            g.dma_start(xt[b][:, :],
                        xsrc[xrow0 + 128 * j:xrow0 + 128 * (j + 1), :])
        ld[j] = ch.snap()

        @ch.step("vector", at=ld[j])
        def _(v, sem=None):
            v.tensor_tensor(pt[:, :], st_[b][:, :], gt[b][:, :], ALU.mult)
            v.tensor_tensor(xt[b][:, :], xt[b][:, :], pt[:, :], ALU.add)
        cmp_[j] = ch.snap()

        @ch.step("gpsimd", at=cmp_[j])
        def _(g, sem=None):
            g.dma_start(xdst[128 * j:128 * (j + 1), :], xt[b][:, :])
        stq[j] = ch.snap()


def emit_qkv(nc, ch, st, dr):
    sb = lambda n, sh, dt: st.enter_context(nc.sbuf_tensor(_u(n), sh, dt))
    NV = 32
    xsb = [sb(f"qk_x{v}", [128, TH], BF16) for v in range(NV)]
    wc = sb("qk_wc", [128, D], BF16)
    qout = sb("qk_qout", [128, TH], BF16)
    q_ps = st.enter_context(nc.psum_tensor(_u("qk_ps"), [128, TH], F32))
    for th in range(2):
        t0 = th * TH

        @ch.step("sync")
        def _(sync, sem=None):
            for v in range(NV):
                sync.dma_start(xsb[v][:, :],
                               dr["xn1_scr"][t0:t0 + TH,
                                             128 * v:128 * (v + 1)],
                               transpose=True)

        ocs = [32, 33] if th == 0 else [32, 33] + list(range(32))
        for oc in ocs:
            @ch.step("gpsimd")
            def _(gp, sem=None):
                if oc < 32:
                    gp.dma_start(ap(wc, 0, (D, 128), (128, 32), (1, 128)),
                                 ap(dr["Wq"], oc * 128, (D, 128),
                                    (128 * D, 32), (1, 128)))
                else:
                    o2 = oc - 32
                    gp.dma_start(ap(wc, 0, (D, 128), (128, 32), (1, 128)),
                                 ap(dr["Wkv"], o2 * 128, (256, 128),
                                    (128 * 256, 32), (1, 128)))

            @ch.step("tensor")
            def _(te, sem=None):
                for tq in range(4):
                    for g in range(NV):
                        mi = te.matmul_q(q_ps[:, 512 * tq:512 * (tq + 1)],
                                         wc[:, 128 * g:128 * (g + 1)],
                                         xsb[g][:, 512 * tq:512 * (tq + 1)],
                                         start=(g == 0), stop=(g == NV - 1),
                                         skip_group_check=True)
                te.mark(mi)

            @ch.step("vector")
            def _(v, sem=None):
                v.tensor_copy(qout[:, :], q_ps[:, :])

            @ch.step("sync")
            def _(sync, sem=None):
                if oc < 32:
                    sync.dma_start(dr["qt_scr"][128 * oc:128 * (oc + 1), :],
                                   qout[:, :])
                else:
                    o2 = oc - 32
                    sync.dma_start(dr["kvt_scr"][128 * o2:128 * (o2 + 1),
                                                 t0:t0 + TH], qout[:, :])


def emit_attn(nc, ch, st, dr, slopes_np):
    sb = lambda n, sh, dt: st.enter_context(nc.sbuf_tensor(_u(n), sh, dt))
    heads = 32
    qsb = [sb(f"at_q{h}", [128, TH], BF16) for h in range(heads)]
    ksb = sb("at_k", [128, T], BF16)
    vtk = [sb(f"at_v{kt}", [128, 128], BF16) for kt in range(NT)]
    slrep = sb("at_sl", [128, heads], F32)
    pmrep = sb("at_pm", [128, NT], F32)
    iota = sb("at_iota", [128, 1], F32)
    onescol = sb("at_ones", [128, 1], BF16)
    bias_v = sb("at_bias", [128, 16], F32)
    probs = [sb(f"at_probs{i}", [128, 512], BF16) for i in range(2)]
    msc = [sb(f"at_msc{i}", [128, 512], F32) for i in range(2)]
    maskfull = [sb(f"at_mf{i}", [128, 512], F32) for i in range(4)]
    onum_bf = [sb(f"at_onum{i}", [128, 512], BF16) for i in range(2)]
    den_sb = [sb(f"at_den{i}", [1, 512], F32) for i in range(2)]
    sc_ps = [st.enter_context(nc.psum_tensor(_u("at_scps"), [128, 512], F32))
             for _i in range(2)]
    av_ps = [st.enter_context(nc.psum_tensor(_u("at_avps"), [128, 512], F32))
             for _i in range(2)]
    den_ps = [st.enter_context(nc.psum_tensor(_u("at_dps"), [1, 512], F32))
              for _i in range(2)]

    iota_d = nc.inline_tensor(np.arange(128, dtype=np.float32)[:, None],
                              "at_iota_d")
    ones_d = nc.inline_tensor(np.asarray(np.ones((128, 1), dtype=BF)),
                              "at_ones_d")
    mf_np = []
    for d0 in (0, 128, 256, 384):
        p = np.arange(128)[:, None]
        c = np.arange(512)[None, :]
        keep = (c >= d0 + 128) | ((c >= d0) & (p <= c - d0))
        mf_np.append(np.where(keep, 0.0, NEG).astype(np.float32))
    mf_d = [nc.inline_tensor(m, f"at_mf_d{i}") for i, m in enumerate(mf_np)]

    @ch.step("sync")
    def _(sync, sem=None):
        for h in range(heads):
            sync.dma_start(qsb[h][:, :],
                           dr["qt_scr"][128 * h:128 * (h + 1), :])
        sync.dma_start(ksb[:, :], dr["kvt_scr"][0:128, :])
        for kt in range(NT):
            sync.dma_start(vtk[kt][:, :],
                           dr["kvt_scr"][128:256, 128 * kt:128 * (kt + 1)],
                           transpose=True)
        sync.dma_start(slrep[:, :],
                       bass.AP(dr["slopes"], 0, [[0, 128], [1, heads]]))
        sync.dma_start(pmrep[:, :], dr["pmask"][:, :])
        sync.dma_start(iota[:, :], iota_d[:, :])
        sync.dma_start(onescol[:, :], ones_d[:, :])
        for i in range(4):
            sync.dma_start(maskfull[i][:, :], mf_d[i][:, :])

    ct0, dt0, gt0 = ch.snap()
    A = nc.alloc_semaphore(_u("at_A"))
    E = nc.alloc_semaphore(_u("at_E"))
    Vm = nc.alloc_semaphore(_u("at_V"))
    S = nc.alloc_semaphore(_u("at_S"))

    iters = []
    hcs = []
    for h in range(heads):
        sl = float(slopes_np[h])
        ebw = 512 if sl * 511.0 <= WTHRESH else 64
        nex = 512 // ebw
        for qc in range(4):
            q0 = qc * 512
            qglob0 = TH + q0
            kts = []
            for kt in range(NT):
                kb = kt * 128
                if kb > qglob0 + 511:
                    continue
                if sl * (qglob0 - (kb + 127)) > WTHRESH:
                    continue
                kts.append(kt)
            hc = len(hcs)
            g0 = len(iters)
            for ki, kt in enumerate(kts):
                kb = kt * 128
                iters.append(dict(
                    h=h, qc=qc, ki=ki, kt=kt, kb=kb, hc=hc,
                    first=(ki == 0), last=(ki == len(kts) - 1),
                    diag=(kb >= qglob0), d0=kb - qglob0,
                    ebw=ebw, nex=nex, q0=q0, qglob0=qglob0))
            hcs.append(dict(h=h, qc=qc, n=len(kts), g0=g0, q0=q0))
    G = len(iters)
    NHC = len(hcs)

    a_score = [0] * G
    a_den = [0] * G
    e_after = [0] * G
    v_after = [0] * G
    v_copy = [0] * NHC
    s_store = [0] * NHC
    a = 0
    a += 1
    a_score[0] = a
    for g in range(G):
        if g + 1 < G:
            a += 1
            a_score[g + 1] = a
        a += 2
        a_den[g] = a
    a_tot = a
    e = 0
    for g in range(G):
        e += iters[g]["nex"]
        e_after[g] = e
    e_tot = e
    v = 0
    for g in range(G):
        v += 2 * iters[g]["nex"] + (1 if iters[g]["diag"] else 0)
        v_after[g] = v
        if iters[g]["last"]:
            v += 2
            v_copy[iters[g]["hc"]] = v
    v_tot = v
    s = 0
    for hc in range(NHC):
        s += 32
        s_store[hc] = s
    s_tot = s

    def _entry(eng):
        if ct0 > 0:
            eng.wait_ge(ch.csem, ct0)
        if dt0 > 0:
            eng.wait_ge(ch.dsem, dt0)
        if gt0 > 0:
            eng.wait_ge(ch.gsem, gt0)

    def tensor_body(te):
        _entry(te)

        def score(g):
            it = iters[g]
            if g >= 2:
                te.wait_ge(E, e_after[g - 2])
            te.matmul(sc_ps[g % 2][:, :], ksb[:, it["kb"]:it["kb"] + 128],
                      qsb[it["h"]][:, it["q0"]:it["q0"] + 512],
                      start=True, stop=True,
                      skip_group_check=True).then_inc(A, 1)

        def av(g):
            it = iters[g]
            te.wait_ge(E, e_after[g])
            hc = it["hc"]
            if it["first"] and hc >= 2:
                te.wait_ge(Vm, v_copy[hc - 2])
            te.matmul(av_ps[hc % 2][:, :], vtk[it["kt"]][:, :],
                      probs[g % 2][:, :], start=it["first"], stop=False,
                      skip_group_check=True).then_inc(A, 1)
            te.matmul(den_ps[hc % 2][0:1, :], onescol[:, :],
                      probs[g % 2][:, :], start=it["first"], stop=False,
                      skip_group_check=True).then_inc(A, 1)

        score(0)
        for g in range(G):
            if g + 1 < G:
                score(g + 1)
            av(g)
    ch.block.tensor(tensor_body)

    def vector_body(v):
        _entry(v)
        vc = 0
        for g in range(G):
            it = iters[g]
            if g >= 2:
                v.wait_ge(E, e_after[g - 2])
            bb = (g % 2) * 8
            for ce in range(it["nex"]):
                c = ce * it["ebw"]
                col = bias_v[:, bb + ce:bb + ce + 1]
                v.tensor_scalar(col, iota[:, :],
                                float(it["kb"] - it["qglob0"] - c),
                                slrep[:, it["h"]:it["h"] + 1],
                                ALU.add, ALU.mult).then_inc(Vm, 1)
                vc += 1
                v.wait_ge(Vm, vc)
                v.tensor_tensor(col, col, pmrep[:, it["kt"]:it["kt"] + 1],
                                ALU.add).then_inc(Vm, 1)
                vc += 1
            if it["diag"]:
                v.wait_ge(A, a_score[g])
                v.tensor_tensor(msc[g % 2][:, :], sc_ps[g % 2][:, :],
                                maskfull[it["d0"] // 128][:, :],
                                ALU.add).then_inc(Vm, 1)
                vc += 1
            if it["last"]:
                hc = it["hc"]
                v.wait_ge(A, a_den[g])
                if hc >= 2:
                    v.wait_ge(S, s_store[hc - 2])
                v.tensor_copy(den_sb[hc % 2][0:1, :],
                              den_ps[hc % 2][0:1, :]).then_inc(Vm, 1)
                v.tensor_copy(onum_bf[hc % 2][:, :],
                              av_ps[hc % 2][:, :]).then_inc(Vm, 1)
                vc += 2
    ch.block.vector(vector_body)

    def scalar_body(s):
        _entry(s)
        for g in range(G):
            it = iters[g]
            s.wait_ge(A, a_score[g])
            s.wait_ge(Vm, v_after[g])
            bb = (g % 2) * 8
            src = msc[g % 2] if it["diag"] else sc_ps[g % 2]
            for ce in range(it["nex"]):
                c = ce * it["ebw"]
                s.activation(probs[g % 2][:, c:c + it["ebw"]],
                             src[:, c:c + it["ebw"]], AF.Exp,
                             bias=bias_v[:, bb + ce:bb + ce + 1],
                             scale=ATT_SCALE).then_inc(E, 1)
    ch.block.scalar(scalar_body)

    def sync_body(sy):
        _entry(sy)
        for hc in range(NHC):
            d = hcs[hc]
            sy.wait_ge(Vm, v_copy[hc])
            sy.dma_start(dr["onum_scr"][128 * d["h"]:128 * (d["h"] + 1),
                                        d["q0"]:d["q0"] + 512],
                         onum_bf[hc % 2][:, :]).then_inc(S, 16)
            sy.dma_start(dr["oden_scr"][d["h"]:d["h"] + 1,
                                        d["q0"]:d["q0"] + 512],
                         den_sb[hc % 2][0:1, :]).then_inc(S, 16)
    ch.block.sync(sync_body)

    @ch.step("vector")
    def _(v, sem=None):
        v.wait_ge(A, a_tot)
        v.wait_ge(E, e_tot)
        v.wait_ge(Vm, v_tot)
        v.wait_ge(S, s_tot)
        v.tensor_copy(bias_v[:, 0:1], iota[:, :])


def emit_wo(nc, ch, st, dr):
    sb = lambda n, sh, dt: st.enter_context(nc.sbuf_tensor(_u(n), sh, dt))
    atsb = [sb(f"wo_a{h}", [128, TH], BF16) for h in range(32)]
    drow = sb("wo_drow", [1, TH], F32)
    rrow = sb("wo_rrow", [1, TH], F32)
    onesrow = sb("wo_ones", [1, 128], F32)
    drep = sb("wo_drep", [128, TH], F32)
    numf = sb("wo_numf", [128, TH], F32)
    wosb = [sb(f"wo_w{cb}", [128, 512], BF16) for cb in range(32)]
    sout = sb("wo_sout", [128, 512], BF16)
    dr_ps = st.enter_context(nc.psum_tensor(_u("wo_dps"), [128, TH], F32))
    s_ps = [st.enter_context(nc.psum_tensor(_u("wo_sps"), [128, 512], F32))
            for _i in range(2)]
    onesr_d = nc.inline_tensor(np.ones((1, 128), dtype=np.float32), "wo_or_d")

    @ch.step("sync")
    def _(sync, sem=None):
        sync.dma_start(onesrow[:, :], onesr_d[:, :])

    for h in range(32):
        @ch.step("gpsimd")
        def _(gp, sem=None):
            gp.dma_start(numf[:, :],
                         dr["onum_scr"][128 * h:128 * (h + 1), :])

        @ch.step("sync")
        def _(sync, sem=None):
            sync.dma_start(drow[0:1, :], dr["oden_scr"][h:h + 1, :])

        @ch.step("vector")
        def _(v, sem=None):
            v.reciprocal(rrow[0:1, :], drow[0:1, :])

        @ch.step("tensor")
        def _(te, sem=None):
            for c in range(0, TH, 512):
                mi = te.matmul_q(dr_ps[:, c:c + 512], onesrow[:, :],
                                 rrow[0:1, c:c + 512], start=True, stop=True,
                                 skip_group_check=True)
            te.mark(mi)

        @ch.step("vector")
        def _(v, sem=None):
            v.tensor_copy(drep[:, :], dr_ps[:, :])
            v.tensor_tensor(atsb[h][:, :], numf[:, :], drep[:, :], ALU.mult)

    for oc in range(8):
        @ch.step("gpsimd")
        def _(gp, sem=None):
            for cb in range(32):
                gp.dma_start(wosb[cb][:, :],
                             dr["Wo"][128 * cb:128 * (cb + 1),
                                      512 * oc:512 * (oc + 1)])

        for t in range(NTH):
            @ch.step("tensor")
            def _(te, sem=None):
                for cb in range(32):
                    mi = te.matmul_q(s_ps[t % 2][:, :],
                                     atsb[cb][:, 128 * t:128 * (t + 1)],
                                     wosb[cb][:, :],
                                     start=(cb == 0), stop=(cb == 31))
                te.mark(mi)

            @ch.step("vector")
            def _(v, sem=None):
                v.tensor_copy(sout[:, :], s_ps[t % 2][:, :])

            @ch.step("sync")
            def _(sync, sem=None):
                sync.dma_start(dr["s1_scr"][128 * t:128 * (t + 1),
                                            512 * oc:512 * (oc + 1)],
                               sout[:, :])


def emit_ffn1(nc, ch, st, dr):
    sb = lambda n, sh, dt: st.enter_context(nc.sbuf_tensor(_u(n), sh, dt))
    NV = 32
    xsb = [sb(f"f1_x{v}", [128, TH], BF16) for v in range(NV)]
    wc = sb("f1_wc", [128, D], BF16)
    atsb = sb("f1_at", [128, 48 * 32], BF16)
    bssb = sb("f1_bs", [48, D], BF16)
    bsf = sb("f1_bsf", [128, 32], F32)
    lsirep = sb("f1_lsi", [48, 4], I32)
    lsif = sb("f1_lsif", [48, 4], F32)
    brep = sb("f1_bv", [48, 1], F32)
    arsb = sb("f1_ar", [48, 1], F32)
    cnt = sb("f1_cnt", [48, 1], F32)
    seg = sb("f1_seg", [48, 1], F32)
    use = sb("f1_use", [48, 1], F32)
    aidx = sb("f1_aidx", [48, 1], F32)
    msk = sb("f1_msk", [48, 1], F32)
    rT = sb("f1_rT", [48, TH], BF16)
    gout = sb("f1_gout", [128, TH], BF16)
    r_ps = st.enter_context(nc.psum_tensor(_u("f1_rps"), [48, TH], F32))
    g_ps = st.enter_context(nc.psum_tensor(_u("f1_gps"), [128, TH], F32))

    @ch.step("sync")
    def _(sync, sem=None):
        for v in range(NV):
            sync.dma_start(xsb[v][:, :],
                           dr["xn1_scr"][TH:T, 128 * v:128 * (v + 1)],
                           transpose=True)
        sync.dma_start(lsirep[:, :], bass.AP(dr["lsi"], 0, [[0, 48], [1, 4]]))
        sync.dma_start(brep[:, :], bass.AP(dr["bval"], 0, [[0, 48], [1, 1]]))
        sync.dma_start(arsb[:, :], dr["arow"][:, :])
        sync.dma_start(bsf[:, :], dr["biasc1"][:, :])

    @ch.step("gpsimd")
    def _(gp, sem=None):
        for v in range(NV):
            gp.dma_start(ap(atsb, v * 48, (48 * 32, 128), (1, 48)),
                         dr["AT"][128 * v:128 * (v + 1), :])
        gp.dma_start(bssb[0:48, :], dr["BS"][:, :])

    @ch.step("vector")
    def _(v, sem=None):
        v.tensor_copy(lsif[0:48, :], lsirep[0:48, :])
        v.tensor_scalar(lsif[0:48, :], lsif[0:48, :], brep[0:48, 0:1], None,
                        ALU.is_le)
        v.tensor_reduce(cnt[0:48, :], lsif[0:48, :], mybir.AxisListType.X,
                        ALU.add)
        v.tensor_scalar(seg[0:48, :], cnt[0:48, :], -1.0, None, ALU.add)
        v.tensor_scalar(use[0:48, :], seg[0:48, :], 0.0, None, ALU.is_gt)
        v.tensor_scalar(aidx[0:48, :], seg[0:48, :], -1.0, 0.0, ALU.add,
                        ALU.max)
        v.tensor_scalar(aidx[0:48, :], aidx[0:48, :], 2.0, None, ALU.min)
        v.tensor_tensor(msk[0:48, :], aidx[0:48, :], arsb[0:48, :],
                        ALU.is_equal)
        v.tensor_tensor(msk[0:48, :], msk[0:48, :], use[0:48, :], ALU.mult)

    @ch.step("tensor")
    def _(te, sem=None):
        for tq in range(4):
            for v in range(NV):
                mi = te.matmul_q(r_ps[0:48, 512 * tq:512 * (tq + 1)],
                                 ap(atsb, v * 48, (48 * 32, 128), (1, 48)),
                                 xsb[v][:, 512 * tq:512 * (tq + 1)],
                                 start=(v == 0), stop=(v == NV - 1),
                                 skip_group_check=True)
        te.mark(mi)

    @ch.step("vector")
    def _(v, sem=None):
        v.tensor_scalar(rT[0:48, :], r_ps[0:48, :], msk[0:48, 0:1], None,
                        ALU.mult)

    for oc in range(32):
        @ch.step("gpsimd")
        def _(gp, sem=None):
            gp.dma_start(ap(wc, 0, (D, 128), (128, 32), (1, 128)),
                         ap(dr["WT"], oc * 128, (D, 128), (128 * D, 32),
                            (1, 128)))

        @ch.step("tensor")
        def _(te, sem=None):
            for tq in range(4):
                for g in range(NV):
                    te.matmul_q(g_ps[:, 512 * tq:512 * (tq + 1)],
                                wc[:, 128 * g:128 * (g + 1)],
                                xsb[g][:, 512 * tq:512 * (tq + 1)],
                                start=(g == 0), stop=False,
                                skip_group_check=True)
                mi = te.matmul_q(g_ps[:, 512 * tq:512 * (tq + 1)],
                                 bssb[0:48, 128 * oc:128 * (oc + 1)],
                                 rT[0:48, 512 * tq:512 * (tq + 1)],
                                 start=False, stop=True,
                                 skip_group_check=True)
            te.mark(mi)

        @ch.step("scalar")
        def _(s, sem=None):
            s.activation(gout[:, :], g_ps[:, :], AF.Silu,
                         bias=bsf[:, oc:oc + 1])

        @ch.step("sync")
        def _(sync, sem=None):
            sync.dma_start(dr["g1T_scr"][128 * oc:128 * (oc + 1), :],
                           gout[:, :])


def build_mega(slopes_np):
    nc = bass.Bass("TRN2", target_bir_lowering=False)
    dt_ = lambda n, sh, dty, kind="Internal": nc.dram_tensor(n, sh, dty,
                                                             kind=kind)
    dr = {}
    EI = "ExternalInput"
    dr["xb"] = dt_("xb", [T, D], F32, EI)
    dr["WinT"] = dt_("WinT", [D, MD], F32, EI)
    dr["binm"] = dt_("binm", [1, MD], F32, EI)
    dr["qw"] = dt_("qw", [512, D], I32, EI)
    dr["qz"] = dt_("qz", [32, 512], I32, EI)
    dr["scales"] = dt_("scales", [32, D], F16, EI)
    dr["biasc0"] = dt_("biasc0", [128, 32], F16, EI)
    dr["shamt"] = dt_("shamt", [128, 1], I32, EI)
    dr["cmask"] = dt_("cmask", [128, 8], F32, EI)
    for dd in range(4):
        for nm, sh in [(f"lnw{dd}", [1, MD]), (f"lnb{dd}", [1, MD]),
                       (f"scl{dd}", [1, MD]), (f"phzs{dd}", [128, 8]),
                       (f"phzrow{dd}", [2, MD]), (f"cinit{dd}", [128, 8]),
                       (f"cinj{dd}", [128, 8]), (f"ln2w{dd}", [1, MD]),
                       (f"ln2b{dd}", [1, MD]), (f"W1T{dd}", [MD, MF]),
                       (f"b1c{dd}", [128, 16]), (f"W2T{dd}", [MF, MD]),
                       (f"b2_{dd}", [1, MD])]:
            dr[nm] = dt_(nm, sh, F32, EI)
    dr["lnlw"] = dt_("lnlw", [1, MD], F32, EI)
    dr["lnlb"] = dt_("lnlb", [1, MD], F32, EI)
    dr["WoutT"] = dt_("WoutT", [MD, D], F32, EI)
    dr["bout"] = dt_("bout", [1, D], F32, EI)
    dr["Wq"] = dt_("Wq", [D, D], F32, EI)
    dr["Wkv"] = dt_("Wkv", [D, 256], F32, EI)
    dr["slopes"] = dt_("slopes", [1, 32], F32, EI)
    dr["pmask"] = dt_("pmask", [128, NT], F32, EI)
    dr["Wo"] = dt_("Wo", [D, D], F32, EI)
    dr["WT"] = dt_("WT", [D, D], F32, EI)
    dr["biasc1"] = dt_("biasc1", [128, 32], F32, EI)
    dr["AT"] = dt_("AT", [D, 48], F32, EI)
    dr["BS"] = dt_("BS", [48, D], F32, EI)
    dr["lsi"] = dt_("lsi", [1, 4], I32, EI)
    dr["bval"] = dt_("bval", [1, 1], F32, EI)
    dr["arow"] = dt_("arow", [48, 1], F32, EI)
    dr["out"] = dt_("out", [TH, D], F32, "ExternalOutput")
    dr["xn_scr"] = dt_("xn_scr", [T, D], BF16)
    dr["h_scr"] = dt_("h_scr", [T, MD], F32)
    dr["g0T_scr"] = dt_("g0T_scr", [D, T], BF16)
    dr["s0_scr"] = dt_("s0_scr", [T, D], BF16)
    dr["x1_scr"] = dt_("x1_scr", [T, D], F32)
    dr["xn1_scr"] = dt_("xn1_scr", [T, D], BF16)
    dr["qt_scr"] = dt_("qt_scr", [D, TH], BF16)
    dr["kvt_scr"] = dt_("kvt_scr", [256, T], BF16)
    dr["onum_scr"] = dt_("onum_scr", [D, TH], BF16)
    dr["oden_scr"] = dt_("oden_scr", [32, TH], F32)
    dr["s1_scr"] = dt_("s1_scr", [TH, D], BF16)
    dr["g1T_scr"] = dt_("g1T_scr", [D, TH], BF16)
    dr["lny_scr"] = dt_("lny_scr", [T, MD], BF16)
    dr["w4_scr"] = dt_("w4_scr", [D, D], BF16)
    dr["crow_scr"] = dt_("crow_scr", [2 * 16 * MD], F32)
    dr["prow_scr"] = dt_("prow_scr", [2, MD], F32)

    ch = Chain(nc)
    if True:
        with ExitStack() as st:
            emit_ln(nc, ch, st, dr["xb"], dr["xn_scr"], NT)
        with ExitStack() as st:
            emit_h0_gate0(nc, ch, st, dr)
        with ExitStack() as st:
            emit_mini_stack(nc, ch, st, dr)
        with ExitStack() as st:
            emit_combine(nc, ch, st, dr["xb"], dr["s0_scr"], dr["g0T_scr"],
                         dr["x1_scr"], NT, 0, 0, 0)
        with ExitStack() as st:
            emit_ln(nc, ch, st, dr["x1_scr"], dr["xn1_scr"], NT)
        with ExitStack() as st:
            emit_qkv(nc, ch, st, dr)
        with ExitStack() as st:
            emit_attn(nc, ch, st, dr, slopes_np)
        with ExitStack() as st:
            emit_wo(nc, ch, st, dr)
        with ExitStack() as st:
            emit_ffn1(nc, ch, st, dr)
        with ExitStack() as st:
            emit_combine(nc, ch, st, dr["x1_scr"], dr["s1_scr"],
                         dr["g1T_scr"], dr["out"], NTH, TH, 0, 0)
    ch.finish()
    return nc


_PROG = {}


def kernel(**inp):
    C = np.ascontiguousarray
    f32 = np.float32
    sl_np = alibi_slopes(32)
    if "mega" not in _PROG:
        _PROG["mega"] = build_mega(tuple(float(v) for v in sl_np))
    nc = _PROG["mega"]

    x = np.asarray(inp["L0_attn_x"], f32)
    scan_layout = lambda a: C(np.asarray(a, f32).reshape(4, 128).T)
    zeros8 = np.zeros((128, 8), f32)
    ones8 = np.ones((128, 8), f32)

    shamt = ((np.arange(128) % 8) * 4).astype(np.int32)[:, None]
    biasc0 = C(np.asarray(inp["L0_ffn_bias"]).reshape(32, 128).T)
    WinT = C(np.asarray(inp["L0_attn_Win"], f32).T)
    WoutT = C(np.asarray(inp["L0_attn_Wout"], f32).T)
    WT = C(np.asarray(inp["L1_ffn_weight"], f32).T)
    biasc1 = C(np.asarray(inp["L1_ffn_bias"], f32).reshape(32, 128).T)
    AT = C(np.asarray(inp["L1_ffn_lora_A"],
                      f32).transpose(2, 0, 1).reshape(D, 48))
    BS = C(np.asarray(inp["L1_ffn_lora_B"],
                      f32).transpose(0, 2, 1).reshape(48, D))
    arow = np.repeat(np.arange(3, dtype=f32), 16)[:, None]
    lsi = np.asarray(inp["L1_ffn_lora_start_indices"], np.int32)[None]

    in_maps = []
    for c in range(NCORE):
        b, hf = c // 2, c % 2
        if hf == 0:
            xb = np.concatenate([np.zeros((TH, D), f32),
                                 np.asarray(x[b, 0:TH], f32)], 0)
        else:
            xb = np.asarray(x[b], f32)
        m = {
            "xb": C(xb), "WinT": WinT,
            "binm": np.asarray(inp["L0_attn_bin"], f32)[None],
            "qw": np.asarray(inp["L0_ffn_qweight"]),
            "qz": np.asarray(inp["L0_ffn_qzeros"]),
            "scales": np.asarray(inp["L0_ffn_scales"]),
            "biasc0": biasc0, "shamt": shamt,
            "cmask": ones8 if hf == 1 else zeros8,
            "lnlw": np.asarray(inp["L0_attn_lnl_w"], f32)[None],
            "lnlb": np.asarray(inp["L0_attn_lnl_b"], f32)[None],
            "WoutT": WoutT,
            "bout": np.asarray(inp["L0_attn_bout"], f32)[None],
            "Wq": np.asarray(inp["L1_attn_Wq"], f32),
            "Wkv": np.asarray(inp["L1_attn_Wkv"], f32),
            "slopes": sl_np[None],
            "pmask": np.concatenate(
                [np.full((128, 16), NEG if hf == 0 else 0.0, f32),
                 np.zeros((128, 16), f32)], 1),
            "Wo": np.asarray(inp["L1_attn_Wo"], f32),
            "WT": WT, "biasc1": biasc1, "AT": AT, "BS": BS,
            "lsi": lsi, "bval": np.full((1, 1), float(b), f32),
            "arow": arow,
        }
        for dd in range(4):
            lc8 = np.concatenate([scan_layout(inp["L0_attn_lc_re"][dd]),
                                  scan_layout(inp["L0_attn_lc_im"][dd])], 1)
            m[f"lnw{dd}"] = np.asarray(inp["L0_attn_ln1_w"][dd], f32)[None]
            m[f"lnb{dd}"] = np.asarray(inp["L0_attn_ln1_b"][dd], f32)[None]
            m[f"scl{dd}"] = np.asarray(inp["L0_attn_sc_lin"][dd], f32)[None]
            m[f"phzs{dd}"] = np.concatenate(
                [scan_layout(inp["L0_attn_phz_re"][dd]),
                 scan_layout(inp["L0_attn_phz_im"][dd])], 1)
            m[f"phzrow{dd}"] = C(np.stack(
                [np.asarray(inp["L0_attn_phz_re"][dd], f32),
                 np.asarray(inp["L0_attn_phz_im"][dd], f32)]))
            m[f"cinit{dd}"] = lc8 if hf == 1 else zeros8.copy()
            m[f"cinj{dd}"] = lc8 if hf == 0 else zeros8.copy()
            m[f"ln2w{dd}"] = np.asarray(inp["L0_attn_ln2_w"][dd], f32)[None]
            m[f"ln2b{dd}"] = np.asarray(inp["L0_attn_ln2_b"][dd], f32)[None]
            m[f"W1T{dd}"] = C(np.asarray(inp["L0_attn_W1"][dd], f32).T)
            m[f"b1c{dd}"] = C(np.asarray(inp["L0_attn_b1"][dd],
                                         f32).reshape(16, 128).T)
            m[f"W2T{dd}"] = C(np.asarray(inp["L0_attn_W2"][dd], f32).T)
            m[f"b2_{dd}"] = np.asarray(inp["L0_attn_b2"][dd], f32)[None]
        in_maps.append(m)

    res = run_bass_kernel_spmd(nc, in_maps, core_ids=list(range(NCORE)))
    out = np.empty((4, 4096, D), f32)
    for c in range(NCORE):
        b, hf = c // 2, c % 2
        out[b, hf * TH:(hf + 1) * TH] = np.asarray(res.results[c]["out"], f32)
    return out


# version 11 (final)
# speedup vs baseline: 1.5772x; 1.1814x over previous
import math
from contextlib import ExitStack
import numpy as np
import ml_dtypes
import concourse.bass as bass
import concourse.mybir as mybir
from concourse.bass_utils import run_bass_kernel_spmd

F32 = mybir.dt.float32
BF16 = mybir.dt.bfloat16
F16 = mybir.dt.float16
I32 = mybir.dt.int32
AF = mybir.ActivationFunctionType
ALU = mybir.AluOpType
BF = ml_dtypes.bfloat16

NCORE = 8
T = 4096
TH = 2048
D = 4096
MD = 512
MF = 2048
NT = T // 128
NTH = TH // 128
EPS = 1e-5
ATT_SCALE = float(128 ** -0.5)
WTHRESH = 60.0
NEG = -1e30

_UN = [0]


def _u(n):
    _UN[0] += 1
    return f"{n}_{_UN[0]}"


def alibi_slopes(n_heads=32):
    cp2 = 2 ** math.floor(math.log2(n_heads))
    base = 2.0 ** (-(2.0 ** (-(math.log2(cp2) - 3))))
    slopes = [base ** k for k in range(1, cp2 + 1)]
    if cp2 != n_heads:
        eb = 2.0 ** (-(2.0 ** (-(math.log2(2 * cp2) - 3))))
        ne = min(cp2, n_heads - cp2)
        slopes += [eb ** k for k in range(1, 2 * ne + 1, 2)]
    return np.asarray(slopes, dtype=np.float32)


def ap(tsr, off, *dims):
    return bass.AP(tsr, off, [list(d) for d in dims])


def smax(*snaps):
    snaps = [s for s in snaps if s is not None]
    if not snaps:
        return None
    return tuple(max(v) for v in zip(*snaps))


_NOSYNC = {"wait_ge", "nop", "engine_nop", "register", "reg_mov", "print"}


class _Auto:

    def __init__(self, ch, eng):
        self._ch = ch
        self._eng = eng

    def matmul_q(self, *a, **k):
        return self._eng.matmul(*a, **k)

    def mark(self, inst):
        inst.then_inc(self._ch.csem, 1)
        self._ch.ct += 1
        return inst

    def __getattr__(self, name):
        target = getattr(self._eng, name)
        if name.startswith("_") or not callable(target) or name in _NOSYNC:
            return target

        def wrap(*a, **k):
            inst = target(*a, **k)
            if inst is None or not hasattr(inst, "then_inc"):
                return inst
            if name in ("dma_start", "indirect_dma_start"):
                from concourse.bass import BassGpSimd
                if isinstance(self._eng, BassGpSimd):
                    inst.then_inc(self._ch.gsem, 16)
                    self._ch.gt += 16
                else:
                    inst.then_inc(self._ch.dsem, 16)
                    self._ch.dt += 16
            else:
                inst.then_inc(self._ch.csem, 1)
                self._ch.ct += 1
                self._eng.wait_ge(self._ch.csem, self._ch.ct)
            return inst
        return wrap


class Chain:

    def __init__(self, nc):
        self.nc = nc
        self.block = nc.Block()
        self.block.__enter__()
        self.csem = nc.alloc_semaphore("ch_c")
        self.dsem = nc.alloc_semaphore("ch_d")
        self.gsem = nc.alloc_semaphore("ch_g")
        self.ct = 0
        self.dt = 0
        self.gt = 0
        self.last = {}

    def snap(self):
        return (self.ct, self.dt, self.gt)

    def step(self, engine, at=None):
        def deco(fn):
            if at is None:
                ct0, dt0, gt0 = self.ct, self.dt, self.gt
            else:
                ct0, dt0, gt0 = at

            def body(eng):
                if ct0 > 0 and self.last.get("c") != (engine, ct0):
                    eng.wait_ge(self.csem, ct0)
                if dt0 > 0 and self.last.get("d") != (engine, dt0):
                    eng.wait_ge(self.dsem, dt0)
                if gt0 > 0 and self.last.get("g") != (engine, gt0):
                    eng.wait_ge(self.gsem, gt0)
                fn(_Auto(self, eng), None)
            getattr(self.block, engine)(body)
            self.last["c"] = (engine, ct0)
            self.last["d"] = (engine, dt0)
            self.last["g"] = (engine, gt0)
            return fn
        return deco

    def finish(self):
        ct, dt, gt = self.ct, self.dt, self.gt

        def fin(g):
            g.wait_ge(self.csem, ct)
            g.wait_ge(self.dsem, dt)
            g.wait_ge(self.gsem, gt)
        self.block.gpsimd(fin)
        self.block.__exit__(None, None, None)


def emit_ln(nc, ch, st, x_dram, xn_dram, ntiles):
    sb = lambda n, sh, dt: st.enter_context(nc.sbuf_tensor(_u(n), sh, dt))
    NB = 3
    xt = [sb("ln_xt", [128, D], F32) for _i in range(NB)]
    ot = [sb("ln_ot", [128, D], BF16) for _i in range(NB)]
    scr = [sb("ln_scr", [128, D], F32) for _i in range(2)]
    sums = [sb("ln_sums", [128, 1], F32) for _i in range(2)]
    sumsq = [sb("ln_sumsq", [128, 1], F32) for _i in range(2)]
    nmu = [sb("ln_nmu", [128, 1], F32) for _i in range(2)]
    ve = [sb("ln_ve", [128, 1], F32) for _i in range(2)]
    rstd = [sb("ln_rstd", [128, 1], F32) for _i in range(2)]
    scr0 = [sb("ln_scr0", [128, 1], F32) for _i in range(2)]
    ld = {}
    stats = {}
    sq = {}
    aff = {}
    stq = {}
    for j in range(ntiles):
        b = j % NB
        p = j % 2

        @ch.step("sync", at=smax(aff.get(j - NB), stq.get(j - NB)))
        def _(sync, sem=None):
            sync.dma_start(xt[b][:, :], x_dram[128 * j:128 * (j + 1), :])
        ld[j] = ch.snap()

        @ch.step("vector", at=smax(ld[j], sq.get(j - 2)))
        def _(v, sem=None):
            v.tensor_scalar(scr[p][:, :], xt[b][:, :], 1.0, 0.0, ALU.mult,
                            ALU.add, accum_out=sums[p][:, :])
            v.scalar_tensor_tensor(scr[p][:, :], xt[b][:, :], 1.0, xt[b][:, :],
                                   ALU.mult, ALU.mult,
                                   accum_out=sumsq[p][:, :])
            v.tensor_scalar(nmu[p][:, :], sums[p][:, :], -1.0 / D, None,
                            ALU.mult)
            v.tensor_scalar(ve[p][:, :], sumsq[p][:, :], 1.0 / D, EPS,
                            ALU.mult, ALU.add)
            v.tensor_tensor(scr0[p][:, :], nmu[p][:, :], nmu[p][:, :],
                            ALU.mult)
            v.tensor_tensor(ve[p][:, :], ve[p][:, :], scr0[p][:, :],
                            ALU.subtract)
            v.reciprocal(ve[p][:, :], ve[p][:, :])
        stats[j] = ch.snap()

        @ch.step("scalar", at=stats[j])
        def _(s, sem=None):
            s.activation(rstd[p][:, :], ve[p][:, :], AF.Sqrt)
        sq[j] = ch.snap()

        @ch.step("vector", at=smax(sq[j], stq.get(j - NB)))
        def _(v, sem=None):
            v.tensor_scalar(ot[b][:, :], xt[b][:, :], nmu[p][:, 0:1],
                            rstd[p][:, 0:1], ALU.add, ALU.mult)
        aff[j] = ch.snap()

        @ch.step("gpsimd", at=aff[j])
        def _(g, sem=None):
            g.dma_start(xn_dram[128 * j:128 * (j + 1), :], ot[b][:, :])
        stq[j] = ch.snap()


def emit_wln_chunk(ch, hsb_j, lwrep, lbrep, out_bf, tmp, width=MD):
    scr, sums, sumsq, nmu, ve, rstd, scr0, lnt = tmp

    @ch.step("vector")
    def _(v, sem=None):
        v.tensor_scalar(scr[:, :], hsb_j[:, :], 1.0, 0.0, ALU.mult, ALU.add,
                        accum_out=sums[:, :])
        v.scalar_tensor_tensor(scr[:, :], hsb_j[:, :], 1.0, hsb_j[:, :],
                               ALU.mult, ALU.mult, accum_out=sumsq[:, :])
        v.tensor_scalar(nmu[:, :], sums[:, :], -1.0 / width, None, ALU.mult)
        v.tensor_scalar(ve[:, :], sumsq[:, :], 1.0 / width, EPS,
                        ALU.mult, ALU.add)
        v.tensor_tensor(scr0[:, :], nmu[:, :], nmu[:, :], ALU.mult)
        v.tensor_tensor(ve[:, :], ve[:, :], scr0[:, :], ALU.subtract)
        v.reciprocal(ve[:, :], ve[:, :])

    @ch.step("scalar")
    def _(s, sem=None):
        s.activation(rstd[:, :], ve[:, :], AF.Sqrt)

    @ch.step("vector")
    def _(v, sem=None):
        v.tensor_scalar(lnt[:, :], hsb_j[:, :], nmu[:, 0:1], rstd[:, 0:1],
                        ALU.add, ALU.mult)
        v.tensor_tensor(lnt[:, :], lnt[:, :], lwrep[:, :], ALU.mult)
        v.tensor_tensor(out_bf[:, :], lnt[:, :], lbrep[:, :], ALU.add)


def emit_h0_gate0(nc, ch, st, dr):
    sb = lambda n, sh, dt: st.enter_context(nc.sbuf_tensor(_u(n), sh, dt))
    NV = D // 128
    zsn = sb("g0_zsn", [32, D], BF16)
    with ExitStack() as dst:
        dsb = lambda n, sh, dt: dst.enter_context(
            nc.sbuf_tensor(_u(n), sh, dt))
        ssb = dsb("g0_ssb", [32, D], BF16)
        qwrep = [dsb("g0_qwrep", [128, D], I32) for _i in range(2)]
        srep = [dsb("g0_srep", [128, D], BF16) for _i in range(2)]
        w4g = [dsb("g0_w4g", [128, D], BF16) for _i in range(2)]
        sh_sb = dsb("g0_sh", [128, 1], I32)
        ztmp = dsb("g0_ztmp", [32, 512], I32)
        qzsb = dsb("g0_qz", [32, 512], I32)

        @ch.step("sync")
        def _(sync, sem=None):
            sync.dma_start(qzsb[:, :], dr["qz"][:, :])
            sync.dma_start(sh_sb[:, :], dr["shamt"][:, :])

        @ch.step("gpsimd")
        def _(gp, sem=None):
            gp.dma_start(ssb[0:32, :], dr["scales"][:, :])

        @ch.step("vector")
        def _(v, sem=None):
            for nib in range(8):
                v.tensor_scalar(ztmp[0:32, :], qzsb[0:32, :], 4 * nib, 15,
                                ALU.logical_shift_right, ALU.bitwise_and)
                v.tensor_copy(ap(zsn, nib, (D, 32), (8, 512)), ztmp[0:32, :])
            v.tensor_tensor(zsn[0:32, :], zsn[0:32, :], ssb[0:32, :], ALU.mult)
            v.tensor_scalar(zsn[0:32, :], zsn[0:32, :], -1.0, None, ALU.mult)

        dld = {}
        dcp = {}
        dst_ = {}
        for g in range(32):
            b = g % 2

            @ch.step("sync", at=dcp.get(g - 2))
            def _(sync, sem=None):
                sync.dma_start(qwrep[b][:, :],
                               bass.AP(dr["qw"], 16 * g * D,
                                       [[D, 16], [0, 8], [1, D]]))

            @ch.step("gpsimd", at=dcp.get(g - 2))
            def _(gp, sem=None):
                gp.dma_start(srep[b][:, :],
                             bass.AP(dr["scales"], g * D, [[0, 128], [1, D]]))
            dld[g] = ch.snap()

            @ch.step("vector", at=smax(dld[g], dst_.get(g - 2)))
            def _(v, sem=None):
                v.tensor_scalar(qwrep[b][:, :], qwrep[b][:, :], sh_sb[:, 0:1],
                                15, ALU.logical_shift_right, ALU.bitwise_and)
                v.tensor_copy(w4g[b][:, :], qwrep[b][:, :])
                v.tensor_tensor(w4g[b][:, :], w4g[b][:, :], srep[b][:, :],
                                ALU.mult)
            dcp[g] = ch.snap()

            @ch.step("gpsimd", at=dcp[g])
            def _(gp, sem=None):
                gp.dma_start(dr["w4_scr"][128 * g:128 * (g + 1), :],
                             w4g[b][:, :])
            dst_[g] = ch.snap()

    xsb = [sb(f"hg_x{v}", [128, TH], BF16) for v in range(NV)]
    winsb = sb("hg_win", [128, NV * MD], BF16)
    brep = sb("hg_bin", [128, MD], F32)
    b0rep = sb("hg_b0", [128, 32], F32)
    b0h = sb("hg_b0h", [128, 32], F16)
    hout = [sb("hg_hout", [128, MD], F32) for _i in range(2)]
    w4c = [sb("hg_w4c", [128, D], BF16) for _i in range(2)]
    gout = [sb("hg_gout", [128, TH], BF16) for _i in range(2)]
    xs_sb = sb("hg_xs", [32, TH], BF16)
    OHsb = sb("hg_OH", [128, 32 * 32], BF16)
    OH32_np = np.zeros((128, 32 * 32), dtype=BF)
    for g in range(32):
        OH32_np[:, g * 32 + g] = 1.0
    OH32_d = nc.inline_tensor(np.asarray(OH32_np), "hg_OH_d")

    @ch.step("sync")
    def _(sync, sem=None):
        sync.dma_start(OHsb[:, :], OH32_d[:, :])
        sync.dma_start(brep[:, :], bass.AP(dr["binm"], 0, [[0, 128], [1, MD]]))
        sync.dma_start(b0h[:, :], dr["biasc0"][:, :])

    @ch.step("gpsimd")
    def _(gp, sem=None):
        for v in range(NV):
            gp.dma_start(winsb[:, v * MD:(v + 1) * MD],
                         dr["WinT"][128 * v:128 * (v + 1), :])

    @ch.step("vector")
    def _(v, sem=None):
        v.tensor_copy(b0rep[:, :], b0h[:, :])

    for th in range(2):
        t0 = th * TH

        @ch.step("sync")
        def _(sync, sem=None):
            for v in range(NV):
                sync.dma_start(xsb[v][:, :],
                               dr["xn_scr"][t0:t0 + TH,
                                            128 * v:128 * (v + 1)],
                               transpose=True)

        with ExitStack() as hps_st:
            h_ps = [hps_st.enter_context(
                nc.psum_tensor(_u("hg_hps"), [128, MD], F32))
                for _i in range(2)]
            xmm = {}
            xcp = {}
            for tq in range(4):
                @ch.step("tensor", at=(None if tq < 2 else xcp.get(tq - 2))
                         if tq else None)
                def _(te, sem=None):
                    for g in range(NV):
                        mi = te.matmul_q(h_ps[tq % 2][0:32, :],
                                         OHsb[:, 32 * g:32 * (g + 1)],
                                         xsb[g][:, 512 * tq:512 * (tq + 1)],
                                         start=(g == 0), stop=(g == NV - 1),
                                         skip_group_check=True)
                    te.mark(mi)
                xmm[tq] = ch.snap()

                @ch.step("vector", at=xmm[tq])
                def _(v, sem=None):
                    v.tensor_copy(xs_sb[0:32, 512 * tq:512 * (tq + 1)],
                                  h_ps[tq % 2][0:32, :])
                xcp[tq] = ch.snap()

            hmm = {}
            hcp = {}
            hst = {}
            for t in range(NTH):
                @ch.step("tensor", at=smax(hcp.get(t - 2),
                                           xcp[3] if t == 0 else None))
                def _(te, sem=None):
                    for v in range(NV):
                        mi = te.matmul_q(h_ps[t % 2][:, :],
                                         xsb[v][:, 128 * t:128 * (t + 1)],
                                         winsb[:, v * MD:(v + 1) * MD],
                                         start=(v == 0), stop=(v == NV - 1))
                    te.mark(mi)
                hmm[t] = ch.snap()

                @ch.step("vector", at=smax(hmm[t], hst.get(t - 2)))
                def _(v, sem=None):
                    v.tensor_tensor(hout[t % 2][:, :], h_ps[t % 2][:, :],
                                    brep[:, :], ALU.add)
                hcp[t] = ch.snap()

                @ch.step("gpsimd", at=hcp[t])
                def _(g, sem=None):
                    g.dma_start(
                        dr["h_scr"][t0 + 128 * t:t0 + 128 * (t + 1), :],
                        hout[t % 2][:, :])
                hst[t] = ch.snap()

        with ExitStack() as gps_st:
            g_ps = [gps_st.enter_context(
                nc.psum_tensor(_u("hg_gps"), [128, TH], F32))
                for _i in range(2)]
            gld = {}
            gmm = {}
            gsl = {}
            gst_ = {}
            for oc in range(32):
                b = oc % 2

                @ch.step("sync", at=gmm.get(oc - 2))
                def _(sync, sem=None):
                    sync.dma_start(
                        ap(w4c[b], 0, (D, 128), (128, 32), (1, 128)),
                        ap(dr["w4_scr"], oc * 128, (D, 128), (128 * D, 32),
                           (1, 128)))
                gld[oc] = ch.snap()

                @ch.step("tensor", at=smax(gld[oc], gsl.get(oc - 2)))
                def _(te, sem=None):
                    for tq in range(4):
                        for g in range(NV):
                            te.matmul_q(g_ps[b][:, 512 * tq:512 * (tq + 1)],
                                        w4c[b][:, 128 * g:128 * (g + 1)],
                                        xsb[g][:, 512 * tq:512 * (tq + 1)],
                                        start=(g == 0), stop=False,
                                        skip_group_check=True)
                        mi = te.matmul_q(g_ps[b][:, 512 * tq:512 * (tq + 1)],
                                         zsn[0:32, 128 * oc:128 * (oc + 1)],
                                         xs_sb[0:32,
                                               512 * tq:512 * (tq + 1)],
                                         start=False, stop=True,
                                         skip_group_check=True)
                    te.mark(mi)
                gmm[oc] = ch.snap()

                @ch.step("scalar", at=smax(gmm[oc], gst_.get(oc - 2)))
                def _(s, sem=None):
                    s.activation(gout[b][:, :], g_ps[b][:, :], AF.Silu,
                                 bias=b0rep[:, oc:oc + 1])
                gsl[oc] = ch.snap()

                @ch.step("gpsimd", at=gsl[oc])
                def _(g, sem=None):
                    g.dma_start(dr["g0T_scr"][128 * oc:128 * (oc + 1),
                                              t0:t0 + TH], gout[b][:, :])
                gst_[oc] = ch.snap()


def _dft_consts_np():
    i = np.arange(128)
    f = np.arange(128)
    ang = 2 * np.pi * np.outer(i, f) / 256.0
    WfrT = np.cos(ang).astype(BF)
    WfiT = (-np.sin(ang)).astype(BF)
    alt = ((-1.0) ** i).astype(BF)[:, None]
    w = np.full(128, 2.0)
    w[0] = 1.0
    angl = 2 * np.pi * np.outer(f, np.arange(128)) / 256.0
    VcT = (w[:, None] * np.cos(angl) / 256.0).astype(BF)
    VsT = (-w[:, None] * np.sin(angl) / 256.0).astype(BF)
    altl = ((((-1.0) ** np.arange(128)) / 256.0).astype(BF))[None, :]
    ident = np.eye(128, dtype=np.float32)
    OH = np.zeros((128, 256), dtype=BF)
    for j in range(16):
        OH[:, j * 16 + j] = 1.0
    return WfrT, WfiT, alt, VcT, VsT, altl, ident, OH


def emit_mini_stack(nc, ch, st, dr):
    sb = lambda n, sh, dt: st.enter_context(nc.sbuf_tensor(_u(n), sh, dt))
    consts = _dft_consts_np()
    names = ["WfrT", "WfiT", "alt", "VcT", "VsT", "altl", "ident", "OH"]
    shapes = [[128, 128], [128, 128], [128, 1], [128, 128], [128, 128],
              [1, 128], [128, 128], [128, 256]]
    dts = [BF16, BF16, BF16, BF16, BF16, BF16, F32, BF16]
    csb = {}
    cds = {}
    for nm, shp, dty, arr in zip(names, shapes, dts, consts):
        csb[nm] = sb("m_" + nm, shp, dty)
        cds[nm] = nc.inline_tensor(np.asarray(arr), "m_d_" + nm)
    hsb = [sb(f"m_h{t}", [128, MD], F32) for t in range(NT)]
    cmask = sb("m_cmask", [128, 8], F32)

    @ch.step("sync")
    def _(sync, sem=None):
        for nm in names:
            sync.dma_start(csb[nm][:, :], cds[nm][:, :])
        sync.dma_start(cmask[:, :], dr["cmask"][:, :])
        for t in range(NT):
            sync.dma_start(hsb[t][:, :], dr["h_scr"][128 * t:128 * (t + 1), :])

    for dd in range(4):
        _emit_mini_layer(nc, ch, st, dr, dd, hsb, csb, cmask)

    with ExitStack() as fst:
        fsb = lambda n, sh, dt: fst.enter_context(
            nc.sbuf_tensor(_u(n), sh, dt))
        lwrep = fsb("sf_lw", [128, MD], F32)
        lbrep = fsb("sf_lb", [128, MD], F32)
        borep = fsb("sf_bo", [128, D], F32)
        lny = fsb("sf_lny", [128, MD], BF16)
        hnT = [fsb(f"sf_hnT{cb}", [128, T], BF16) for cb in range(4)]
        wsb = [fsb(f"sf_w{cb}", [128, D], BF16) for cb in range(4)]
        sout = fsb("sf_sout", [128, D], BF16)
        tmp = [fsb("sf_scr", [128, MD], F32), fsb("sf_su", [128, 1], F32),
               fsb("sf_sq", [128, 1], F32), fsb("sf_nm", [128, 1], F32),
               fsb("sf_ve", [128, 1], F32), fsb("sf_rs", [128, 1], F32),
               fsb("sf_s0", [128, 1], F32), fsb("sf_lnt", [128, MD], F32)]
        psum = fst.enter_context(nc.psum_tensor(_u("sf_ps"), [128, D], F32))

        @ch.step("sync")
        def _(sync, sem=None):
            sync.dma_start(lwrep[:, :],
                           bass.AP(dr["lnlw"], 0, [[0, 128], [1, MD]]))
            sync.dma_start(lbrep[:, :],
                           bass.AP(dr["lnlb"], 0, [[0, 128], [1, MD]]))
            sync.dma_start(borep[:, :],
                           bass.AP(dr["bout"], 0, [[0, 128], [1, D]]))

        @ch.step("gpsimd")
        def _(gp, sem=None):
            for cb in range(4):
                gp.dma_start(wsb[cb][:, :],
                             dr["WoutT"][128 * cb:128 * (cb + 1), :])

        for j in range(NT):
            emit_wln_chunk(ch, hsb[j], lwrep, lbrep, lny, tmp)

            @ch.step("sync")
            def _(sync, sem=None):
                sync.dma_start(dr["lny_scr"][128 * j:128 * (j + 1), :],
                               lny[:, :])

        @ch.step("sync")
        def _(sync, sem=None):
            for cb in range(4):
                sync.dma_start(hnT[cb][:, :],
                               dr["lny_scr"][:, 128 * cb:128 * (cb + 1)],
                               transpose=True)

        for t in range(NT):
            @ch.step("tensor")
            def _(te, sem=None):
                for oq in range(8):
                    for cb in range(4):
                        mi = te.matmul_q(psum[:, 512 * oq:512 * (oq + 1)],
                                         hnT[cb][:, 128 * t:128 * (t + 1)],
                                         wsb[cb][:, 512 * oq:512 * (oq + 1)],
                                         start=(cb == 0), stop=(cb == 3))
                te.mark(mi)

            @ch.step("vector")
            def _(v, sem=None):
                v.tensor_tensor(sout[:, :], psum[:, :], borep[:, :], ALU.add)

            @ch.step("sync")
            def _(sync, sem=None):
                sync.dma_start(dr["s0_scr"][128 * t:128 * (t + 1), :],
                               sout[:, :])


def _emit_mini_layer(nc, ch, st, dr, dd, hsb, cc, cmask):
    with ExitStack() as lst:
        sb = lambda n, sh, dt: lst.enter_context(nc.sbuf_tensor(_u(n), sh, dt))
        WfrT, WfiT, alt = cc["WfrT"], cc["WfiT"], cc["alt"]
        VcT, VsT, altl = cc["VcT"], cc["VsT"], cc["altl"]
        ident, OH = cc["ident"], cc["OH"]
        PIT = lambda t_: t_.shape[1]

        lnt = sb("ml_lnt", [128, MD], F32)
        sums = sb("ml_sums", [128, 1], F32)
        sumsq = sb("ml_sumsq", [128, 1], F32)
        nmu = sb("ml_nmu", [128, 1], F32)
        ve = sb("ml_ve", [128, 1], F32)
        rstd = sb("ml_rstd", [128, 1], F32)
        scr = sb("ml_scr", [128, MD], F32)
        scr0 = sb("ml_scr0", [128, 1], F32)
        pst = ExitStack()
        sv = lambda n, sh, dt: pst.enter_context(
            nc.sbuf_tensor(_u(n), sh, dt))
        lwrep = sv("ml_lw", [128, MD], F32)
        lbrep = sv("ml_lb", [128, MD], F32)
        sclrep = sv("ml_scl", [128, MD], F32)
        prep_r = sv("ml_pr", [128, MD], F32)
        prep_i = sv("ml_pi", [128, MD], F32)
        pz = sv("ml_pz", [128, 8], F32)
        psc = sv("ml_psc", [128, 8], F32)
        przr = sv("ml_przr", [1, MD], F32)
        przi = sv("ml_przi", [1, MD], F32)
        rowsq = sv("ml_rowsq", [1, MD], F32)
        rowtmp = sv("ml_rowtmp", [1, MD], F32)
        prowpr = sv("ml_prowpr", [1, MD], F32)
        prowpi = sv("ml_prowpi", [1, MD], F32)
        tr8 = sv("ml_tr8", [128, 8], F32)
        Pcm_r = sv("ml_Pcmr", [128, 4 * 132], F32)
        Pcm_i = sv("ml_Pcmi", [128, 4 * 132], F32)
        tmpA = sv("ml_tmpA", [128, 4 * 64], F32)
        tmpB = sv("ml_tmpB", [128, 4 * 64], F32)
        Pmc_r = sv("ml_Pmcr", [128, MD], F32)
        Pmc_i = sv("ml_Pmci", [128, MD], F32)
        Pmc_rbf = sv("ml_Pmcrb", [128, MD], BF16)
        Prev_r = sv("ml_Prevr", [128, MD], F32)
        Prev_i = sv("ml_Previ", [128, MD], F32)
        frrev = sv("ml_frrev", [128, MD], BF16)
        firev = sv("ml_firev", [128, MD], BF16)
        Frr = sv("ml_Frr", [128, MD], BF16)
        Fri = sv("ml_Fri", [128, MD], BF16)
        Frr128 = sv("ml_Frr128", [1, MD], BF16)
        Pp1r = sv("ml_Pp1r", [128, MD], BF16)
        Pp1i = sv("ml_Pp1i", [128, MD], BF16)
        tbig1 = sv("ml_tb1", [128, MD], F32)
        tbig2 = sv("ml_tb2", [128, MD], F32)
        P128 = sv("ml_P128", [128, 8], F32)
        P128p = [sv(f"ml_P128p{k}", [128, 8], F32) for k in range(4)]
        conv_sb = sv("ml_conv", [128, MD], F32)
        ysb = sv("ml_ysb", [128, MD], BF16)
        Yr_sb = sv("ml_Yr", [128, MD], BF16)
        Yi_sb = sv("ml_Yi", [128, MD], BF16)
        yr128 = sv("ml_yr128", [1, MD], BF16)
        Hr_sb = sv("ml_Hr", [128, MD], BF16)
        Hi_sb = sv("ml_Hi", [128, MD], BF16)
        Hr128 = sv("ml_Hr128", [1, MD], BF16)
        tre = sv("ml_tre", [128, MD], BF16)
        tim = sv("ml_tim", [128, MD], BF16)
        tA = sv("ml_tA", [128, MD], BF16)
        tB = sv("ml_tB", [128, MD], BF16)
        U_r = sv("ml_Ur", [128, 64], F32)
        U_i = sv("ml_Ui", [128, 64], F32)
        Urow_r = sv("ml_Urowr", [16, MD], F32)
        Urow_i = sv("ml_Urowi", [16, MD], F32)
        PPc_r = sv("ml_PPcr", [128, 64], F32)
        PPc_i = sv("ml_PPci", [128, 64], F32)
        ts1 = sv("ml_ts1", [128, 64], F32)
        ts2 = sv("ml_ts2", [128, 64], F32)
        ts3 = sv("ml_ts3", [128, 64], F32)
        ts4 = sv("ml_ts4", [128, 64], F32)
        csh_r = sv("ml_cshr", [128, 64], F32)
        csh_i = sv("ml_cshi", [128, 64], F32)
        crow_sb = sv("ml_crow", [16, 2 * MD], F32)
        ci_sb = sv("ml_ci", [128, 8], F32)
        ci2_sb = sv("ml_ci2", [128, 8], F32)
        cfin_sb = sv("ml_cfin", [128, 8], F32)
        cinj_sb = sv("ml_cinj", [128, 8], F32)
        crj_r = sv("ml_crjr", [128, MD], BF16)
        crj_i = sv("ml_crji", [128, MD], BF16)

        @ch.step("sync")
        def _(sync, sem=None):
            for dst_, src_ in [(lwrep, dr[f"lnw{dd}"]),
                               (lbrep, dr[f"lnb{dd}"]),
                               (sclrep, dr[f"scl{dd}"])]:
                sync.dma_start(dst_[:, :],
                               bass.AP(src_, 0, [[0, 128], [1, MD]]))
            sync.dma_start(pz[:, :], dr[f"phzs{dd}"][:, :])
            sync.dma_start(przr[:, :], dr[f"phzrow{dd}"][0:1, :])
            sync.dma_start(przi[:, :], dr[f"phzrow{dd}"][1:2, :])
            sync.dma_start(ci_sb[:, :], dr[f"cinit{dd}"][:, :])
            sync.dma_start(cinj_sb[:, :], dr[f"cinj{dd}"][:, :])

        psm = lambda n, shp: pst.enter_context(
            nc.psum_tensor(_u(n), shp, F32))
        Yr_ps = psm("ml_Yrps", [128, MD])
        Yi_ps = psm("ml_Yips", [128, MD])
        yrow_ps = psm("ml_yrps", [1, MD])
        conv_ps = psm("ml_cps", [128, MD])
        Ur_ps = psm("ml_Urps", [16, MD])
        Ui_ps = psm("ml_Uips", [16, MD])
        tr_ps = psm("ml_trps", [128, 128])

        @ch.step("vector")
        def _(v, sem=None):
            v.tensor_tensor(tr8[:, 0:4], pz[:, 0:4], pz[:, 0:4], ALU.mult)
            v.tensor_tensor(tr8[:, 4:8], pz[:, 4:8], pz[:, 4:8], ALU.mult)
            v.tensor_tensor(tr8[:, 0:4], tr8[:, 0:4], tr8[:, 4:8],
                            ALU.add)
            v.tensor_tensor(rowsq[:, :], przr[:, :], przr[:, :], ALU.mult)
            v.tensor_tensor(rowtmp[:, :], przi[:, :], przi[:, :],
                            ALU.mult)
            v.tensor_tensor(rowsq[:, :], rowsq[:, :], rowtmp[:, :],
                            ALU.add)

        @ch.step("scalar")
        def _(s, sem=None):
            s.activation(tr8[:, 4:8], tr8[:, 0:4], AF.Sqrt)
            s.activation(tr8[:, 0:4], tr8[:, 4:8], AF.Tanh)
            s.activation(rowsq[:, :], rowsq[:, :], AF.Sqrt)
            s.activation(rowtmp[:, :], rowsq[:, :], AF.Tanh)

        @ch.step("vector")
        def _(v, sem=None):
            v.reciprocal(tr8[:, 4:8], tr8[:, 4:8])
            v.tensor_tensor(tr8[:, 4:8], tr8[:, 4:8], tr8[:, 0:4],
                            ALU.mult)
            v.tensor_tensor(psc[:, 0:4], pz[:, 0:4], tr8[:, 4:8],
                            ALU.mult)
            v.tensor_tensor(psc[:, 4:8], pz[:, 4:8], tr8[:, 4:8],
                            ALU.mult)
            v.reciprocal(rowsq[:, :], rowsq[:, :])
            v.tensor_tensor(rowsq[:, :], rowsq[:, :], rowtmp[:, :],
                            ALU.mult)
            v.tensor_tensor(prowpr[:, :], przr[:, :], rowsq[:, :],
                            ALU.mult)
            v.tensor_tensor(prowpi[:, :], przi[:, :], rowsq[:, :],
                            ALU.mult)

        @ch.step("gpsimd")
        def _(g, sem=None):
            g.dma_start(dr["prow_scr"][0:1, :], prowpr[:, :])
            g.dma_start(dr["prow_scr"][1:2, :], prowpi[:, :])

        @ch.step("gpsimd")
        def _(g, sem=None):
            g.dma_start(prep_r[:, :],
                        bass.AP(dr["prow_scr"], 0, [[0, 128], [1, MD]]))
            g.dma_start(prep_i[:, :],
                        bass.AP(dr["prow_scr"], MD, [[0, 128], [1, MD]]))

        @ch.step("vector")
        def _(v, sem=None):
            for cb in range(4):
                o = cb * 132
                v.memset(Pcm_r[:, o:o + 1], 1.0)
                v.memset(Pcm_i[:, o:o + 1], 0.0)
                v.tensor_copy(Pcm_r[:, o + 1:o + 2], psc[:, cb:cb + 1])
                v.tensor_copy(Pcm_i[:, o + 1:o + 2],
                              psc[:, 4 + cb:5 + cb])
            w = 1
            for k in range(7):
                sl = lambda tsr, off, cnt: ap(tsr, off, (PIT(tsr), 128),
                                              (132, 4), (1, cnt))
                Ar, Ai = sl(Pcm_r, 1, w), sl(Pcm_i, 1, w)
                Br = ap(Pcm_r, w, (PIT(Pcm_r), 128), (132, 4), (0, w))
                Bi = ap(Pcm_i, w, (PIT(Pcm_i), 128), (132, 4), (0, w))
                Or, Oi = sl(Pcm_r, w + 1, w), sl(Pcm_i, w + 1, w)
                t1 = ap(tmpA, 0, (PIT(tmpA), 128), (64, 4), (1, w))
                t2 = ap(tmpB, 0, (PIT(tmpB), 128), (64, 4), (1, w))
                v.tensor_tensor(t1, Ar, Br, ALU.mult)
                v.tensor_tensor(t2, Ai, Bi, ALU.mult)
                v.tensor_tensor(Or, t1, t2, ALU.subtract)
                v.tensor_tensor(t1, Ar, Bi, ALU.mult)
                v.tensor_tensor(t2, Ai, Br, ALU.mult)
                v.tensor_tensor(Oi, t1, t2, ALU.add)
                w *= 2

        for cb in range(4):
            for ri in range(2):
                @ch.step("tensor")
                def _(te, sem=None):
                    src = [Pcm_r, Pcm_i][ri]
                    te.transpose(tr_ps[:, :],
                                 src[:, cb * 132:cb * 132 + 128],
                                 ident[:, :])

                @ch.step("vector")
                def _(v, sem=None):
                    dst = [Pmc_r, Pmc_i][ri]
                    v.tensor_copy(dst[:, cb * 128:(cb + 1) * 128],
                                  tr_ps[:, :])

        @ch.step("vector")
        def _(v, sem=None):
            v.tensor_copy(ap(Prev_r, 0, (PIT(Prev_r), 128), (128, 4),
                             (1, 128)),
                          ap(Pcm_r, 127, (PIT(Pcm_r), 128), (132, 4),
                             (-1, 128)))
            v.tensor_copy(ap(Prev_i, 0, (PIT(Prev_i), 128), (128, 4),
                             (1, 128)),
                          ap(Pcm_i, 127, (PIT(Pcm_i), 128), (132, 4),
                             (-1, 128)))

        for cb in range(4):
            for ri in range(2):
                @ch.step("tensor")
                def _(te, sem=None):
                    src = [Prev_r, Prev_i][ri]
                    te.transpose(tr_ps[:, :],
                                 src[:, cb * 128:(cb + 1) * 128],
                                 ident[:, :])

                @ch.step("vector")
                def _(v, sem=None):
                    dst = [frrev, firev][ri]
                    v.tensor_copy(dst[:, cb * 128:(cb + 1) * 128],
                                  tr_ps[:, :])

        @ch.step("vector")
        def _(v, sem=None):
            v.tensor_copy(Pmc_rbf[:, :], Pmc_r[:, :])

        @ch.step("tensor")
        def _(te, sem=None):
            te.matmul_q(Yr_ps[:, :], WfrT[:, :], Pmc_rbf[:, :],
                        start=True, stop=True)
            te.matmul_q(Yi_ps[:, :], WfiT[:, :], Pmc_rbf[:, :],
                        start=True, stop=True)
            te.matmul(yrow_ps[0:1, :], alt[:, :], Pmc_rbf[:, :],
                      start=True, stop=True)

        @ch.step("vector")
        def _(v, sem=None):
            v.tensor_tensor(Frr[:, :], Yr_ps[:, :], sclrep[:, :],
                            ALU.mult)
            v.tensor_tensor(Fri[:, :], Yi_ps[:, :], sclrep[:, :],
                            ALU.mult)
            v.tensor_tensor(Frr128[0:1, :], yrow_ps[0:1, :],
                            sclrep[0:1, :], ALU.mult)

        @ch.step("vector")
        def _(v, sem=None):
            v.tensor_tensor(tbig1[:, :], Pmc_r[:, :], prep_r[:, :],
                            ALU.mult)
            v.tensor_tensor(tbig2[:, :], Pmc_i[:, :], prep_i[:, :],
                            ALU.mult)
            v.tensor_tensor(tbig1[:, :], tbig1[:, :], tbig2[:, :],
                            ALU.subtract)
            v.tensor_tensor(scr[:, :], Pmc_r[:, :], prep_i[:, :],
                            ALU.mult)
            v.tensor_tensor(tbig2[:, :], Pmc_i[:, :], prep_r[:, :],
                            ALU.mult)
            v.tensor_tensor(tbig2[:, :], scr[:, :], tbig2[:, :], ALU.add)
            v.tensor_tensor(Pp1r[:, :], tbig1[:, :], sclrep[:, :],
                            ALU.mult)
            v.tensor_tensor(Pp1i[:, :], tbig2[:, :], sclrep[:, :],
                            ALU.mult)
            v.tensor_copy(P128[:, 0:4],
                          ap(Pcm_r, 128, (PIT(Pcm_r), 128), (132, 4)))
            v.tensor_copy(P128[:, 4:8],
                          ap(Pcm_i, 128, (PIT(Pcm_i), 128), (132, 4)))
            v.tensor_copy(P128p[0][:, :], P128[:, :])
            for k in range(1, 4):
                a = P128p[k - 1]
                o = P128p[k]
                v.tensor_tensor(tr8[:, 0:4], a[:, 0:4], a[:, 0:4],
                                ALU.mult)
                v.tensor_tensor(tr8[:, 4:8], a[:, 4:8], a[:, 4:8],
                                ALU.mult)
                v.tensor_tensor(o[:, 0:4], tr8[:, 0:4], tr8[:, 4:8],
                                ALU.subtract)
                v.tensor_tensor(tr8[:, 0:4], a[:, 0:4], a[:, 4:8],
                                ALU.mult)
                v.tensor_tensor(o[:, 4:8], tr8[:, 0:4], tr8[:, 0:4],
                                ALU.add)

        for seg in range(2):
            j0 = seg * 16
            cinit = ci_sb if seg == 0 else ci2_sb
            for jl in range(16):
                j = j0 + jl

                @ch.step("vector")
                def _(v, sem=None):
                    v.tensor_scalar(scr[:, :], hsb[j][:, :], 1.0, 0.0,
                                    ALU.mult, ALU.add,
                                    accum_out=sums[:, :])
                    v.scalar_tensor_tensor(scr[:, :], hsb[j][:, :], 1.0,
                                           hsb[j][:, :], ALU.mult,
                                           ALU.mult,
                                           accum_out=sumsq[:, :])
                    v.tensor_scalar(nmu[:, :], sums[:, :], -1.0 / MD,
                                    None, ALU.mult)
                    v.tensor_scalar(ve[:, :], sumsq[:, :], 1.0 / MD, EPS,
                                    ALU.mult, ALU.add)
                    v.tensor_tensor(scr0[:, :], nmu[:, :], nmu[:, :],
                                    ALU.mult)
                    v.tensor_tensor(ve[:, :], ve[:, :], scr0[:, :],
                                    ALU.subtract)
                    v.reciprocal(ve[:, :], ve[:, :])

                @ch.step("scalar")
                def _(s, sem=None):
                    s.activation(rstd[:, :], ve[:, :], AF.Sqrt)

                @ch.step("vector")
                def _(v, sem=None):
                    v.tensor_scalar(lnt[:, :], hsb[j][:, :], nmu[:, 0:1],
                                    rstd[:, 0:1], ALU.add, ALU.mult)
                    v.tensor_tensor(lnt[:, :], lnt[:, :], lwrep[:, :],
                                    ALU.mult)
                    v.tensor_tensor(lnt[:, :], lnt[:, :], lbrep[:, :],
                                    ALU.add)

                @ch.step("scalar")
                def _(s, sem=None):
                    s.activation(ysb[:, :], lnt[:, :], AF.Silu)

                @ch.step("tensor")
                def _(te, sem=None):
                    te.matmul_q(Yr_ps[:, :], WfrT[:, :], ysb[:, :],
                                start=True, stop=True)
                    te.matmul_q(Yi_ps[:, :], WfiT[:, :], ysb[:, :],
                                start=True, stop=True)
                    te.matmul(yrow_ps[0:1, :], alt[:, :], ysb[:, :],
                              start=True, stop=True)

                @ch.step("scalar")
                def _(s, sem=None):
                    s.copy(Yr_sb[:, :], Yr_ps[:, :])
                    s.copy(Yi_sb[:, :], Yi_ps[:, :])
                    s.copy(yr128[0:1, :], yrow_ps[0:1, :])

                @ch.step("vector")
                def _(v, sem=None):
                    v.tensor_tensor(tA[:, :], Yr_sb[:, :], Frr[:, :],
                                    ALU.mult)
                    v.tensor_tensor(tB[:, :], Yi_sb[:, :], Fri[:, :],
                                    ALU.mult)
                    v.tensor_tensor(Hr_sb[:, :], tA[:, :], tB[:, :],
                                    ALU.subtract)
                    v.tensor_tensor(tA[:, :], Yr_sb[:, :], Fri[:, :],
                                    ALU.mult)
                    v.tensor_tensor(tB[:, :], Yi_sb[:, :], Frr[:, :],
                                    ALU.mult)
                    v.tensor_tensor(Hi_sb[:, :], tA[:, :], tB[:, :],
                                    ALU.add)
                    v.tensor_tensor(Hr128[0:1, :], yr128[0:1, :],
                                    Frr128[0:1, :], ALU.mult)
                    v.tensor_tensor(tre[:, :], ysb[:, :], frrev[:, :],
                                    ALU.mult)
                    v.tensor_tensor(tim[:, :], ysb[:, :], firev[:, :],
                                    ALU.mult)

                @ch.step("tensor")
                def _(te, sem=None):
                    te.matmul_q(Ur_ps[0:16, :],
                                OH[:, jl * 16:(jl + 1) * 16],
                                tre[:, :], start=(jl == 0),
                                stop=(jl == 15), skip_group_check=True)
                    te.matmul_q(Ui_ps[0:16, :],
                                OH[:, jl * 16:(jl + 1) * 16],
                                tim[:, :], start=(jl == 0),
                                stop=(jl == 15), skip_group_check=True)
                    te.matmul_q(conv_ps[:, :], VcT[:, :], Hr_sb[:, :],
                                start=True, stop=False,
                                skip_group_check=True)
                    te.matmul_q(conv_ps[:, :], VsT[:, :], Hi_sb[:, :],
                                start=False, stop=False,
                                skip_group_check=True)
                    te.matmul(conv_ps[:, :], altl[0:1, :], Hr128[0:1, :],
                              start=False, stop=True,
                              skip_group_check=True)

                @ch.step("scalar")
                def _(s, sem=None):
                    s.copy(conv_sb[:, :], conv_ps[:, :])

                @ch.step("vector")
                def _(v, sem=None):
                    v.tensor_tensor(hsb[j][:, :], hsb[j][:, :],
                                    conv_sb[:, :], ALU.add)

            @ch.step("vector")
            def _(v, sem=None):
                v.tensor_copy(Urow_r[:, :], Ur_ps[:, :])
                v.tensor_copy(Urow_i[:, :], Ui_ps[:, :])

            for cb in range(4):
                for ri in range(2):
                    @ch.step("tensor")
                    def _(te, sem=None):
                        src = [Urow_r, Urow_i][ri]
                        te.transpose(tr_ps[0:128, 0:16],
                                     src[0:16, cb * 128:(cb + 1) * 128],
                                     ident[0:16, 0:16])

                    @ch.step("vector")
                    def _(v, sem=None):
                        dst = [U_r, U_i][ri]
                        v.tensor_copy(dst[:, cb * 16:(cb + 1) * 16],
                                      tr_ps[0:128, 0:16])

            @ch.step("vector")
            def _(v, sem=None):
                w = 1
                for k in range(4):
                    n = 16 - w
                    sl = lambda tsr, off: ap(tsr, off, (PIT(tsr), 128),
                                             (16, 4), (1, n))
                    zsr, zsi = sl(U_r, 0), sl(U_i, 0)
                    tgr, tgi = sl(U_r, w), sl(U_i, w)
                    Pr = ap(P128p[k], 0, (8, 128), (1, 4), (0, n))
                    Pi = ap(P128p[k], 4, (8, 128), (1, 4), (0, n))
                    t1 = ap(ts1, 0, (PIT(ts1), 128), (16, 4), (1, n))
                    t2 = ap(ts2, 0, (PIT(ts2), 128), (16, 4), (1, n))
                    t3 = ap(ts3, 0, (PIT(ts3), 128), (16, 4), (1, n))
                    t4 = ap(ts4, 0, (PIT(ts4), 128), (16, 4), (1, n))
                    v.tensor_tensor(t1, zsr, Pr, ALU.mult)
                    v.tensor_tensor(t2, zsi, Pi, ALU.mult)
                    v.tensor_tensor(t3, zsr, Pi, ALU.mult)
                    v.tensor_tensor(t4, zsi, Pr, ALU.mult)
                    v.tensor_tensor(t1, t1, t2, ALU.subtract)
                    v.tensor_tensor(t3, t3, t4, ALU.add)
                    v.tensor_tensor(tgr, tgr, t1, ALU.add)
                    v.tensor_tensor(tgi, tgi, t3, ALU.add)
                    w *= 2
                v.tensor_copy(ap(PPc_r, 0, (PIT(PPc_r), 128), (16, 4)),
                              P128[:, 0:4])
                v.tensor_copy(ap(PPc_i, 0, (PIT(PPc_i), 128), (16, 4)),
                              P128[:, 4:8])
                w = 1
                for k in range(4):
                    n = min(2 * w, 16) - w
                    sl2 = lambda tsr, off: ap(tsr, off, (PIT(tsr), 128),
                                              (16, 4), (1, n))
                    ar, ai = sl2(PPc_r, 0), sl2(PPc_i, 0)
                    tgr, tgi = sl2(PPc_r, w), sl2(PPc_i, w)
                    Pr = ap(P128p[k], 0, (8, 128), (1, 4), (0, n))
                    Pi = ap(P128p[k], 4, (8, 128), (1, 4), (0, n))
                    t1 = ap(ts1, 0, (PIT(ts1), 128), (16, 4), (1, n))
                    t2 = ap(ts2, 0, (PIT(ts2), 128), (16, 4), (1, n))
                    v.tensor_tensor(t1, ar, Pr, ALU.mult)
                    v.tensor_tensor(t2, ai, Pi, ALU.mult)
                    v.tensor_tensor(tgr, t1, t2, ALU.subtract)
                    v.tensor_tensor(t1, ar, Pi, ALU.mult)
                    v.tensor_tensor(t2, ai, Pr, ALU.mult)
                    v.tensor_tensor(tgi, t1, t2, ALU.add)
                    w *= 2
                cir = ap(cinit, 0, (8, 128), (1, 4), (0, 16))
                cii = ap(cinit, 4, (8, 128), (1, 4), (0, 16))
                sl3 = lambda tsr: ap(tsr, 0, (PIT(tsr), 128), (16, 4),
                                     (1, 16))
                v.tensor_tensor(sl3(ts1), sl3(PPc_r), cir, ALU.mult)
                v.tensor_tensor(sl3(ts2), sl3(PPc_i), cii, ALU.mult)
                v.tensor_tensor(sl3(ts1), sl3(ts1), sl3(ts2),
                                ALU.subtract)
                v.tensor_tensor(sl3(U_r), sl3(U_r), sl3(ts1), ALU.add)
                v.tensor_tensor(sl3(ts1), sl3(PPc_r), cii, ALU.mult)
                v.tensor_tensor(sl3(ts2), sl3(PPc_i), cir, ALU.mult)
                v.tensor_tensor(sl3(ts1), sl3(ts1), sl3(ts2), ALU.add)
                v.tensor_tensor(sl3(U_i), sl3(U_i), sl3(ts1), ALU.add)
                v.tensor_copy(cfin_sb[:, 0:4],
                              ap(U_r, 15, (PIT(U_r), 128), (16, 4)))
                v.tensor_copy(cfin_sb[:, 4:8],
                              ap(U_i, 15, (PIT(U_i), 128), (16, 4)))
                v.tensor_copy(ap(csh_r, 0, (PIT(csh_r), 128), (16, 4)),
                              cinit[:, 0:4])
                v.tensor_copy(ap(csh_i, 0, (PIT(csh_i), 128), (16, 4)),
                              cinit[:, 4:8])
                v.tensor_copy(ap(csh_r, 1, (PIT(csh_r), 128), (16, 4),
                                 (1, 15)),
                              ap(U_r, 0, (PIT(U_r), 128), (16, 4),
                                 (1, 15)))
                v.tensor_copy(ap(csh_i, 1, (PIT(csh_i), 128), (16, 4),
                                 (1, 15)),
                              ap(U_i, 0, (PIT(U_i), 128), (16, 4),
                                 (1, 15)))
                if seg == 0:
                    v.tensor_tensor(ci2_sb[:, :], cfin_sb[:, :],
                                    cmask[:, :], ALU.mult)
                    v.tensor_tensor(ci2_sb[:, :], ci2_sb[:, :],
                                    cinj_sb[:, :], ALU.add)

            for cb in range(4):
                for ri in range(2):
                    @ch.step("tensor")
                    def _(te, sem=None):
                        src = [csh_r, csh_i][ri]
                        te.transpose(tr_ps[0:16, 0:128],
                                     src[:, cb * 16:(cb + 1) * 16],
                                     ident[:, :])

                    @ch.step("vector")
                    def _(v, sem=None):
                        v.tensor_copy(
                            crow_sb[0:16, ri * MD + cb * 128:
                                    ri * MD + (cb + 1) * 128],
                            tr_ps[0:16, 0:128])

            @ch.step("sync")
            def _(sync, sem=None):
                sync.dma_start(
                    ap(dr["crow_scr"], 0, (512, 16), (8192, 2), (1, 512)),
                    ap(crow_sb, 0, (PIT(crow_sb), 16), (MD, 2), (1, MD)))

            for jl in range(16):
                j = j0 + jl

                @ch.step("gpsimd")
                def _(g, sem=None):
                    g.dma_start(crj_r[:, :],
                                bass.AP(dr["crow_scr"], jl * MD,
                                        [[0, 128], [1, MD]]))
                    g.dma_start(crj_i[:, :],
                                bass.AP(dr["crow_scr"], 8192 + jl * MD,
                                        [[0, 128], [1, MD]]))

                @ch.step("vector")
                def _(v, sem=None):
                    v.tensor_tensor(tbig1[:, :], crj_r[:, :], Pp1r[:, :],
                                    ALU.mult)
                    v.tensor_tensor(tbig2[:, :], crj_i[:, :], Pp1i[:, :],
                                    ALU.mult)
                    v.tensor_tensor(tbig1[:, :], tbig1[:, :], tbig2[:, :],
                                    ALU.subtract)
                    v.tensor_tensor(hsb[j][:, :], hsb[j][:, :],
                                    tbig1[:, :], ALU.add)

        pst.close()

        lw2 = sb("ml_lw2", [128, MD], F32)
        lb2 = sb("ml_lb2", [128, MD], F32)
        b2rep = sb("ml_b2", [128, MD], F32)
        b1sb = sb("ml_b1", [128, 16], F32)
        lny = sb("ml_lny", [128, MD], BF16)
        lnyT = [sb(f"ml_lnyT{cb}", [128, TH], BF16) for cb in range(4)]
        w1sb = [sb(f"ml_w1{cb}", [128, MF], BF16) for cb in range(4)]
        w2sb = [sb(f"ml_w2{f}", [128, MD], BF16) for f in range(16)]
        y2sb = [sb(f"ml_y2{f}", [128, 512], BF16) for f in range(16)]
        ffn_ps = [lst.enter_context(nc.psum_tensor(_u("ml_fps"), [128, 512],
                                                   F32)) for _i in range(2)]
        ffh_ps = [lst.enter_context(nc.psum_tensor(_u("ml_fhps"), [128, MD],
                                                   F32)) for _i in range(2)]
        tmp = (scr, sums, sumsq, nmu, ve, rstd, scr0, lnt)

        @ch.step("sync")
        def _(sync, sem=None):
            for dst_, src_ in [(lw2, dr[f"ln2w{dd}"]), (lb2, dr[f"ln2b{dd}"]),
                               (b2rep, dr[f"b2_{dd}"])]:
                sync.dma_start(dst_[:, :],
                               bass.AP(src_, 0, [[0, 128], [1, MD]]))
            sync.dma_start(b1sb[:, :], dr[f"b1c{dd}"][:, :])

        @ch.step("gpsimd")
        def _(g, sem=None):
            for cb in range(4):
                g.dma_start(w1sb[cb][:, :],
                            dr[f"W1T{dd}"][128 * cb:128 * (cb + 1), :])
            for f in range(16):
                g.dma_start(w2sb[f][:, :],
                            dr[f"W2T{dd}"][128 * f:128 * (f + 1), :])

        for j in range(NT):
            emit_wln_chunk(ch, hsb[j], lw2, lb2, lny, tmp)

            @ch.step("sync")
            def _(sync, sem=None):
                sync.dma_start(dr["lny_scr"][128 * j:128 * (j + 1), :],
                               lny[:, :])

        for th2 in range(2):
            t0 = th2 * TH

            @ch.step("sync")
            def _(sync, sem=None):
                for cb in range(4):
                    sync.dma_start(lnyT[cb][:, :],
                                   dr["lny_scr"][t0:t0 + TH,
                                                 128 * cb:128 * (cb + 1)],
                                   transpose=True)

            for tb in range(4):
                c0 = tb * 512
                for f in range(16):
                    @ch.step("tensor")
                    def _(te, sem=None):
                        for cb in range(4):
                            mi = te.matmul_q(
                                ffn_ps[f % 2][:, :],
                                w1sb[cb][:, 128 * f:128 * (f + 1)],
                                lnyT[cb][:, c0:c0 + 512],
                                start=(cb == 0), stop=(cb == 3))
                        te.mark(mi)

                    @ch.step("scalar")
                    def _(s, sem=None):
                        s.activation(y2sb[f][:, :], ffn_ps[f % 2][:, :],
                                     AF.Silu, bias=b1sb[:, f:f + 1])

                for tt in range(4):
                    j = th2 * 16 + tb * 4 + tt

                    @ch.step("tensor")
                    def _(te, sem=None):
                        for f in range(16):
                            mi = te.matmul_q(
                                ffh_ps[tt % 2][:, :],
                                y2sb[f][:, 128 * tt:128 * (tt + 1)],
                                w2sb[f][:, :],
                                start=(f == 0), stop=(f == 15))
                        te.mark(mi)

                    @ch.step("vector")
                    def _(v, sem=None):
                        v.tensor_tensor(scr[:, :], ffh_ps[tt % 2][:, :],
                                        b2rep[:, :], ALU.add)
                        v.tensor_tensor(hsb[j][:, :], hsb[j][:, :], scr[:, :],
                                        ALU.add)


def emit_combine(nc, ch, st, xsrc, s_scr, gT_scr, xdst, ntiles,
                 xrow0, srow0, gcol0):
    sb = lambda n, sh, dt: st.enter_context(nc.sbuf_tensor(_u(n), sh, dt))
    NB = 3
    xt = [sb("cb_x", [128, D], F32) for _i in range(NB)]
    st_ = [sb("cb_s", [128, D], BF16) for _i in range(NB)]
    gt = [sb("cb_g", [128, D], BF16) for _i in range(NB)]
    pt = sb("cb_p", [128, D], F32)
    ld = {}
    cmp_ = {}
    stq = {}
    for j in range(ntiles):
        b = j % NB

        @ch.step("sync", at=smax(cmp_.get(j - NB), stq.get(j - NB)))
        def _(sync, sem=None):
            sync.dma_start(gt[b][:, :],
                           gT_scr[:, gcol0 + 128 * j:gcol0 + 128 * (j + 1)],
                           transpose=True)
            sync.dma_start(st_[b][:, :],
                           s_scr[srow0 + 128 * j:srow0 + 128 * (j + 1), :])

        @ch.step("gpsimd", at=smax(cmp_.get(j - NB), stq.get(j - NB)))
        def _(g, sem=None):
            g.dma_start(xt[b][:, :],
                        xsrc[xrow0 + 128 * j:xrow0 + 128 * (j + 1), :])
        ld[j] = ch.snap()

        @ch.step("vector", at=ld[j])
        def _(v, sem=None):
            v.tensor_tensor(pt[:, :], st_[b][:, :], gt[b][:, :], ALU.mult)
            v.tensor_tensor(xt[b][:, :], xt[b][:, :], pt[:, :], ALU.add)
        cmp_[j] = ch.snap()

        @ch.step("gpsimd", at=cmp_[j])
        def _(g, sem=None):
            g.dma_start(xdst[128 * j:128 * (j + 1), :], xt[b][:, :])
        stq[j] = ch.snap()


def emit_qkv(nc, ch, st, dr):
    sb = lambda n, sh, dt: st.enter_context(nc.sbuf_tensor(_u(n), sh, dt))
    NV = 32
    xsb = [sb(f"qk_x{v}", [128, TH], BF16) for v in range(NV)]
    wc = sb("qk_wc", [128, D], BF16)
    qout = sb("qk_qout", [128, TH], BF16)
    q_ps = st.enter_context(nc.psum_tensor(_u("qk_ps"), [128, TH], F32))
    for th in range(2):
        t0 = th * TH

        @ch.step("sync")
        def _(sync, sem=None):
            for v in range(NV):
                sync.dma_start(xsb[v][:, :],
                               dr["xn1_scr"][t0:t0 + TH,
                                             128 * v:128 * (v + 1)],
                               transpose=True)

        ocs = [32, 33] if th == 0 else [32, 33] + list(range(32))
        for oc in ocs:
            @ch.step("gpsimd")
            def _(gp, sem=None):
                if oc < 32:
                    gp.dma_start(ap(wc, 0, (D, 128), (128, 32), (1, 128)),
                                 ap(dr["Wq"], oc * 128, (D, 128),
                                    (128 * D, 32), (1, 128)))
                else:
                    o2 = oc - 32
                    gp.dma_start(ap(wc, 0, (D, 128), (128, 32), (1, 128)),
                                 ap(dr["Wkv"], o2 * 128, (256, 128),
                                    (128 * 256, 32), (1, 128)))

            @ch.step("tensor")
            def _(te, sem=None):
                for tq in range(4):
                    for g in range(NV):
                        mi = te.matmul_q(q_ps[:, 512 * tq:512 * (tq + 1)],
                                         wc[:, 128 * g:128 * (g + 1)],
                                         xsb[g][:, 512 * tq:512 * (tq + 1)],
                                         start=(g == 0), stop=(g == NV - 1),
                                         skip_group_check=True)
                te.mark(mi)

            @ch.step("vector")
            def _(v, sem=None):
                v.tensor_copy(qout[:, :], q_ps[:, :])

            @ch.step("sync")
            def _(sync, sem=None):
                if oc < 32:
                    sync.dma_start(dr["qt_scr"][128 * oc:128 * (oc + 1), :],
                                   qout[:, :])
                else:
                    o2 = oc - 32
                    sync.dma_start(dr["kvt_scr"][128 * o2:128 * (o2 + 1),
                                                 t0:t0 + TH], qout[:, :])


def emit_attn(nc, ch, st, dr, slopes_np):
    sb = lambda n, sh, dt: st.enter_context(nc.sbuf_tensor(_u(n), sh, dt))
    heads = 32
    qsb = [sb(f"at_q{h}", [128, TH], BF16) for h in range(heads)]
    ksb = sb("at_k", [128, T], BF16)
    vtk = [sb(f"at_v{kt}", [128, 128], BF16) for kt in range(NT)]
    slrep = sb("at_sl", [128, heads], F32)
    pmrep = sb("at_pm", [128, NT], F32)
    iota = sb("at_iota", [128, 1], F32)
    onescol = sb("at_ones", [128, 1], BF16)
    bias_v = sb("at_bias", [128, 16], F32)
    probs = [sb(f"at_probs{i}", [128, 512], BF16) for i in range(2)]
    msc = [sb(f"at_msc{i}", [128, 512], F32) for i in range(2)]
    maskfull = [sb(f"at_mf{i}", [128, 512], F32) for i in range(4)]
    onum_bf = [sb(f"at_onum{i}", [128, 512], BF16) for i in range(2)]
    den_sb = [sb(f"at_den{i}", [1, 512], F32) for i in range(2)]
    sc_ps = [st.enter_context(nc.psum_tensor(_u("at_scps"), [128, 512], F32))
             for _i in range(2)]
    av_ps = [st.enter_context(nc.psum_tensor(_u("at_avps"), [128, 512], F32))
             for _i in range(2)]
    den_ps = [st.enter_context(nc.psum_tensor(_u("at_dps"), [1, 512], F32))
              for _i in range(2)]

    iota_d = nc.inline_tensor(np.arange(128, dtype=np.float32)[:, None],
                              "at_iota_d")
    ones_d = nc.inline_tensor(np.asarray(np.ones((128, 1), dtype=BF)),
                              "at_ones_d")
    mf_np = []
    for d0 in (0, 128, 256, 384):
        p = np.arange(128)[:, None]
        c = np.arange(512)[None, :]
        keep = (c >= d0 + 128) | ((c >= d0) & (p <= c - d0))
        mf_np.append(np.where(keep, 0.0, NEG).astype(np.float32))
    mf_d = [nc.inline_tensor(m, f"at_mf_d{i}") for i, m in enumerate(mf_np)]

    @ch.step("sync")
    def _(sync, sem=None):
        for h in range(heads):
            sync.dma_start(qsb[h][:, :],
                           dr["qt_scr"][128 * h:128 * (h + 1), :])
        sync.dma_start(ksb[:, :], dr["kvt_scr"][0:128, :])
        for kt in range(NT):
            sync.dma_start(vtk[kt][:, :],
                           dr["kvt_scr"][128:256, 128 * kt:128 * (kt + 1)],
                           transpose=True)
        sync.dma_start(slrep[:, :],
                       bass.AP(dr["slopes"], 0, [[0, 128], [1, heads]]))
        sync.dma_start(pmrep[:, :], dr["pmask"][:, :])
        sync.dma_start(iota[:, :], iota_d[:, :])
        sync.dma_start(onescol[:, :], ones_d[:, :])
        for i in range(4):
            sync.dma_start(maskfull[i][:, :], mf_d[i][:, :])

    ct0, dt0, gt0 = ch.snap()
    A = nc.alloc_semaphore(_u("at_A"))
    E = nc.alloc_semaphore(_u("at_E"))
    Vm = nc.alloc_semaphore(_u("at_V"))
    S = nc.alloc_semaphore(_u("at_S"))

    iters = []
    hcs = []
    for h in range(heads):
        sl = float(slopes_np[h])
        for ebw in (512, 256, 128, 64):
            if sl * (ebw - 1.0) <= WTHRESH or ebw == 64:
                break
        nex = 512 // ebw
        for qc in range(4):
            q0 = qc * 512
            qglob0 = TH + q0
            kts = []
            for kt in range(NT):
                kb = kt * 128
                if kb > qglob0 + 511:
                    continue
                if sl * (qglob0 - (kb + 127)) > WTHRESH:
                    continue
                kts.append(kt)
            hc = len(hcs)
            g0 = len(iters)
            for ki, kt in enumerate(kts):
                kb = kt * 128
                iters.append(dict(
                    h=h, qc=qc, ki=ki, kt=kt, kb=kb, hc=hc,
                    first=(ki == 0), last=(ki == len(kts) - 1),
                    diag=(kb >= qglob0), d0=kb - qglob0,
                    ebw=ebw, nex=nex, q0=q0, qglob0=qglob0))
            hcs.append(dict(h=h, qc=qc, n=len(kts), g0=g0, q0=q0))
    G = len(iters)
    NHC = len(hcs)

    a_score = [0] * G
    a_den = [0] * G
    e_after = [0] * G
    v_after = [0] * G
    v_copy = [0] * NHC
    s_store = [0] * NHC
    a = 0
    a += 1
    a_score[0] = a
    for g in range(G):
        if g + 1 < G:
            a += 1
            a_score[g + 1] = a
        a += 2
        a_den[g] = a
    a_tot = a
    e = 0
    for g in range(G):
        e += iters[g]["nex"]
        e_after[g] = e
    e_tot = e
    v = 0
    for g in range(G):
        v += 2 * iters[g]["nex"] + (1 if iters[g]["diag"] else 0)
        v_after[g] = v
        if iters[g]["last"]:
            v += 2
            v_copy[iters[g]["hc"]] = v
    v_tot = v
    s = 0
    for hc in range(NHC):
        s += 32
        s_store[hc] = s
    s_tot = s

    def _entry(eng):
        if ct0 > 0:
            eng.wait_ge(ch.csem, ct0)
        if dt0 > 0:
            eng.wait_ge(ch.dsem, dt0)
        if gt0 > 0:
            eng.wait_ge(ch.gsem, gt0)

    def tensor_body(te):
        _entry(te)

        def score(g):
            it = iters[g]
            if g >= 2:
                te.wait_ge(E, e_after[g - 2])
            te.matmul(sc_ps[g % 2][:, :], ksb[:, it["kb"]:it["kb"] + 128],
                      qsb[it["h"]][:, it["q0"]:it["q0"] + 512],
                      start=True, stop=True,
                      skip_group_check=True).then_inc(A, 1)

        def av(g):
            it = iters[g]
            te.wait_ge(E, e_after[g])
            hc = it["hc"]
            if it["first"] and hc >= 2:
                te.wait_ge(Vm, v_copy[hc - 2])
            te.matmul(av_ps[hc % 2][:, :], vtk[it["kt"]][:, :],
                      probs[g % 2][:, :], start=it["first"], stop=False,
                      skip_group_check=True).then_inc(A, 1)
            te.matmul(den_ps[hc % 2][0:1, :], onescol[:, :],
                      probs[g % 2][:, :], start=it["first"], stop=False,
                      skip_group_check=True).then_inc(A, 1)

        score(0)
        for g in range(G):
            if g + 1 < G:
                score(g + 1)
            av(g)
    ch.block.tensor(tensor_body)

    def vector_body(v):
        _entry(v)
        vc = 0
        for g in range(G):
            it = iters[g]
            if g >= 2:
                v.wait_ge(E, e_after[g - 2])
            bb = (g % 2) * 8
            for ce in range(it["nex"]):
                c = ce * it["ebw"]
                col = bias_v[:, bb + ce:bb + ce + 1]
                v.tensor_scalar(col, iota[:, :],
                                float(it["kb"] - it["qglob0"] - c),
                                slrep[:, it["h"]:it["h"] + 1],
                                ALU.add, ALU.mult).then_inc(Vm, 1)
                vc += 1
                v.wait_ge(Vm, vc)
                v.tensor_tensor(col, col, pmrep[:, it["kt"]:it["kt"] + 1],
                                ALU.add).then_inc(Vm, 1)
                vc += 1
            if it["diag"]:
                v.wait_ge(A, a_score[g])
                v.tensor_tensor(msc[g % 2][:, :], sc_ps[g % 2][:, :],
                                maskfull[it["d0"] // 128][:, :],
                                ALU.add).then_inc(Vm, 1)
                vc += 1
            if it["last"]:
                hc = it["hc"]
                v.wait_ge(A, a_den[g])
                if hc >= 2:
                    v.wait_ge(S, s_store[hc - 2])
                v.tensor_copy(den_sb[hc % 2][0:1, :],
                              den_ps[hc % 2][0:1, :]).then_inc(Vm, 1)
                v.tensor_copy(onum_bf[hc % 2][:, :],
                              av_ps[hc % 2][:, :]).then_inc(Vm, 1)
                vc += 2
    ch.block.vector(vector_body)

    def scalar_body(s):
        _entry(s)
        for g in range(G):
            it = iters[g]
            s.wait_ge(A, a_score[g])
            s.wait_ge(Vm, v_after[g])
            bb = (g % 2) * 8
            src = msc[g % 2] if it["diag"] else sc_ps[g % 2]
            for ce in range(it["nex"]):
                c = ce * it["ebw"]
                s.activation(probs[g % 2][:, c:c + it["ebw"]],
                             src[:, c:c + it["ebw"]], AF.Exp,
                             bias=bias_v[:, bb + ce:bb + ce + 1],
                             scale=ATT_SCALE).then_inc(E, 1)
    ch.block.scalar(scalar_body)

    def sync_body(sy):
        _entry(sy)
        for hc in range(NHC):
            d = hcs[hc]
            sy.wait_ge(Vm, v_copy[hc])
            sy.dma_start(dr["onum_scr"][128 * d["h"]:128 * (d["h"] + 1),
                                        d["q0"]:d["q0"] + 512],
                         onum_bf[hc % 2][:, :]).then_inc(S, 16)
            sy.dma_start(dr["oden_scr"][d["h"]:d["h"] + 1,
                                        d["q0"]:d["q0"] + 512],
                         den_sb[hc % 2][0:1, :]).then_inc(S, 16)
    ch.block.sync(sync_body)

    @ch.step("vector")
    def _(v, sem=None):
        v.wait_ge(A, a_tot)
        v.wait_ge(E, e_tot)
        v.wait_ge(Vm, v_tot)
        v.wait_ge(S, s_tot)
        v.tensor_copy(bias_v[:, 0:1], iota[:, :])


def emit_wo(nc, ch, st, dr):
    sb = lambda n, sh, dt: st.enter_context(nc.sbuf_tensor(_u(n), sh, dt))
    atsb = [sb(f"wo_a{h}", [128, TH], BF16) for h in range(32)]
    drow = sb("wo_drow", [1, TH], F32)
    rrow = sb("wo_rrow", [1, TH], F32)
    onesrow = sb("wo_ones", [1, 128], F32)
    drep = sb("wo_drep", [128, TH], F32)
    numf = sb("wo_numf", [128, TH], F32)
    wosb = [sb(f"wo_w{cb}", [128, 512], BF16) for cb in range(32)]
    sout = sb("wo_sout", [128, 512], BF16)
    dr_ps = st.enter_context(nc.psum_tensor(_u("wo_dps"), [128, TH], F32))
    s_ps = [st.enter_context(nc.psum_tensor(_u("wo_sps"), [128, 512], F32))
            for _i in range(2)]
    onesr_d = nc.inline_tensor(np.ones((1, 128), dtype=np.float32), "wo_or_d")

    @ch.step("sync")
    def _(sync, sem=None):
        sync.dma_start(onesrow[:, :], onesr_d[:, :])

    for h in range(32):
        @ch.step("gpsimd")
        def _(gp, sem=None):
            gp.dma_start(numf[:, :],
                         dr["onum_scr"][128 * h:128 * (h + 1), :])

        @ch.step("sync")
        def _(sync, sem=None):
            sync.dma_start(drow[0:1, :], dr["oden_scr"][h:h + 1, :])

        @ch.step("vector")
        def _(v, sem=None):
            v.reciprocal(rrow[0:1, :], drow[0:1, :])

        @ch.step("tensor")
        def _(te, sem=None):
            for c in range(0, TH, 512):
                mi = te.matmul_q(dr_ps[:, c:c + 512], onesrow[:, :],
                                 rrow[0:1, c:c + 512], start=True, stop=True,
                                 skip_group_check=True)
            te.mark(mi)

        @ch.step("vector")
        def _(v, sem=None):
            v.tensor_copy(drep[:, :], dr_ps[:, :])
            v.tensor_tensor(atsb[h][:, :], numf[:, :], drep[:, :], ALU.mult)

    for oc in range(8):
        @ch.step("gpsimd")
        def _(gp, sem=None):
            for cb in range(32):
                gp.dma_start(wosb[cb][:, :],
                             dr["Wo"][128 * cb:128 * (cb + 1),
                                      512 * oc:512 * (oc + 1)])

        for t in range(NTH):
            @ch.step("tensor")
            def _(te, sem=None):
                for cb in range(32):
                    mi = te.matmul_q(s_ps[t % 2][:, :],
                                     atsb[cb][:, 128 * t:128 * (t + 1)],
                                     wosb[cb][:, :],
                                     start=(cb == 0), stop=(cb == 31))
                te.mark(mi)

            @ch.step("vector")
            def _(v, sem=None):
                v.tensor_copy(sout[:, :], s_ps[t % 2][:, :])

            @ch.step("sync")
            def _(sync, sem=None):
                sync.dma_start(dr["s1_scr"][128 * t:128 * (t + 1),
                                            512 * oc:512 * (oc + 1)],
                               sout[:, :])


def emit_ffn1(nc, ch, st, dr):
    sb = lambda n, sh, dt: st.enter_context(nc.sbuf_tensor(_u(n), sh, dt))
    NV = 32
    xsb = [sb(f"f1_x{v}", [128, TH], BF16) for v in range(NV)]
    wc = sb("f1_wc", [128, D], BF16)
    atsb = sb("f1_at", [128, 48 * 32], BF16)
    bssb = sb("f1_bs", [48, D], BF16)
    bsf = sb("f1_bsf", [128, 32], F32)
    lsirep = sb("f1_lsi", [48, 4], I32)
    lsif = sb("f1_lsif", [48, 4], F32)
    brep = sb("f1_bv", [48, 1], F32)
    arsb = sb("f1_ar", [48, 1], F32)
    cnt = sb("f1_cnt", [48, 1], F32)
    seg = sb("f1_seg", [48, 1], F32)
    use = sb("f1_use", [48, 1], F32)
    aidx = sb("f1_aidx", [48, 1], F32)
    msk = sb("f1_msk", [48, 1], F32)
    rT = sb("f1_rT", [48, TH], BF16)
    gout = sb("f1_gout", [128, TH], BF16)
    r_ps = st.enter_context(nc.psum_tensor(_u("f1_rps"), [48, TH], F32))
    g_ps = st.enter_context(nc.psum_tensor(_u("f1_gps"), [128, TH], F32))

    @ch.step("sync")
    def _(sync, sem=None):
        for v in range(NV):
            sync.dma_start(xsb[v][:, :],
                           dr["xn1_scr"][TH:T, 128 * v:128 * (v + 1)],
                           transpose=True)
        sync.dma_start(lsirep[:, :], bass.AP(dr["lsi"], 0, [[0, 48], [1, 4]]))
        sync.dma_start(brep[:, :], bass.AP(dr["bval"], 0, [[0, 48], [1, 1]]))
        sync.dma_start(arsb[:, :], dr["arow"][:, :])
        sync.dma_start(bsf[:, :], dr["biasc1"][:, :])

    @ch.step("gpsimd")
    def _(gp, sem=None):
        for v in range(NV):
            gp.dma_start(ap(atsb, v * 48, (48 * 32, 128), (1, 48)),
                         dr["AT"][128 * v:128 * (v + 1), :])
        gp.dma_start(bssb[0:48, :], dr["BS"][:, :])

    @ch.step("vector")
    def _(v, sem=None):
        v.tensor_copy(lsif[0:48, :], lsirep[0:48, :])
        v.tensor_scalar(lsif[0:48, :], lsif[0:48, :], brep[0:48, 0:1], None,
                        ALU.is_le)
        v.tensor_reduce(cnt[0:48, :], lsif[0:48, :], mybir.AxisListType.X,
                        ALU.add)
        v.tensor_scalar(seg[0:48, :], cnt[0:48, :], -1.0, None, ALU.add)
        v.tensor_scalar(use[0:48, :], seg[0:48, :], 0.0, None, ALU.is_gt)
        v.tensor_scalar(aidx[0:48, :], seg[0:48, :], -1.0, 0.0, ALU.add,
                        ALU.max)
        v.tensor_scalar(aidx[0:48, :], aidx[0:48, :], 2.0, None, ALU.min)
        v.tensor_tensor(msk[0:48, :], aidx[0:48, :], arsb[0:48, :],
                        ALU.is_equal)
        v.tensor_tensor(msk[0:48, :], msk[0:48, :], use[0:48, :], ALU.mult)

    @ch.step("tensor")
    def _(te, sem=None):
        for tq in range(4):
            for v in range(NV):
                mi = te.matmul_q(r_ps[0:48, 512 * tq:512 * (tq + 1)],
                                 ap(atsb, v * 48, (48 * 32, 128), (1, 48)),
                                 xsb[v][:, 512 * tq:512 * (tq + 1)],
                                 start=(v == 0), stop=(v == NV - 1),
                                 skip_group_check=True)
        te.mark(mi)

    @ch.step("vector")
    def _(v, sem=None):
        v.tensor_scalar(rT[0:48, :], r_ps[0:48, :], msk[0:48, 0:1], None,
                        ALU.mult)

    for oc in range(32):
        @ch.step("gpsimd")
        def _(gp, sem=None):
            gp.dma_start(ap(wc, 0, (D, 128), (128, 32), (1, 128)),
                         ap(dr["WT"], oc * 128, (D, 128), (128 * D, 32),
                            (1, 128)))

        @ch.step("tensor")
        def _(te, sem=None):
            for tq in range(4):
                for g in range(NV):
                    te.matmul_q(g_ps[:, 512 * tq:512 * (tq + 1)],
                                wc[:, 128 * g:128 * (g + 1)],
                                xsb[g][:, 512 * tq:512 * (tq + 1)],
                                start=(g == 0), stop=False,
                                skip_group_check=True)
                mi = te.matmul_q(g_ps[:, 512 * tq:512 * (tq + 1)],
                                 bssb[0:48, 128 * oc:128 * (oc + 1)],
                                 rT[0:48, 512 * tq:512 * (tq + 1)],
                                 start=False, stop=True,
                                 skip_group_check=True)
            te.mark(mi)

        @ch.step("scalar")
        def _(s, sem=None):
            s.activation(gout[:, :], g_ps[:, :], AF.Silu,
                         bias=bsf[:, oc:oc + 1])

        @ch.step("sync")
        def _(sync, sem=None):
            sync.dma_start(dr["g1T_scr"][128 * oc:128 * (oc + 1), :],
                           gout[:, :])


def build_mega(slopes_np):
    nc = bass.Bass("TRN2", target_bir_lowering=False)
    dt_ = lambda n, sh, dty, kind="Internal": nc.dram_tensor(n, sh, dty,
                                                             kind=kind)
    dr = {}
    EI = "ExternalInput"
    dr["xb"] = dt_("xb", [T, D], F32, EI)
    dr["WinT"] = dt_("WinT", [D, MD], F32, EI)
    dr["binm"] = dt_("binm", [1, MD], F32, EI)
    dr["qw"] = dt_("qw", [512, D], I32, EI)
    dr["qz"] = dt_("qz", [32, 512], I32, EI)
    dr["scales"] = dt_("scales", [32, D], F16, EI)
    dr["biasc0"] = dt_("biasc0", [128, 32], F16, EI)
    dr["shamt"] = dt_("shamt", [128, 1], I32, EI)
    dr["cmask"] = dt_("cmask", [128, 8], F32, EI)
    for dd in range(4):
        for nm, sh in [(f"lnw{dd}", [1, MD]), (f"lnb{dd}", [1, MD]),
                       (f"scl{dd}", [1, MD]), (f"phzs{dd}", [128, 8]),
                       (f"phzrow{dd}", [2, MD]), (f"cinit{dd}", [128, 8]),
                       (f"cinj{dd}", [128, 8]), (f"ln2w{dd}", [1, MD]),
                       (f"ln2b{dd}", [1, MD]), (f"W1T{dd}", [MD, MF]),
                       (f"b1c{dd}", [128, 16]), (f"W2T{dd}", [MF, MD]),
                       (f"b2_{dd}", [1, MD])]:
            dr[nm] = dt_(nm, sh, F32, EI)
    dr["lnlw"] = dt_("lnlw", [1, MD], F32, EI)
    dr["lnlb"] = dt_("lnlb", [1, MD], F32, EI)
    dr["WoutT"] = dt_("WoutT", [MD, D], F32, EI)
    dr["bout"] = dt_("bout", [1, D], F32, EI)
    dr["Wq"] = dt_("Wq", [D, D], F32, EI)
    dr["Wkv"] = dt_("Wkv", [D, 256], F32, EI)
    dr["slopes"] = dt_("slopes", [1, 32], F32, EI)
    dr["pmask"] = dt_("pmask", [128, NT], F32, EI)
    dr["Wo"] = dt_("Wo", [D, D], F32, EI)
    dr["WT"] = dt_("WT", [D, D], F32, EI)
    dr["biasc1"] = dt_("biasc1", [128, 32], F32, EI)
    dr["AT"] = dt_("AT", [D, 48], F32, EI)
    dr["BS"] = dt_("BS", [48, D], F32, EI)
    dr["lsi"] = dt_("lsi", [1, 4], I32, EI)
    dr["bval"] = dt_("bval", [1, 1], F32, EI)
    dr["arow"] = dt_("arow", [48, 1], F32, EI)
    dr["out"] = dt_("out", [TH, D], F32, "ExternalOutput")
    dr["xn_scr"] = dt_("xn_scr", [T, D], BF16)
    dr["h_scr"] = dt_("h_scr", [T, MD], F32)
    dr["g0T_scr"] = dt_("g0T_scr", [D, T], BF16)
    dr["s0_scr"] = dt_("s0_scr", [T, D], BF16)
    dr["x1_scr"] = dt_("x1_scr", [T, D], F32)
    dr["xn1_scr"] = dt_("xn1_scr", [T, D], BF16)
    dr["qt_scr"] = dt_("qt_scr", [D, TH], BF16)
    dr["kvt_scr"] = dt_("kvt_scr", [256, T], BF16)
    dr["onum_scr"] = dt_("onum_scr", [D, TH], BF16)
    dr["oden_scr"] = dt_("oden_scr", [32, TH], F32)
    dr["s1_scr"] = dt_("s1_scr", [TH, D], BF16)
    dr["g1T_scr"] = dt_("g1T_scr", [D, TH], BF16)
    dr["lny_scr"] = dt_("lny_scr", [T, MD], BF16)
    dr["w4_scr"] = dt_("w4_scr", [D, D], BF16)
    dr["crow_scr"] = dt_("crow_scr", [2 * 16 * MD], F32)
    dr["prow_scr"] = dt_("prow_scr", [2, MD], F32)

    ch = Chain(nc)
    if True:
        with ExitStack() as st:
            emit_ln(nc, ch, st, dr["xb"], dr["xn_scr"], NT)
        with ExitStack() as st:
            emit_h0_gate0(nc, ch, st, dr)
        with ExitStack() as st:
            emit_mini_stack(nc, ch, st, dr)
        with ExitStack() as st:
            emit_combine(nc, ch, st, dr["xb"], dr["s0_scr"], dr["g0T_scr"],
                         dr["x1_scr"], NT, 0, 0, 0)
        with ExitStack() as st:
            emit_ln(nc, ch, st, dr["x1_scr"], dr["xn1_scr"], NT)
        with ExitStack() as st:
            emit_qkv(nc, ch, st, dr)
        with ExitStack() as st:
            emit_attn(nc, ch, st, dr, slopes_np)
        with ExitStack() as st:
            emit_wo(nc, ch, st, dr)
        with ExitStack() as st:
            emit_ffn1(nc, ch, st, dr)
        with ExitStack() as st:
            emit_combine(nc, ch, st, dr["x1_scr"], dr["s1_scr"],
                         dr["g1T_scr"], dr["out"], NTH, TH, 0, 0)
    ch.finish()
    return nc


_PROG = {}


def kernel(**inp):
    C = np.ascontiguousarray
    f32 = np.float32
    sl_np = alibi_slopes(32)
    if "mega" not in _PROG:
        _PROG["mega"] = build_mega(tuple(float(v) for v in sl_np))
    nc = _PROG["mega"]

    x = np.asarray(inp["L0_attn_x"], f32)
    scan_layout = lambda a: C(np.asarray(a, f32).reshape(4, 128).T)
    zeros8 = np.zeros((128, 8), f32)
    ones8 = np.ones((128, 8), f32)

    shamt = ((np.arange(128) % 8) * 4).astype(np.int32)[:, None]
    biasc0 = C(np.asarray(inp["L0_ffn_bias"]).reshape(32, 128).T)
    WinT = C(np.asarray(inp["L0_attn_Win"], f32).T)
    WoutT = C(np.asarray(inp["L0_attn_Wout"], f32).T)
    WT = C(np.asarray(inp["L1_ffn_weight"], f32).T)
    biasc1 = C(np.asarray(inp["L1_ffn_bias"], f32).reshape(32, 128).T)
    AT = C(np.asarray(inp["L1_ffn_lora_A"],
                      f32).transpose(2, 0, 1).reshape(D, 48))
    BS = C(np.asarray(inp["L1_ffn_lora_B"],
                      f32).transpose(0, 2, 1).reshape(48, D))
    arow = np.repeat(np.arange(3, dtype=f32), 16)[:, None]
    lsi = np.asarray(inp["L1_ffn_lora_start_indices"], np.int32)[None]

    in_maps = []
    for c in range(NCORE):
        b, hf = c // 2, c % 2
        if hf == 0:
            xb = np.concatenate([np.zeros((TH, D), f32),
                                 np.asarray(x[b, 0:TH], f32)], 0)
        else:
            xb = np.asarray(x[b], f32)
        m = {
            "xb": C(xb), "WinT": WinT,
            "binm": np.asarray(inp["L0_attn_bin"], f32)[None],
            "qw": np.asarray(inp["L0_ffn_qweight"]),
            "qz": np.asarray(inp["L0_ffn_qzeros"]),
            "scales": np.asarray(inp["L0_ffn_scales"]),
            "biasc0": biasc0, "shamt": shamt,
            "cmask": ones8 if hf == 1 else zeros8,
            "lnlw": np.asarray(inp["L0_attn_lnl_w"], f32)[None],
            "lnlb": np.asarray(inp["L0_attn_lnl_b"], f32)[None],
            "WoutT": WoutT,
            "bout": np.asarray(inp["L0_attn_bout"], f32)[None],
            "Wq": np.asarray(inp["L1_attn_Wq"], f32),
            "Wkv": np.asarray(inp["L1_attn_Wkv"], f32),
            "slopes": sl_np[None],
            "pmask": np.concatenate(
                [np.full((128, 16), NEG if hf == 0 else 0.0, f32),
                 np.zeros((128, 16), f32)], 1),
            "Wo": np.asarray(inp["L1_attn_Wo"], f32),
            "WT": WT, "biasc1": biasc1, "AT": AT, "BS": BS,
            "lsi": lsi, "bval": np.full((1, 1), float(b), f32),
            "arow": arow,
        }
        for dd in range(4):
            lc8 = np.concatenate([scan_layout(inp["L0_attn_lc_re"][dd]),
                                  scan_layout(inp["L0_attn_lc_im"][dd])], 1)
            m[f"lnw{dd}"] = np.asarray(inp["L0_attn_ln1_w"][dd], f32)[None]
            m[f"lnb{dd}"] = np.asarray(inp["L0_attn_ln1_b"][dd], f32)[None]
            m[f"scl{dd}"] = np.asarray(inp["L0_attn_sc_lin"][dd], f32)[None]
            m[f"phzs{dd}"] = np.concatenate(
                [scan_layout(inp["L0_attn_phz_re"][dd]),
                 scan_layout(inp["L0_attn_phz_im"][dd])], 1)
            m[f"phzrow{dd}"] = C(np.stack(
                [np.asarray(inp["L0_attn_phz_re"][dd], f32),
                 np.asarray(inp["L0_attn_phz_im"][dd], f32)]))
            m[f"cinit{dd}"] = lc8 if hf == 1 else zeros8.copy()
            m[f"cinj{dd}"] = lc8 if hf == 0 else zeros8.copy()
            m[f"ln2w{dd}"] = np.asarray(inp["L0_attn_ln2_w"][dd], f32)[None]
            m[f"ln2b{dd}"] = np.asarray(inp["L0_attn_ln2_b"][dd], f32)[None]
            m[f"W1T{dd}"] = C(np.asarray(inp["L0_attn_W1"][dd], f32).T)
            m[f"b1c{dd}"] = C(np.asarray(inp["L0_attn_b1"][dd],
                                         f32).reshape(16, 128).T)
            m[f"W2T{dd}"] = C(np.asarray(inp["L0_attn_W2"][dd], f32).T)
            m[f"b2_{dd}"] = np.asarray(inp["L0_attn_b2"][dd], f32)[None]
        in_maps.append(m)

    res = run_bass_kernel_spmd(nc, in_maps, core_ids=list(range(NCORE)))
    out = np.empty((4, 4096, D), f32)
    for c in range(NCORE):
        b, hf = c // 2, c % 2
        out[b, hf * TH:(hf + 1) * TH] = np.asarray(res.results[c]["out"], f32)
    return out
